# version 27
# baseline (speedup 1.0000x reference)
import numpy as np
import ml_dtypes
from contextlib import ExitStack

import concourse.bass as bass
import concourse.mybir as mybir
from concourse.bass_utils import run_bass_kernel_spmd

F32 = mybir.dt.float32
BF16 = mybir.dt.bfloat16
I32 = mybir.dt.int32
AF = mybir.ActivationFunctionType
ALU = mybir.AluOpType
AX = mybir.AxisListType

ENGINES = ("tensor", "vector", "scalar", "gpsimd", "sync")

N_CORES = 8
B_CORE = 4
T = 2048
D = 1024
NT = T // 128
ND = D // 128



class DmaStream:
    __slots__ = ("sem", "count")

    def __init__(self, sem):
        self.sem = sem
        self.count = 0

    def tok(self):
        return (self.sem, self.count)


class Buf:
    __slots__ = ("h", "name", "w", "readers", "cm", "freed")

    def __init__(self, h, name, cm=None):
        self.h = h
        self.name = name
        self.w = None
        self.readers = []
        self.cm = cm
        self.freed = False

    def __getitem__(self, idx):
        return self.h[idx]


class Eng:
    def __init__(self, ctx, key, sem):
        self.ctx = ctx
        self.key = key
        self.sem = sem
        self.count = 0
        self.watermark = {}

    def _rec(self, fn):
        self.ctx.cmds[self.key].append(fn)

    def _wait_tok(self, tok):
        if tok is None:
            return
        sem, val = tok
        key = sem.name
        if self.watermark.get(key, -1) >= val:
            return
        self.watermark[key] = val
        self._rec(lambda eng, s=sem, v=val: eng.wait_ge(s, v))

    def dep(self, outs, ins):
        for b in ins:
            if b is not None:
                self._wait_tok(b.w)
        for b in outs:
            if b is not None:
                self._wait_tok(b.w)
                for t_ in b.readers:
                    self._wait_tok(t_)

    def _mark(self, outs, ins):
        self.count += 1
        tok = (self.sem, self.count)
        for b in outs:
            if b is not None:
                b.w = tok
                b.readers = []
        for b in ins:
            if b is not None:
                b.readers.append(tok)
        return tok

    def op(self, method, outs, ins, *args, inc=True, **kwargs):
        self.dep(outs, ins)
        if inc:
            tok = self._mark(outs, ins)
            sem = self.sem
            self._rec(lambda eng, m=method, a=args, k=kwargs:
                      getattr(eng, m)(*a, **k).then_inc(sem, 1))
            return tok
        else:
            self._rec(lambda eng, m=method, a=args, k=kwargs:
                      getattr(eng, m)(*a, **k))
            return None

    def raw(self, fn):
        self._rec(fn)

    def tok(self):
        return (self.sem, self.count)


class Ctx:
    def __init__(self):
        self.nc = bass.Bass("TRN2", target_bir_lowering=False)
        self.es = ExitStack()
        self.pfx = ""
        self.alloc_log = []
        self.cmds = {k: [] for k in ENGINES}
        self.t = Eng(self, "tensor", self._sem("semP"))
        self.v = Eng(self, "vector", self._sem("semD"))
        self.s = Eng(self, "scalar", self._sem("semA"))
        self.g = Eng(self, "gpsimd", self._sem("semG"))
        self.sy = Eng(self, "sync", self._sem("semS"))
        self._consts = {}
        self._const_tile = None
        self._nconst = 0
        self._ident = {}
        self._iota = None
        self._retired = {}

    def _sem(self, name):
        return self.es.enter_context(self.nc.semaphore(self.pfx + name))

    def eng(self, key):
        return {"tensor": self.t, "vector": self.v, "scalar": self.s,
                "gpsimd": self.g, "sync": self.sy}[key]

    def sbuf(self, name, shape, dt=F32, side=None):
        name = self.pfx + name
        cm = self.nc.sbuf_tensor(name, shape, dt, side=side)
        b = Buf(cm.__enter__(), name, cm)
        b.readers = list(self._retired.values())
        self.alloc_log.append(b)
        return b

    def free(self, *bufs):
        toks = []
        for b in bufs:
            if b.freed or b.cm is None:
                continue
            if b.w is not None:
                toks.append(b.w)
            toks.extend(b.readers)
            b.cm.__exit__(None, None, None)
            b.freed = True
        for tk in toks:
            k = tk[0].name
            if k not in self._retired or self._retired[k][1] < tk[1]:
                self._retired[k] = tk
        return toks

    def psum(self, name, shape, dt=F32, after=None):
        name = self.pfx + name
        cm = self.nc.psum_tensor(name, shape, dt)
        b = Buf(cm.__enter__(), name, cm)
        b.readers = list(self._retired.values())
        if after:
            b.readers += list(after)
        self.alloc_log.append(b)
        return b

    def dram(self, name, shape, dt=F32, kind="Internal"):
        if kind == "Internal":
            name = self.pfx + name
        return Buf(self.nc.dram_tensor(name, shape, dt, kind=kind).ap(), name)

    def phase(self, pfx):
        from contextlib import contextmanager

        @contextmanager
        def _cm():
            old = self.pfx
            self.pfx = pfx
            mark = len(self.alloc_log)
            yield
            for b in reversed(self.alloc_log[mark:]):
                self.free(b)
            del self.alloc_log[mark:]
            self.pfx = old
        return _cm()

    def dma_stream(self, name):
        return DmaStream(self._sem(name))

    def dma_op(self, queue, stream, pairs, outs, ins, **kwargs):
        e = self.eng(queue)
        e.dep(outs, ins)
        if stream.count > 0:
            e._wait_tok((stream.sem, stream.count))
        sem = stream.sem
        for (o, i) in pairs:
            e._rec(lambda eng, o=o, i=i, k=kwargs, s=sem:
                   eng.dma_start(out=o, in_=i, **k).then_inc(s, 16))
        stream.count += 16 * len(pairs)
        tok = (sem, stream.count)
        for b in outs:
            if b is not None:
                b.w = tok
                b.readers = []
        for b in ins:
            if b is not None:
                b.readers.append(tok)
        return tok

    def dma_transpose_op(self, queue, stream, out, in_, outs, ins):
        assert queue == "sync", "keep every dma_start_transpose on the sync queue"
        e = self.eng(queue)
        e.dep(outs, ins)
        if stream.count > 0:
            e._wait_tok((stream.sem, stream.count))
        sem = stream.sem
        e._rec(lambda eng, o=out, i=in_, s=sem:
               eng.dma_start_transpose(o, i).then_inc(s, 16))
        stream.count += 16
        tok = (sem, stream.count)
        for b in outs:
            if b is not None:
                b.w = tok
                b.readers = []
        for b in ins:
            if b is not None:
                b.readers.append(tok)
        return tok

    def const(self, value):
        if value in self._consts:
            return self._consts[value]
        if self._const_tile is None:
            self._const_tile = self.sbuf("consts", [128, 32], F32)
        i = self._nconst
        self._nconst += 1
        assert i < 32
        sl = Buf(self._const_tile.h[:, i:i + 1], f"const_{value}")
        self.g.op("memset", [sl], [], sl.h, float(value))
        self._consts[value] = sl
        return sl

    def identity(self, dt=BF16):
        if dt in self._ident:
            return self._ident[dt]
        if self._iota is None:
            self._iota = self.sbuf("iden_iota", [128, 128], F32)
            self.g.op("iota", [self._iota], [], self._iota[:],
                      pattern=[[1, 128]], base=0, channel_multiplier=-1,
                      allow_small_or_imprecise_dtypes=True)
        idb = self.sbuf(f"ident_{dt}", [128, 128], dt)
        self.v.op("tensor_single_scalar", [idb], [self._iota], idb[:],
                  self._iota[:], 0.0, op=ALU.is_equal)
        self._ident[dt] = idb
        return idb

    def emit(self):
        nc = self.nc
        with nc.Block() as block:
            for key, dec in (("sync", block.sync), ("tensor", block.tensor),
                             ("vector", block.vector), ("scalar", block.scalar),
                             ("gpsimd", block.gpsimd)):
                cmds = self.cmds[key]

                def section(eng, cmds=cmds):
                    for fn in cmds:
                        fn(eng)
                dec(section)
        self.es.close()
        return nc

    def n_inst(self):
        return sum(len(c) for c in self.cmds.values())


def alias_pull(dst, *srcs):
    for s_ in srcs:
        if s_.w is not None:
            dst.readers.append(s_.w)
        dst.readers.extend(s_.readers)
        s_.readers = []



def make_stat_slices(ctx, stats, nsets):
    out = []
    for j in range(nsets):
        out.append(tuple(Buf(stats.h[:, 4 * j + k:4 * j + k + 1],
                             f"{stats.name}_{j}_{k}") for k in range(4)))
    return out


def emit_ln_stats(ctx, x, x_ap, sl, eps_c, scratch, dim=1024):
    v, s = ctx.v, ctx.s
    nsum, sumsq, nmu, rstd = sl
    v.op("tensor_reduce", [nsum], [x], nsum.h, x_ap, axis=AX.X, op=ALU.add,
         negate=True)
    s.op("activation", [scratch, sumsq], [x], scratch.h[:, 0:x_ap.shape[-1]],
         x_ap, AF.Square, accum_out=sumsq.h)
    v.op("tensor_scalar", [nmu], [nsum], nmu.h, nsum.h, 1.0 / dim, None,
         op0=ALU.mult, op1=ALU.bypass)
    v.op("tensor_tensor", [scratch], [nmu], scratch.h[:, 0:1], nmu.h, nmu.h,
         op=ALU.mult)
    v.op("scalar_tensor_tensor", [rstd], [sumsq, scratch], rstd.h, sumsq.h,
         1.0 / dim, scratch.h[:, 0:1], op0=ALU.mult, op1=ALU.subtract)
    s.op("activation", [rstd], [rstd, eps_c], rstd.h, rstd.h, AF.Sqrt,
         bias=eps_c.h)
    v.op("reciprocal", [rstd], [rstd], rstd.h, rstd.h)
    return nmu, rstd



def emit_ln(ctx, ext, tag="ln0"):
    x = ext["x"]
    xnT = ext["xnT"]
    eps_c = ctx.const(1e-5)
    ident = ctx.identity(BF16)

    t, v, s, g, sy = ctx.t, ctx.v, ctx.s, ctx.g, ctx.sy
    xin = [ctx.sbuf(f"xin{i}", [128, D], F32) for i in range(2)]
    xh = [ctx.sbuf(f"xh{i}", [128, D], BF16) for i in range(2)]
    scratch = [ctx.sbuf(f"scr{i}", [128, D], F32) for i in range(2)]
    stats = ctx.sbuf("stats", [128, 8], F32)
    statsl = make_stat_slices(ctx, stats, 2)
    fmacc = ctx.sbuf("fmacc", [128, ND * T], BF16)
    pst = [ctx.psum(f"pst{i}", [128, 128], BF16) for i in range(4)]
    ld = [ctx.dma_stream(f"ld{i}") for i in range(2)]
    sto = ctx.dma_stream("sto")

    for b in range(B_CORE):
        for it in range(NT):
            i = (b * NT + it) % 2
            r0 = (b * NT + it) * 128
            ctx.dma_op("sync", ld[i], [(xin[i][:], x.h[r0:r0 + 128, :])],
                       [xin[i]], [x])
            nmu, rstd = emit_ln_stats(ctx, xin[i], xin[i][:], statsl[i],
                                      eps_c, scratch[i])
            v.op("tensor_scalar", [xh[i]], [xin[i], nmu, rstd], xh[i][:],
                 xin[i][:], nmu.h, rstd.h, op0=ALU.add, op1=ALU.mult)
            for c in range(ND):
                p = pst[(it * ND + c) % 4]
                t.op("transpose", [p], [xh[i], ident], p[:],
                     xh[i][:, c * 128:(c + 1) * 128], ident[:])
                v.op("tensor_copy", [fmacc], [p],
                     fmacc.h[:, c * T + it * 128: c * T + (it + 1) * 128], p[:])
        ctx.dma_op("gpsimd", sto,
                   [(xnT.h[b, c * 128:(c + 1) * 128, :],
                     fmacc.h[0:128, c * T:(c + 1) * T]) for c in range(ND)],
                   [xnT], [fmacc])
    g.raw(lambda eng, s_=sto.sem, v_=sto.count: eng.wait_ge(s_, v_))


def build_ln_launch(tag="ln0"):
    ctx = Ctx()
    ext = {}
    ext["x"] = ctx.dram("x", [B_CORE * T, D], F32, kind="ExternalInput")
    ext["xnT"] = ctx.dram("xnT", [B_CORE, D, T], BF16, kind="ExternalOutput")
    emit_ln(ctx, ext, tag)
    return ctx.emit()


_PROGS = {}


def _get_prog(name, builder):
    if name not in _PROGS:
        _PROGS[name] = builder()
    return _PROGS[name]


def _run(name, builder, per_core, shared):
    nc = _get_prog(name, builder)
    in_maps = []
    for c in range(N_CORES):
        m = {k: v[c] for k, v in per_core.items()}
        m.update(shared)
        in_maps.append(m)
    res = run_bass_kernel_spmd(nc, in_maps, core_ids=list(range(N_CORES)))
    return res.results


def build_fused():
    ctx = Ctx()
    ctx.const(1e-5)
    ctx.const(-np.pi)
    ctx.identity(BF16)
    d = ctx.dram
    ex = {}
    ex["x"] = d("x", [B_CORE * T, D], F32, kind="ExternalInput")
    for n in ("W_q", "W_k", "W_v", "W_gate", "W_o"):
        ex[n] = d(n, [D, D], F32, kind="ExternalInput")
    ex["b_gate0"] = d("b_gate0", [D], F32, kind="ExternalInput")
    ex["conv_w"] = d("conv_w", [D, D * 3], F32, kind="ExternalInput")
    ex["conv_b"] = d("conv_b", [D], F32, kind="ExternalInput")
    ex["wqkv"] = d("wqkv", [3 * D, D], F32, kind="ExternalInput")
    ex["bqkv"] = d("bqkv", [3 * D], F32, kind="ExternalInput")
    ex["wo"] = d("wo", [D, D], F32, kind="ExternalInput")
    ex["bo"] = d("bo", [D], F32, kind="ExternalInput")
    ex["f0_ln1_g"] = d("f0_ln1_g", [D], F32, kind="ExternalInput")
    ex["f0_ln1_b"] = d("f0_ln1_b", [D], F32, kind="ExternalInput")
    ex["gate_w"] = d("gate_w", [D, NEXP], F32, kind="ExternalInput")
    ex["gate_b"] = d("gate_b", [NEXP], F32, kind="ExternalInput")
    ex["w1"] = d("w1", [NEXP * D, DFF], F32, kind="ExternalInput")
    ex["b1"] = d("b1", [NEXP, DFF], F32, kind="ExternalInput")
    ex["w2"] = d("w2", [NEXP * DFF, D], F32, kind="ExternalInput")
    ex["b2"] = d("b2", [NEXP, D], F32, kind="ExternalInput")
    ex["f0_ln2_g"] = d("f0_ln2_g", [D], F32, kind="ExternalInput")
    ex["f0_ln2_b"] = d("f0_ln2_b", [D], F32, kind="ExternalInput")
    ex["w_in"] = d("w_in", [128, D], F32, kind="ExternalInput")
    ex["w_out"] = d("w_out", [D, 128], F32, kind="ExternalInput")
    ex["a1_w_gate"] = d("a1_w_gate", [D, D], F32, kind="ExternalInput")
    ex["a1_b_gate"] = d("a1_b_gate", [D], F32, kind="ExternalInput")
    ex["log_decay"] = d("log_decay", [64], F32, kind="ExternalInput")
    ex["frequency"] = d("frequency", [64], F32, kind="ExternalInput")
    ex["coupling"] = d("coupling", [64, 16], F32, kind="ExternalInput")
    ex["w_mlp1"] = d("w_mlp1", [MLPD, D], F32, kind="ExternalInput")
    ex["b_mlp1"] = d("b_mlp1", [MLPD], F32, kind="ExternalInput")
    ex["w_mlp2"] = d("w_mlp2", [D, MLPD], F32, kind="ExternalInput")
    ex["b_mlp2"] = d("b_mlp2", [D], F32, kind="ExternalInput")
    ex["a1_ln1_g"] = d("a1_ln1_g", [D], F32, kind="ExternalInput")
    ex["a1_ln1_b"] = d("a1_ln1_b", [D], F32, kind="ExternalInput")
    ex["a1_ln2_g"] = d("a1_ln2_g", [D], F32, kind="ExternalInput")
    ex["a1_ln2_b"] = d("a1_ln2_b", [D], F32, kind="ExternalInput")
    ex["f1_w_gate"] = d("f1_w_gate", [HID1, D], I32, kind="ExternalInput")
    ex["f1_w_up"] = d("f1_w_up", [HID1, D], I32, kind="ExternalInput")
    ex["f1_w_down"] = d("f1_w_down", [D, HID1], I32, kind="ExternalInput")
    ex["out"] = d("out", [B_CORE * T, D], F32, kind="ExternalOutput")
    io = {}
    io["xnT"] = d("io_xnT", [B_CORE, D, T], BF16)
    io["a0out"] = d("io_a0out", [B_CORE * T, D], F32)
    io["qh"] = d("io_qh", [B_CORE, D, T], BF16)
    io["kh"] = d("io_kh", [B_CORE, D, T], BF16)
    io["vp"] = d("io_vp", [B_CORE, 128, NT * 1040], BF16)
    io["ybT"] = d("io_ybT", [B_CORE * T, D], BF16)
    io["xl1T"] = d("io_xl1T", [B_CORE, D, T], BF16)
    io["xl1"] = d("io_xl1", [B_CORE * T, D], F32)
    io["f0out"] = d("io_f0out", [B_CORE * T, D], F32)
    io["x1"] = d("io_x1", [B_CORE * T, D], F32)
    io["xn1"] = d("io_xn1", [B_CORE * T, D], F32)
    io["xn1T"] = d("io_xn1T", [B_CORE, D, T], BF16)
    io["a1out"] = d("io_a1out", [B_CORE * T, D], F32)
    io["f1out"] = d("io_f1out", [B_CORE * T, D], F32)

    with ctx.phase("p0_"):
        emit_ln(ctx, {"x": ex["x"], "xnT": io["xnT"]})
    with ctx.phase("p1_"):
        emit_gla(ctx, {"xnT": io["xnT"], "W_q": ex["W_q"], "W_k": ex["W_k"],
                       "W_v": ex["W_v"], "W_gate": ex["W_gate"],
                       "b_gate": ex["b_gate0"], "W_o": ex["W_o"],
                       "a0out": io["a0out"]})
    with ctx.phase("p2_"):
        emit_f0conv(ctx, {"xnT": io["xnT"], "conv_w": ex["conv_w"],
                          "conv_b": ex["conv_b"], "wqkv": ex["wqkv"],
                          "bqkv": ex["bqkv"], "qh": io["qh"], "kh": io["kh"],
                          "vp": io["vp"], "ybT": io["ybT"]})
    with ctx.phase("p3_"):
        emit_f0attn(ctx, {"qh": io["qh"], "kh": io["kh"], "vp": io["vp"],
                          "ybT": io["ybT"], "wo": ex["wo"], "bo": ex["bo"],
                          "ln1_g": ex["f0_ln1_g"], "ln1_b": ex["f0_ln1_b"],
                          "xl1T": io["xl1T"], "xl1": io["xl1"]})
    with ctx.phase("p4_"):
        emit_moe(ctx, {"xl1T": io["xl1T"], "xl1": io["xl1"],
                       "gate_w": ex["gate_w"], "gate_b": ex["gate_b"],
                       "w1": ex["w1"], "b1": ex["b1"], "w2": ex["w2"],
                       "b2": ex["b2"], "ln2_g": ex["f0_ln2_g"],
                       "ln2_b": ex["f0_ln2_b"], "f0out": io["f0out"]})
    with ctx.phase("p5_"):
        emit_sum_ln(ctx, {"x0": ex["x"], "a0": io["a0out"], "f0": io["f0out"],
                          "x1": io["x1"], "xn1": io["xn1"], "xn1T": io["xn1T"]})
    with ctx.phase("p6_"):
        emit_a1(ctx, {"xn1": io["xn1"], "w_in": ex["w_in"], "w_out": ex["w_out"],
                      "w_gate": ex["a1_w_gate"], "b_gate": ex["a1_b_gate"],
                      "log_decay": ex["log_decay"], "frequency": ex["frequency"],
                      "coupling": ex["coupling"], "w_mlp1": ex["w_mlp1"],
                      "b_mlp1": ex["b_mlp1"], "w_mlp2": ex["w_mlp2"],
                      "b_mlp2": ex["b_mlp2"], "ln1_g": ex["a1_ln1_g"],
                      "ln1_b": ex["a1_ln1_b"], "ln2_g": ex["a1_ln2_g"],
                      "ln2_b": ex["a1_ln2_b"], "a1out": io["a1out"]})
    with ctx.phase("p7_"):
        emit_f1(ctx, {"xnT": io["xn1T"], "w_gate": ex["f1_w_gate"],
                      "w_up": ex["f1_w_up"], "w_down": ex["f1_w_down"],
                      "f1out": ex["out"], "x1": io["x1"], "a1": io["a1out"]})
    return ctx.emit()


def kernel(**inp):
    x = np.ascontiguousarray(inp["L0_attn_x"].reshape(32, T, D))
    xs = [np.ascontiguousarray(x[c * B_CORE:(c + 1) * B_CORE].reshape(B_CORE * T, D))
          for c in range(N_CORES)]
    shared = {
        "W_q": inp["L0_attn_W_q"], "W_k": inp["L0_attn_W_k"],
        "W_v": inp["L0_attn_W_v"], "W_gate": inp["L0_attn_W_gate"],
        "b_gate0": inp["L0_attn_b_gate"], "W_o": inp["L0_attn_W_o"],
        "conv_w": np.ascontiguousarray(inp["L0_ffn_conv_w"].reshape(D, 3 * D)),
        "conv_b": inp["L0_ffn_conv_b"], "wqkv": inp["L0_ffn_wqkv"],
        "bqkv": inp["L0_ffn_bqkv"], "wo": inp["L0_ffn_wo"],
        "bo": inp["L0_ffn_bo"], "f0_ln1_g": inp["L0_ffn_ln1_g"],
        "f0_ln1_b": inp["L0_ffn_ln1_b"], "gate_w": inp["L0_ffn_gate_w"],
        "gate_b": inp["L0_ffn_gate_b"],
        "w1": np.ascontiguousarray(inp["L0_ffn_w1"].reshape(NEXP * D, DFF)),
        "b1": inp["L0_ffn_b1"],
        "w2": np.ascontiguousarray(inp["L0_ffn_w2"].reshape(NEXP * DFF, D)),
        "b2": inp["L0_ffn_b2"], "f0_ln2_g": inp["L0_ffn_ln2_g"],
        "f0_ln2_b": inp["L0_ffn_ln2_b"],
        "w_in": inp["L1_attn_w_in"], "w_out": inp["L1_attn_w_out"],
        "a1_w_gate": inp["L1_attn_w_gate"], "a1_b_gate": inp["L1_attn_b_gate"],
        "log_decay": inp["L1_attn_log_decay"],
        "frequency": inp["L1_attn_frequency"],
        "coupling": np.ascontiguousarray(inp["L1_attn_coupling"].reshape(64, 16)),
        "w_mlp1": inp["L1_attn_w_mlp1"], "b_mlp1": inp["L1_attn_b_mlp1"],
        "w_mlp2": inp["L1_attn_w_mlp2"], "b_mlp2": inp["L1_attn_b_mlp2"],
        "a1_ln1_g": inp["L1_attn_ln1_g"], "a1_ln1_b": inp["L1_attn_ln1_b"],
        "a1_ln2_g": inp["L1_attn_ln2_g"], "a1_ln2_b": inp["L1_attn_ln2_b"],
        "f1_w_gate": inp["L1_ffn_w_gate"], "f1_w_up": inp["L1_ffn_w_up"],
        "f1_w_down": inp["L1_ffn_w_down"],
    }
    nc = _get_prog("fused", build_fused)
    in_maps = []
    for c in range(N_CORES):
        m = {"x": xs[c]}
        m.update(shared)
        in_maps.append(m)
    res = run_bass_kernel_spmd(nc, in_maps, core_ids=list(range(N_CORES)))
    out = np.concatenate([np.asarray(r["out"]).reshape(B_CORE, T, D)
                          for r in res.results], axis=0)
    return out


def mkap(buf, offset, dims):
    import concourse.bass as _b
    h = buf.h
    tn = h.tensor if hasattr(h, "tensor") else h
    return bass.AP(tn, offset, dims)


def mm_group(ctx, psum_buf, psum_ap, terms):
    t = ctx.t
    n = len(terms)
    for i, (l, r, ins) in enumerate(terms):
        last = (i == n - 1)
        t.op("matmul", [psum_buf] if (last or i == 0) else [], ins,
             psum_ap, l, r, start=(i == 0), stop=last, inc=last)



def emit_cast_store_T(ctx, src_hbm, scrT_hbm, rows, cols, tmp_raw, tmp_bf,
                      pst, stg, ld_stream, st_stream):
    ident = ctx.identity(BF16)
    t, v, g = ctx.t, ctx.v, ctx.g
    for r in range(rows // 128):
        i = r % 2
        ctx.dma_op("sync", ld_stream[i],
                   [(tmp_raw[i].h[:, 0:cols], src_hbm.h[r * 128:(r + 1) * 128, :])],
                   [tmp_raw[i]], [src_hbm])
        g.op("tensor_copy", [tmp_bf[i]], [tmp_raw[i]], tmp_bf[i].h[:, 0:cols],
             tmp_raw[i].h[:, 0:cols])
        nch = cols // 128
        for c0 in range(0, nch, 8):
            grp = min(8, nch - c0)
            j = (r * nch + c0) % 2
            for c in range(c0, c0 + grp):
                jj = c % 2
                t.op("transpose", [pst[jj]], [tmp_bf[i], ident], pst[jj][:, 0:128],
                     tmp_bf[i].h[:, c * 128:(c + 1) * 128], ident[:])
                (ctx.v if c % 2 == 0 else ctx.s).op(
                     "tensor_copy" if c % 2 == 0 else "copy",
                     [stg[j]], [pst[jj]],
                     stg[j].h[:, (c - c0) * 128:(c - c0 + 1) * 128],
                     pst[jj][:, 0:128])
            ctx.dma_op("gpsimd" if j == 0 else "scalar", st_stream[j],
                       [(mkap(scrT_hbm, (c0 * 128) * rows + r * 128,
                              [[rows, 128], [rows * 128, grp], [1, 128]]),
                         mkap(stg[j], 0, [[1024, 128], [128, grp], [1, 128]]))],
                       [scrT_hbm], [stg[j]])


def emit_cast_store(ctx, src_hbm, scr_hbm, rows, cols, tmp_raw, tmp_bf,
                    ld_stream, st_stream, queue_in="sync", queue_out="gpsimd"):
    nch = rows // 128
    for r in range(nch):
        i = r % 2
        ctx.dma_op(queue_in, ld_stream[i],
                   [(tmp_raw[i].h[:, 0:cols], src_hbm.h[r * 128:(r + 1) * 128, :])],
                   [tmp_raw[i]], [src_hbm])
        ctx.v.op("tensor_copy", [tmp_bf[i]], [tmp_raw[i]], tmp_bf[i].h[:, 0:cols],
                 tmp_raw[i].h[:, 0:cols])
        ctx.dma_op(queue_out, st_stream[i],
                   [(scr_hbm.h[r * 128:(r + 1) * 128, :], tmp_bf[i].h[:, 0:cols])],
                   [scr_hbm], [tmp_bf[i]])



HID1 = 3072
FB = 512
NFB = HID1 // FB


def emit_f1(ctx, ext):
    fuse_sum = "x1" in ext
    xnT = ext["xnT"]
    wg = ext["w_gate"]
    wu = ext["w_up"]
    wd = ext["w_down"]
    out = ext["f1out"]
    wg_s = ctx.dram("wg_s", [D, HID1], BF16)
    wu_s = ctx.dram("wu_s", [D, HID1], BF16)
    wd_s = ctx.dram("wd_s", [HID1, D], BF16)

    t, v, s, g = ctx.t, ctx.v, ctx.s, ctx.g

    tmp_raw = [ctx.sbuf(f"tr{i}", [128, HID1], I32, side="right") for i in range(2)]
    tmp_bf = [ctx.sbuf(f"tb{i}", [128, HID1], BF16, side="right") for i in range(2)]
    stg2 = [ctx.sbuf(f"stg{i}", [128, 1024], BF16, side="right") for i in range(2)]
    pstg = [ctx.psum(f"pstg{i}", [128, 128], BF16) for i in range(2)]
    lds = [ctx.dma_stream(f"plds{i}") for i in range(2)]
    sts = [ctx.dma_stream(f"psts{i}") for i in range(2)]
    for (src, scr, rows, cols) in ((wg, wg_s, HID1, D), (wu, wu_s, HID1, D),
                                   (wd, wd_s, D, HID1)):
        emit_cast_store_T(ctx, src, scr, rows, cols, tmp_raw, tmp_bf, pstg,
                          stg2, lds, sts)
    ctx.free(stg2[1], stg2[0], tmp_bf[1], tmp_bf[0], tmp_raw[1], tmp_raw[0])
    frF = ctx.free(pstg[1], pstg[0])

    xf = ctx.sbuf("xf", [128, ND * T], BF16)
    if fuse_sum:
        resx = [ctx.sbuf(f"resx{i}", [128, D], F32) for i in range(2)]
        resa = [ctx.sbuf(f"resa{i}", [128, D], F32) for i in range(2)]
        lrx = [ctx.dma_stream(f"lrx{i}") for i in range(2)]
        lra = [ctx.dma_stream(f"lra{i}") for i in range(2)]
    wgt = [ctx.sbuf(f"wgt{i}", [128, ND * FB], BF16) for i in range(2)]
    wut = [ctx.sbuf(f"wut{i}", [128, ND * FB], BF16) for i in range(2)]
    wdt = [ctx.sbuf(f"wdt{i}", [128, (FB // 128) * D], BF16) for i in range(2)]
    gsil2 = [ctx.sbuf(f"gsil{i}", [128, 512], BF16) for i in range(2)]
    ht2 = [ctx.sbuf(f"ht{i}", [128, (FB // 128) * T], BF16) for i in range(2)]
    yacc = ctx.sbuf("yacc", [128, NT * D], F32)
    ps_g = [ctx.psum(f"ps_g{i}", [128, 512], F32) for i in range(2)]
    ps_u = [ctx.psum(f"ps_u{i}", [128, 512], F32) for i in range(2)]
    ps_y = [ctx.psum(f"ps_y{i}", [128, D], F32) for i in range(2)]

    ldx = ctx.dma_stream("ldx")
    ldw = [ctx.dma_stream(f"ldw{i}") for i in range(2)]
    stw = ctx.dma_stream("stw")
    SCALE = 1.0 / 127.0

    for b in range(B_CORE):
        ctx.dma_op("sync", ldx,
                   [(xf.h[:, c * T:(c + 1) * T], xnT.h[b, c * 128:(c + 1) * 128, :])
                    for c in range(ND)], [xf], [xnT])
        for fb in range(NFB):
            i = fb % 2
            ht = ht2[i]
            f0 = fb * FB
            for c in range(ND):
                ctx.dma_op("sync" if c % 2 == 0 else "scalar", ldw[i],
                           [(wgt[i].h[:, c * FB:(c + 1) * FB],
                             wg_s.h[c * 128:(c + 1) * 128, f0:f0 + FB]),
                            (wut[i].h[:, c * FB:(c + 1) * FB],
                             wu_s.h[c * 128:(c + 1) * 128, f0:f0 + FB])],
                           [wgt[i], wut[i]], [wg_s, wu_s])
            for fc in range(FB // 128):
                ctx.dma_op("sync", ldw[i],
                           [(wdt[i].h[:, fc * D:(fc + 1) * D],
                             wd_s.h[f0 + fc * 128:f0 + (fc + 1) * 128, 0:D])],
                           [wdt[i]], [wd_s])
            for fc in range(FB // 128):
                for ts in range(T // 512):
                    pg = ps_g[(fc * (T // 512) + ts) % 2]
                    pu = ps_u[(fc * (T // 512) + ts) % 2]
                    mm_group(ctx, pg, pg[:, 0:512],
                             [(wgt[i].h[:, c * FB + fc * 128:c * FB + fc * 128 + 128],
                               xf.h[:, c * T + ts * 512:c * T + (ts + 1) * 512],
                               [wgt[i], xf]) for c in range(ND)])
                    mm_group(ctx, pu, pu[:, 0:512],
                             [(wut[i].h[:, c * FB + fc * 128:c * FB + fc * 128 + 128],
                               xf.h[:, c * T + ts * 512:c * T + (ts + 1) * 512],
                               [wut[i], xf]) for c in range(ND)])
                    gs = gsil2[ts % 2]
                    s.op("activation", [gs], [pg], gs[:],
                         pg[:, 0:512], AF.Silu, scale=SCALE)
                    v.op("tensor_tensor", [ht], [gs, pu],
                         ht.h[:, fc * T + ts * 512:fc * T + (ts + 1) * 512],
                         gs[:], pu[:, 0:512], op=ALU.mult)
            for it in range(NT):
                py = ps_y[it % 2]
                terms = []
                for fc in range(FB // 128):
                    for dh in range(2):
                        terms.append((ht.h[:, fc * T + it * 128:fc * T + (it + 1) * 128],
                                      wdt[i].h[:, fc * D + dh * 512:fc * D + (dh + 1) * 512],
                                      [ht, wdt[i]]))
                terms = [terms[fc * 2 + dh] for dh in range(2) for fc in range(FB // 128)]
                for dh in range(2):
                    grp = [(ht.h[:, fc * T + it * 128:fc * T + (it + 1) * 128],
                            wdt[i].h[:, fc * D + dh * 512:fc * D + (dh + 1) * 512],
                            [ht, wdt[i]]) for fc in range(FB // 128)]
                    mm_group(ctx, py, py[:, dh * 512:(dh + 1) * 512], grp)
                if fb == 0:
                    v.op("tensor_scalar", [yacc], [py],
                         yacc.h[:, it * D:(it + 1) * D], py[:, 0:D],
                         SCALE * SCALE, None, op0=ALU.mult, op1=ALU.bypass)
                else:
                    v.op("scalar_tensor_tensor", [yacc], [py],
                         yacc.h[:, it * D:(it + 1) * D], py[:, 0:D],
                         SCALE * SCALE, yacc.h[:, it * D:(it + 1) * D],
                         op0=ALU.mult, op1=ALU.add)
        if fuse_sum:
            for it in range(NT):
                i2 = it % 2
                r0 = b * T + it * 128
                ctx.dma_op("sync", lrx[i2], [(resx[i2][:], ext["x1"].h[r0:r0 + 128, :])],
                           [resx[i2]], [ext["x1"]])
                ctx.dma_op("scalar", lra[i2], [(resa[i2][:], ext["a1"].h[r0:r0 + 128, :])],
                           [resa[i2]], [ext["a1"]])
                v.op("tensor_tensor", [yacc], [resx[i2]],
                     yacc.h[:, it * D:(it + 1) * D],
                     yacc.h[:, it * D:(it + 1) * D], resx[i2][:], op=ALU.add)
                v.op("tensor_tensor", [yacc], [resa[i2]],
                     yacc.h[:, it * D:(it + 1) * D],
                     yacc.h[:, it * D:(it + 1) * D], resa[i2][:], op=ALU.add)
                ctx.dma_op("gpsimd", stw,
                           [(out.h[r0:r0 + 128, :], yacc.h[:, it * D:(it + 1) * D])],
                           [out], [yacc])
        else:
            ctx.dma_op("gpsimd", stw,
                       [(out.h[b * T + it * 128:b * T + (it + 1) * 128, :],
                         yacc.h[:, it * D:(it + 1) * D]) for it in range(NT)],
                       [out], [yacc])
    g.raw(lambda eng, s_=stw.sem, v_=stw.count: eng.wait_ge(s_, v_))


def build_f1_launch():
    ctx = Ctx()
    ext = {}
    ext["xnT"] = ctx.dram("xnT", [B_CORE, D, T], BF16, kind="ExternalInput")
    ext["w_gate"] = ctx.dram("w_gate", [HID1, D], I32, kind="ExternalInput")
    ext["w_up"] = ctx.dram("w_up", [HID1, D], I32, kind="ExternalInput")
    ext["w_down"] = ctx.dram("w_down", [D, HID1], I32, kind="ExternalInput")
    ext["f1out"] = ctx.dram("f1out", [B_CORE * T, D], F32, kind="ExternalOutput")
    emit_f1(ctx, ext)
    return ctx.emit()


CHUNK = 128
NCH = T // CHUNK


def emit_gla(ctx, ext):
    xnT = ext["xnT"]
    w_hbm = {n: ext[n] for n in ("W_q", "W_k", "W_v", "W_gate", "W_o")}
    bg_hbm = ext["b_gate"]
    out = ext["a0out"]

    t, v, s, g = ctx.t, ctx.v, ctx.s, ctx.g
    ident = ctx.identity(BF16)
    iot = ctx._iota
    maskb = ctx.sbuf("maskb", [128, 128], BF16)
    v.op("tensor_single_scalar", [maskb], [iot], maskb[:], iot[:], 0.0,
         op=ALU.is_ge)

    wts = {}
    wraw = [ctx.sbuf(f"wraw{i}", [128, D], F32, side="right") for i in range(2)]
    lw = [ctx.dma_stream(f"lw{i}") for i in range(2)]
    for n in ("W_q", "W_k", "W_v", "W_gate", "W_o"):
        wt = ctx.sbuf("b" + n, [128, ND * D], BF16)
        for c in range(ND):
            i = c % 2
            ctx.dma_op("sync", lw[i],
                       [(wraw[i][:], w_hbm[n].h[c * 128:(c + 1) * 128, :])],
                       [wraw[i]], [w_hbm[n]])
            v.op("tensor_copy", [wt], [wraw[i]],
                 wt.h[:, c * D:(c + 1) * D], wraw[i][:])
        wts[n] = wt
    ctx.free(wraw[1], wraw[0])
    bgt = ctx.sbuf("bgt", [128, ND], F32)
    lbg = ctx.dma_stream("lbg")
    ctx.dma_op("sync", lbg,
               [(bgt.h[:, c:c + 1], bg_hbm.h[c * 128:(c + 1) * 128]) for c in range(ND)],
               [bgt], [bg_hbm])

    xf = ctx.sbuf("xf", [128, ND * T], BF16)
    vT = ctx.sbuf("vT", [128, NT * D], BF16)
    oF = ctx.sbuf("oF", [128, ND * T], BF16)
    at = ctx.sbuf("at", [128, T], F32)
    bt = ctx.sbuf("bt", [128, T], F32)
    qt = ctx.sbuf("qt", [128, T], BF16)
    kt = ctx.sbuf("kt", [128, T], BF16)
    khc = [ctx.sbuf(f"khc{i}", [128, 128], BF16) for i in range(2)]
    khT = [ctx.sbuf(f"khT{i}", [128, 128], BF16) for i in range(2)]
    sTm = [ctx.sbuf(f"sTm{i}", [128, 128], BF16) for i in range(2)]
    St = ctx.sbuf("St", [128, 64], F32)
    Sb = ctx.sbuf("Sb", [128, 64], BF16)

    yo = [Buf(at.h[:, 0:D], "yo0"), Buf(at.h[:, D:2 * D], "yo1")]
    ldx = ctx.dma_stream("ldx")
    stw = ctx.dma_stream("stw")

    for b in range(B_CORE):
        alias_pull(at, yo[0], yo[1])
        ctx.dma_op("sync", ldx,
                   [(xf.h[:, c * T:(c + 1) * T], xnT.h[b, c * 128:(c + 1) * 128, :])
                    for c in range(ND)], [xf], [xnT])
        pv = [ctx.psum(f"pv{b}_{i}", [128, D], F32) for i in range(2)]
        for it in range(NT):
            p = pv[it % 2]
            for dh in range(2):
                grp = [(xf.h[:, c * T + it * 128:c * T + (it + 1) * 128],
                        wts["W_v"].h[:, c * D + dh * 512:c * D + (dh + 1) * 512],
                        [xf, wts["W_v"]]) for c in range(ND)]
                mm_group(ctx, p, p[:, dh * 512:(dh + 1) * 512], grp)
            v.op("tensor_copy", [vT], [p], vT.h[:, it * D:(it + 1) * D], p[:])
        fr = ctx.free(pv[1], pv[0])
        pz = [ctx.psum(f"pz{b}_{i}", [128, 512], F32, after=fr) for i in range(2)]
        psA = [ctx.psum(f"psA{b}_{i}", [128, 128], F32) for i in range(2)]
        po = [ctx.psum(f"po{b}_{i}", [64, 128], F32) for i in range(2)]
        pD = [ctx.psum(f"pD{b}_0", [64, 64], F32)] * 2
        pT = ctx.psum(f"pT{b}", [128, 128], BF16)
        for hp in range(ND):
            for ts in range(4):
                p = pz[ts % 2]
                grp = [(wts["W_gate"].h[:, c * D + hp * 128:c * D + hp * 128 + 128],
                        xf.h[:, c * T + ts * 512:c * T + (ts + 1) * 512],
                        [wts["W_gate"], xf]) for c in range(ND)]
                mm_group(ctx, p, p[:, 0:512], grp)
                s.op("activation", [at], [p, bgt], at.h[:, ts * 512:(ts + 1) * 512],
                     p[:, 0:512], AF.Sigmoid, bias=bgt.h[:, hp:hp + 1])
            for c in range(NCH):
                v.op("tensor_tensor_scan", [bt], [at],
                     bt.h[:, c * CHUNK:(c + 1) * CHUNK],
                     at.h[:, c * CHUNK:(c + 1) * CHUNK],
                     at.h[:, c * CHUNK:(c + 1) * CHUNK], 1.0,
                     op0=ALU.mult, op1=ALU.bypass)
            s.op("activation", [at], [bt], at.h[:, :], bt.h[:, :], AF.Ln)
            s.op("activation", [at], [at], at.h[:, :], at.h[:, :], AF.Exp,
                 scale=-1.0)
            for ts in range(4):
                for (wn, dst, other) in (("W_q", qt, bt), ("W_k", kt, at)):
                    p = pz[ts % 2]
                    grp = [(wts[wn].h[:, c * D + hp * 128:c * D + hp * 128 + 128],
                            xf.h[:, c * T + ts * 512:c * T + (ts + 1) * 512],
                            [wts[wn], xf]) for c in range(ND)]
                    mm_group(ctx, p, p[:, 0:512], grp)
                    v.op("tensor_tensor", [dst], [p, other],
                         dst.h[:, ts * 512:(ts + 1) * 512], p[:, 0:512],
                         other.h[:, ts * 512:(ts + 1) * 512], op=ALU.mult)
            g.op("memset", [St], [], St[:], 0.0)
            v.op("tensor_copy", [Sb], [St], Sb[:], St[:])
            for c in range(NCH):
                i = c % 2
                cs = slice(c * CHUNK, (c + 1) * CHUNK)
                t.op("matmul", [psA[0]], [kt, qt], psA[0][:, 0:128],
                     kt.h[0:64, cs], qt.h[0:64, cs], start=True, stop=True,
                     tile_position=(0, 0))
                t.op("matmul", [psA[1]], [kt, qt], psA[1][:, 0:128],
                     kt.h[64:128, cs], qt.h[64:128, cs], start=True, stop=True,
                     tile_position=(64, 0))
                for h in range(2):
                    v.op("tensor_tensor", [sTm[h]], [psA[h], maskb], sTm[h][:],
                         psA[h][:, 0:128], maskb[:], op=ALU.mult)
                v.op("tensor_scalar", [khc[i]], [kt, bt], khc[i][:],
                     kt.h[:, cs],
                     bt.h[:, c * CHUNK + CHUNK - 1:c * CHUNK + CHUNK], None,
                     op0=ALU.mult, op1=ALU.bypass)
                t.op("transpose", [pT], [khc[i], ident], pT[:, 0:128],
                     khc[i][:], ident[:])
                v.op("tensor_copy", [khT[i]], [pT], khT[i][:], pT[:, 0:128])
                for h in range(2):
                    hb = 64 * h
                    t.op("matmul", [], [vT, sTm[h]], po[h][0:64, 0:128],
                         vT.h[:, c * D + (hp * 2 + h) * 64:c * D + (hp * 2 + h) * 64 + 64],
                         sTm[h][:], start=True, stop=False, inc=False)
                    t.op("matmul", [po[h]], [Sb, qt], po[h][0:64, 0:128],
                         Sb.h[hb:hb + 64, 0:64], qt.h[hb:hb + 64, cs],
                         start=False, stop=True,
                         tile_position=(hb, 0))
                    t.op("matmul", [pD[h]], [khT[i], vT], pD[h][0:64, 0:64],
                         khT[i][:, hb:hb + 64],
                         vT.h[:, c * D + (hp * 2 + h) * 64:c * D + (hp * 2 + h) * 64 + 64],
                         start=True, stop=True)
                    v.op("scalar_tensor_tensor", [St], [pD[h], bt],
                         St.h[hb:hb + 64, 0:64], St.h[hb:hb + 64, 0:64],
                         bt.h[hb:hb + 64, c * CHUNK + CHUNK - 1:c * CHUNK + CHUNK],
                         pD[h][0:64, 0:64], op0=ALU.mult, op1=ALU.add)
                    v.op("tensor_copy", [oF], [po[h]],
                         oF.h[hb:hb + 64, hp * T + c * CHUNK:hp * T + (c + 1) * CHUNK],
                         po[h][0:64, 0:128])
                v.op("tensor_copy", [Sb], [St], Sb[:], St[:])
        frr = ctx.free(pT, pD[0], po[1], po[0], psA[1], psA[0], pz[1], pz[0])
        pq = [ctx.psum(f"pq{b}_{i}", [128, D], F32, after=frr) for i in range(2)]
        for it in range(NT):
            p = pq[it % 2]
            for dh in range(2):
                grp = [(oF.h[:, c * T + it * 128:c * T + (it + 1) * 128],
                        wts["W_o"].h[:, c * D + dh * 512:c * D + (dh + 1) * 512],
                        [oF, wts["W_o"]]) for c in range(ND)]
                mm_group(ctx, p, p[:, dh * 512:(dh + 1) * 512], grp)
            v.op("tensor_copy", [yo[it % 2]], [p, at], yo[it % 2].h, p[:])
            ctx.dma_op("gpsimd", stw,
                       [(out.h[b * T + it * 128:b * T + (it + 1) * 128, :],
                         yo[it % 2].h)], [out], [yo[it % 2]])
        ctx.free(pq[1], pq[0])
    g.raw(lambda eng, s_=stw.sem, v_=stw.count: eng.wait_ge(s_, v_))


def build_gla_launch():
    ctx = Ctx()
    ext = {}
    for n in ("W_q", "W_k", "W_v", "W_gate", "W_o"):
        ext[n] = ctx.dram(n, [D, D], F32, kind="ExternalInput")
    ext["b_gate"] = ctx.dram("b_gate", [D], F32, kind="ExternalInput")
    ext["a0out"] = ctx.dram("a0out", [B_CORE * T, D], F32, kind="ExternalOutput")
    ext["xnT"] = ctx.dram("xnT", [B_CORE, D, T], BF16, kind="ExternalInput")
    emit_gla(ctx, ext)
    return ctx.emit()

def emit_transpose_weight(ctx, src_hbm, wt, rows, cols, tmp32, tmpbf, pst, ld,
                          row0=0):
    ident = ctx.identity(BF16)
    nro, nci = rows // 128, cols // 128
    for ro in range(nro):
        i = ro % 2
        ctx.dma_op("sync", ld[i],
                   [(tmp32[i].h[:, 0:cols], src_hbm.h[row0 + ro * 128:row0 + (ro + 1) * 128, :])],
                   [tmp32[i]], [src_hbm])
        ctx.v.op("tensor_copy", [tmpbf[i]], [tmp32[i]], tmpbf[i].h[:, 0:cols],
                 tmp32[i].h[:, 0:cols])
        for ci in range(nci):
            ctx.t.op("transpose", [pst], [tmpbf[i], ident], pst[:, 0:128],
                     tmpbf[i].h[:, ci * 128:(ci + 1) * 128], ident[:])
            ctx.v.op("tensor_copy", [wt], [pst],
                     wt.h[:, ci * rows + ro * 128:ci * rows + (ro + 1) * 128],
                     pst[:, 0:128])


def emit_row_load(ctx, dst_row_ap, src_hbm, n, dstbuf, ld, cast_via=None):
    if cast_via is None:
        ctx.dma_op("sync", ld, [(dst_row_ap, src_hbm.h[0:n])], [dstbuf], [src_hbm])
    else:
        via, viabuf = cast_via
        ctx.dma_op("sync", ld, [(via, src_hbm.h[0:n])], [viabuf], [src_hbm])
        ctx.v.op("tensor_copy", [dstbuf], [viabuf], dst_row_ap, via)



def emit_f0conv(ctx, ext):
    xnT = ext["xnT"]
    conv_w = ext["conv_w"]
    conv_b = ext["conv_b"]
    wqkv = ext["wqkv"]
    bqkv = ext["bqkv"]
    qh_o = ext["qh"]
    kh_o = ext["kh"]
    vp_o = ext["vp"]
    ybT_o = ext["ybT"]

    t, v, s, g = ctx.t, ctx.v, ctx.s, ctx.g
    ident = ctx.identity(BF16)

    wqkvT = ctx.sbuf("wqkvT", [128, ND * 3 * D], BF16)
    convT = ctx.sbuf("convT", [128, 3 * ND * D], BF16)
    colv = ctx.sbuf("colv", [128, 16], F32)
    convb_c = ctx.sbuf("convb_c", [128, ND], F32)
    rows_bf = ctx.sbuf("rows_bf", [1, 1024], BF16)
    rowf = ctx.sbuf("rowf", [1, 1024], F32)
    onesb = ctx.sbuf("onesb", [1, 128], BF16)
    g.op("memset", [onesb], [], onesb[:], 1.0)

    lw = [ctx.dma_stream(f"lw{i}") for i in range(2)]
    tmp32 = [ctx.sbuf(f"tmp32_{i}", [128, 3 * D], F32, side="right") for i in range(2)]
    tmpbf = [ctx.sbuf(f"tmpbf_{i}", [128, 3 * D], BF16, side="right") for i in range(2)]
    pst0 = ctx.psum("pst0", [128, 128], BF16)
    emit_transpose_weight(ctx, wqkv, wqkvT, 3 * D, D, tmp32, tmpbf, pst0, lw)
    for oc in range(ND):
        i = oc % 2
        ctx.dma_op("sync", lw[i],
                   [(tmp32[i].h[:, 0:3 * D], conv_w.h[oc * 128:(oc + 1) * 128, :])],
                   [tmp32[i]], [conv_w])
        v.op("tensor_copy", [tmpbf[i]], [tmp32[i]], tmpbf[i].h[:, 0:3 * D],
             tmp32[i].h[:, 0:3 * D])
        for tap in range(3):
            for ic in range(ND):
                src = tmpbf[i].h[:, ic * 384 + tap: ic * 384 + 384: 3]
                t.op("transpose", [pst0], [tmpbf[i], ident], pst0[:, 0:128],
                     src, ident[:])
                v.op("tensor_copy", [convT], [pst0],
                     convT.h[:, tap * ND * D + ic * D + oc * 128:
                             tap * ND * D + ic * D + (oc + 1) * 128],
                     pst0[:, 0:128])
    lb = ctx.dma_stream("lb")
    for c in range(ND):
        ctx.dma_op("sync", lb, [(colv.h[:, c:c + 1], bqkv.h[c * 128:(c + 1) * 128])],
                   [colv], [bqkv])
        ctx.dma_op("sync", lb, [(colv.h[:, 8 + c:9 + c], bqkv.h[D + c * 128:D + (c + 1) * 128])],
                   [colv], [bqkv])
        ctx.dma_op("sync", lb, [(convb_c.h[:, c:c + 1], conv_b.h[c * 128:(c + 1) * 128])],
                   [convb_c], [conv_b])
    ctx.dma_op("sync", lb, [(rowf.h[0:1, 0:1024], bqkv.h[2 * D:3 * D])], [rowf], [bqkv])
    v.op("tensor_copy", [rows_bf], [rowf], rows_bf.h[0:1, 0:1024], rowf.h[0:1, 0:1024])
    ctx.free(tmpbf[1], tmpbf[0], tmp32[1], tmp32[0])

    yb = ctx.sbuf("yb", [128, ND * T], BF16, side="right")
    out_st = [ctx.sbuf(f"ost{i}", [128, 1040], BF16, side="right") for i in range(2)]
    ldx = ctx.dma_stream("ldx")
    so = [ctx.dma_stream(f"so{i}") for i in range(2)]
    pcv = [ctx.psum(f"pcv{i}", [128, 512], F32) for i in range(2)]
    pTt = ctx.psum("pTt", [128, 128], BF16)
    pv = [ctx.psum(f"pv{i}", [128, 128], F32) for i in range(2)]

    for b in range(B_CORE):
        xf = ctx.sbuf(f"xf{b}", [128, ND * T], BF16, side="right")
        ctx.dma_op("sync", ldx,
                   [(xf.h[:, c * T:(c + 1) * T], xnT.h[b, c * 128:(c + 1) * 128, :])
                    for c in range(ND)], [xf], [xnT])
        for oc in range(ND):
            for ts in range(4):
                p = pcv[(oc * 4 + ts) % 2]
                first = True
                nterm = 0
                ops = []
                for tap in range(3):
                    off = tap - 1
                    lo = max(0, ts * 512 + off)
                    hi = min(T, (ts + 1) * 512 + off)
                    plo = lo - off - ts * 512
                    phi = hi - off - ts * 512
                    for ic in range(ND):
                        ops.append((tap, ic, lo, hi, plo, phi))
                for j, (tap, ic, lo, hi, plo, phi) in enumerate(ops):
                    lT = convT.h[:, tap * ND * D + ic * D + oc * 128:
                                 tap * ND * D + ic * D + (oc + 1) * 128]
                    rT = xf.h[:, ic * T + lo:ic * T + hi]
                    last = (j == len(ops) - 1)
                    first_ = (j == 0)
                    t.op("matmul", [p] if (first_ or last) else [],
                         [convT, xf], p[:, plo:phi], lT, rT,
                         start=first_, stop=last, inc=last)
                v.op("tensor_scalar", [yb], [p, convb_c],
                     yb.h[:, oc * T + ts * 512:oc * T + (ts + 1) * 512],
                     p[:, 0:512], convb_c.h[:, oc:oc + 1], None,
                     op0=ALU.add, op1=ALU.bypass)
        ctx.free(xf)
        for it in range(NT):
            i = it % 2
            for oc in range(ND):
                t.op("transpose", [pTt], [yb, ident], pTt[:, 0:128],
                     yb.h[:, oc * T + it * 128:oc * T + (it + 1) * 128], ident[:])
                v.op("tensor_copy", [out_st[i]], [pTt],
                     out_st[i].h[:, oc * 128:(oc + 1) * 128], pTt[:, 0:128])
            ctx.dma_op("gpsimd", so[i],
                       [(ybT_o.h[b * T + it * 128:b * T + (it + 1) * 128, :],
                         out_st[i][:, 0:1024])], [ybT_o], [out_st[i]])
        for (name, oh, bcol, scale) in (("q", qh_o, 0, 0.125), ("k", kh_o, 8, 1.0)):
            for hp in range(ND):
                outcol = hp * 128 if name == "q" else D + hp * 128
                for ts in range(4):
                    i = ts % 2
                    p = pcv[i]
                    grp = [(wqkvT.h[:, c * 3 * D + outcol:c * 3 * D + outcol + 128],
                            yb.h[:, c * T + ts * 512:c * T + (ts + 1) * 512],
                            [wqkvT, yb]) for c in range(ND)]
                    mm_group(ctx, p, p[:, 0:512], grp)
                    v.op("tensor_scalar", [out_st[i]], [p, colv],
                         out_st[i].h[:, 0:512], p[:, 0:512],
                         colv.h[:, bcol + hp:bcol + hp + 1], scale,
                         op0=ALU.add, op1=ALU.mult)
                    ctx.dma_op("gpsimd", so[i],
                               [(oh.h[b, hp * 128:(hp + 1) * 128, ts * 512:(ts + 1) * 512],
                                 out_st[i][:, 0:512])], [oh], [out_st[i]])
        for it in range(NT):
            i = it % 2
            for hp in range(ND):
                p = pv[i]
                t.op("matmul", [p], [onesb, rows_bf], p[:, 0:128],
                     onesb.h[0:1, 0:128], rows_bf.h[0:1, hp * 128:(hp + 1) * 128],
                     start=True, stop=False, inc=False)
                grp_ins = [yb, wqkvT]
                for c in range(ND):
                    last = (c == ND - 1)
                    t.op("matmul", [p] if last else [], grp_ins, p[:, 0:128],
                         yb.h[:, c * T + it * 128:c * T + (it + 1) * 128],
                         wqkvT.h[:, c * 3 * D + 2 * D + hp * 128:c * 3 * D + 2 * D + (hp + 1) * 128],
                         start=False, stop=last, inc=last)
                v.op("tensor_copy", [out_st[i]], [p],
                     out_st[i].h[:, hp * 130:hp * 130 + 64], p[:, 0:64])
                v.op("tensor_copy", [out_st[i]], [p],
                     out_st[i].h[:, hp * 130 + 65:hp * 130 + 129], p[:, 64:128])
            v.op("memset", [out_st[i]], [], out_st[i].h[:, 64:1040:130], 1.0)
            v.op("memset", [out_st[i]], [], out_st[i].h[:, 129:1040:130], 1.0)
            ctx.dma_op("gpsimd", so[i],
                       [(vp_o.h[b, 0:128, it * 1040:(it + 1) * 1040],
                         out_st[i][:, 0:1040])], [vp_o], [out_st[i]])
    g.raw(lambda eng, s_=so[0].sem, v_=so[0].count: eng.wait_ge(s_, v_))
    g.raw(lambda eng, s_=so[1].sem, v_=so[1].count: eng.wait_ge(s_, v_))


def build_f0conv_launch():
    ctx = Ctx()
    ext = {}
    ext["xnT"] = ctx.dram("xnT", [B_CORE, D, T], BF16, kind="ExternalInput")
    ext["conv_w"] = ctx.dram("conv_w", [D, D * 3], F32, kind="ExternalInput")
    ext["conv_b"] = ctx.dram("conv_b", [D], F32, kind="ExternalInput")
    ext["wqkv"] = ctx.dram("wqkv", [3 * D, D], F32, kind="ExternalInput")
    ext["bqkv"] = ctx.dram("bqkv", [3 * D], F32, kind="ExternalInput")
    ext["qh"] = ctx.dram("qh", [B_CORE, D, T], BF16, kind="ExternalOutput")
    ext["kh"] = ctx.dram("kh", [B_CORE, D, T], BF16, kind="ExternalOutput")
    ext["vp"] = ctx.dram("vp", [B_CORE, 128, NT * 1040], BF16, kind="ExternalOutput")
    ext["ybT"] = ctx.dram("ybT", [B_CORE * T, D], BF16, kind="ExternalOutput")
    emit_f0conv(ctx, ext)
    return ctx.emit()


def emit_f0attn(ctx, ext):
    qh_i = ext["qh"]
    kh_i = ext["kh"]
    vp_i = ext["vp"]
    ybT_i = ext["ybT"]
    wo_h = ext["wo"]
    bo_h = ext["bo"]
    ln1g_h = ext["ln1_g"]
    ln1b_h = ext["ln1_b"]
    xl1T_o = ext["xl1T"]
    xl1_o = ext["xl1"]

    t, v, s, g = ctx.t, ctx.v, ctx.s, ctx.g
    eps_c = ctx.const(1e-5)
    ident = ctx.identity(BF16)

    woT = ctx.sbuf("woT", [128, ND * D], BF16)
    ln1g_c = ctx.sbuf("ln1g_c", [128, ND], F32)
    ln1b_c = ctx.sbuf("ln1b_c", [128, ND], F32)
    rows_bf = ctx.sbuf("rows_bf", [1, 1024], BF16)
    rowf = ctx.sbuf("rowf", [1, 1024], F32)
    onesb = ctx.sbuf("onesb", [1, 128], BF16)
    onescols = ctx.sbuf("onescols", [128, 2], BF16)
    gbc = ctx.sbuf("gbc", [128, D], F32)
    bbc = ctx.sbuf("bbc", [128, D], F32)
    stats = ctx.sbuf("stats", [128, 8], F32)
    statsl = make_stat_slices(ctx, stats, 2)
    g.op("memset", [onesb], [], onesb[:], 1.0)
    g.op("memset", [onescols], [], onescols[:], 0.0)
    g.op("memset", [onescols], [], onescols.h[0:64, 0:1], 1.0)
    g.op("memset", [onescols], [], onescols.h[0:64, 1:2], 1.0)

    lw = [ctx.dma_stream(f"lw{i}") for i in range(2)]
    tmp32 = [ctx.sbuf(f"tmp32_{i}", [128, D], F32, side="right") for i in range(2)]
    tmpbf = [ctx.sbuf(f"tmpbf_{i}", [128, D], BF16, side="right") for i in range(2)]
    pst0 = ctx.psum("pst0", [128, 128], BF16)
    emit_transpose_weight(ctx, wo_h, woT, D, D, tmp32, tmpbf, pst0, lw)
    lb = ctx.dma_stream("lb")
    for c in range(ND):
        ctx.dma_op("sync", lb, [(ln1g_c.h[:, c:c + 1], ln1g_h.h[c * 128:(c + 1) * 128])],
                   [ln1g_c], [ln1g_h])
        ctx.dma_op("sync", lb, [(ln1b_c.h[:, c:c + 1], ln1b_h.h[c * 128:(c + 1) * 128])],
                   [ln1b_c], [ln1b_h])
    ctx.dma_op("sync", lb, [(rowf.h[0:1, 0:1024], bo_h.h[0:D])], [rowf], [bo_h])
    v.op("tensor_copy", [rows_bf], [rowf], rows_bf.h[0:1, 0:1024], rowf.h[0:1, 0:1024])
    rbf2 = ctx.sbuf("rbf2", [1, 2 * D], BF16)
    ctx.dma_op("sync", lb, [(rowf.h[0:1, 0:1024], ln1g_h.h[0:D])], [rowf], [ln1g_h])
    v.op("tensor_copy", [rbf2], [rowf], rbf2.h[0:1, 0:D], rowf.h[0:1, 0:1024])
    ctx.dma_op("sync", lb, [(rowf.h[0:1, 0:1024], ln1b_h.h[0:D])], [rowf], [ln1b_h])
    v.op("tensor_copy", [rbf2], [rowf], rbf2.h[0:1, D:2 * D], rowf.h[0:1, 0:1024])
    pgb = ctx.psum("pgb", [128, 512], F32)
    for (dst, off) in ((gbc, 0), (bbc, D)):
        for half in range(2):
            t.op("matmul", [pgb], [onesb, rbf2], pgb[:, 0:512],
                 onesb.h[0:1, 0:128], rbf2.h[0:1, off + half * 512:off + (half + 1) * 512],
                 start=True, stop=True)
            v.op("tensor_copy", [dst], [pgb], dst.h[:, half * 512:(half + 1) * 512],
                 pgb[:, 0:512])
    ctx.free(tmpbf[1], tmpbf[0], tmp32[1], tmp32[0])
    frQ = ctx.free(pgb, pst0)

    ldb = ctx.dma_stream("ldb")
    ldp = ctx.dma_stream("ldp")
    sto1 = ctx.dma_stream("sto1")
    sto2 = ctx.dma_stream("sto2")

    for b in range(B_CORE):
        ybT = ctx.sbuf(f"ybT{b}", [128, NT * D], BF16, side="right")
        ctxF = ctx.sbuf(f"ctxF{b}", [128, ND * T], BF16, side="right")
        vpa = ctx.sbuf(f"vpa{b}", [128, NT * 1040], BF16, side="right")
        qp = ctx.sbuf(f"qp{b}", [65, 2 * T], BF16, side="right")
        kp = ctx.sbuf(f"kp{b}", [65, 2 * T], BF16, side="right")
        qnr = ctx.sbuf(f"qnr{b}", [1, 2 * T], BF16, side="right")
        knr = ctx.sbuf(f"knr{b}", [1, T], BF16, side="right")
        sqk2 = [ctx.sbuf(f"sqk{b}_{i}", [128, 512], BF16, side="right") for i in range(4)]
        sqk = sqk2[0]
        bcb = ctx.sbuf(f"bcb{b}", [64, 1024], BF16, side="right")
        rr = ctx.sbuf(f"rr{b}", [1, 1024], F32, side="right")
        rrb = ctx.sbuf(f"rrb{b}", [1, 1024], BF16, side="right")
        pending = []
        km2 = ctx.sbuf(f"km2{b}", [1, 4], F32, side="right")
        ctx.dma_op("sync", ldb,
                   [(ybT.h[:, it * D:(it + 1) * D],
                     ybT_i.h[b * T + it * 128:b * T + (it + 1) * 128, :])
                    for it in range(NT)], [ybT], [ybT_i])
        ctx.dma_op("sync", ldb,
                   [(vpa.h[:, it * 1040:(it + 1) * 1040],
                     vp_i.h[b, 0:128, it * 1040:(it + 1) * 1040])
                    for it in range(NT)], [vpa], [vp_i])
        g.op("memset", [kp], [], kp.h[64:65, :], 1.0)
        pe_ = [ctx.psum(f"pe{b}_{i}", [128, 512], F32, after=frQ) for i in range(2)]
        pcx = [ctx.psum(f"pcx{b}_{h}", [65, 1024], F32) for h in range(2)]
        paux2 = [ctx.psum(f"paux{b}_{i}", [64, 512], F32) for i in range(2)]
        for hp in range(ND):
            ctx.dma_op("sync", ldp,
                       [(qp.h[0:64, 0:T], qh_i.h[b, hp * 128:hp * 128 + 64, :]),
                        (qp.h[0:64, T:2 * T], qh_i.h[b, hp * 128 + 64:hp * 128 + 128, :])],
                       [qp], [qh_i])
            ctx.dma_op("sync", ldp,
                       [(kp.h[0:64, 0:T], kh_i.h[b, hp * 128:hp * 128 + 64, :]),
                        (kp.h[0:64, T:2 * T], kh_i.h[b, hp * 128 + 64:hp * 128 + 128, :])],
                       [kp], [kh_i])
            for (src, dstrow, dcols) in ((qp, qnr, 2), (kp, knr, 1)):
                for h in range(2):
                    for ts in range(4):
                        sq = sqk2[ts % 2]
                        px = paux2[ts % 2]
                        s.op("activation", [sq], [src], sq.h[0:64, 0:512],
                             src.h[0:64, h * T + ts * 512:h * T + (ts + 1) * 512],
                             AF.Square)
                        t.op("matmul", [px], [sq, onescols], px[0:1, 0:512],
                             onescols.h[0:64, h:h + 1], sq.h[0:64, 0:512],
                             start=True, stop=True)
                        if dcols == 2:
                            s.op("activation", [dstrow], [px],
                                 dstrow.h[0:1, h * T + ts * 512:h * T + (ts + 1) * 512],
                                 px[0:1, 0:512], AF.Sqrt)
                        else:
                            v.op("tensor_copy", [dstrow], [px],
                                 dstrow.h[0:1, ts * 512:(ts + 1) * 512],
                                 px[0:1, 0:512])
                    if dcols == 1:
                        v.op("tensor_reduce", [km2], [dstrow], km2.h[0:1, h:h + 1],
                             dstrow.h[0:1, 0:T], axis=AX.X, op=ALU.max)
                        s.op("activation", [km2], [km2], km2.h[0:1, h:h + 1],
                             km2.h[0:1, h:h + 1], AF.Sqrt)
                        v.op("tensor_scalar", [qp], [qnr, km2],
                             qp.h[64:65, h * T:(h + 1) * T],
                             qnr.h[0:1, h * T:(h + 1) * T],
                             km2.h[0:1, h:h + 1], -1.0,
                             op0=ALU.mult, op1=ALU.mult)
            for ts in range(4):
                def ctx_mm(h, kc):
                    sq = sqk2[2 * h + kc % 2]
                    t.op("matmul", [pcx[h]] if kc == NT - 1 else [],
                         [vpa, sq], pcx[h][0:65, (ts % 2) * 512:(ts % 2) * 512 + 512],
                         vpa.h[:, kc * 1040 + (hp * 2 + h) * 65:kc * 1040 + (hp * 2 + h) * 65 + 65],
                         sq.h[:, 0:512],
                         start=(kc == 0), stop=(kc == NT - 1),
                         inc=(kc == NT - 1))
                for kc in range(NT):
                    for h in range(2):
                        p = pe_[h]
                        sq = sqk2[2 * h + kc % 2]
                        t.op("matmul", [p], [kp, qp], p[:, 0:512],
                             kp.h[0:65, h * T + kc * 128:h * T + (kc + 1) * 128],
                             qp.h[0:65, h * T + ts * 512:h * T + (ts + 1) * 512],
                             start=True, stop=True)
                        s.op("activation", [sq], [p], sq.h[:, 0:512], p[:, 0:512], AF.Exp)
                        if kc >= 1:
                            ctx_mm(h, kc - 1)
                for h in range(2):
                    ctx_mm(h, NT - 1)
                for h in range(2):
                    hb = 64 * h
                    j2 = (2 * ts + h) % 2
                    v.op("reciprocal", [rr], [pcx[h]], rr.h[0:1, j2 * 512:j2 * 512 + 512],
                         pcx[h][64:65, (ts % 2) * 512:(ts % 2) * 512 + 512])
                    v.op("tensor_copy", [rrb], [rr],
                         rrb.h[0:1, j2 * 512:j2 * 512 + 512],
                         rr.h[0:1, j2 * 512:j2 * 512 + 512])

                    def fin(h=h, ts=ts, hp=hp, hb=hb, j2=j2):
                        px = paux2[j2]
                        t.op("matmul", [px], [onesb, rrb], px[0:64, 0:512],
                             onesb.h[0:1, 0:64],
                             rrb.h[0:1, j2 * 512:j2 * 512 + 512],
                             start=True, stop=True)
                        v.op("tensor_copy", [bcb], [px],
                             bcb.h[0:64, j2 * 512:j2 * 512 + 512],
                             px[0:64, 0:512])
                        v.op("tensor_tensor", [ctxF], [pcx[h], bcb],
                             ctxF.h[hb:hb + 64, hp * T + ts * 512:hp * T + (ts + 1) * 512],
                             pcx[h][0:64, (ts % 2) * 512:(ts % 2) * 512 + 512],
                             bcb.h[0:64, j2 * 512:j2 * 512 + 512],
                             op=ALU.mult)
                    pending.append(fin)
                    if len(pending) > 1:
                        pending.pop(0)()
        for fn_ in pending:
            fn_()
        pending = []
        frA = ctx.free(km2, rrb, rr, bcb, sqk2[3], sqk2[2], sqk2[1], sqk2[0],
                       knr, qnr, kp, qp, vpa)
        frP = ctx.free(paux2[1], paux2[0], pcx[1], pcx[0], pe_[1], pe_[0])
        fmacc = ctx.sbuf(f"fmacc{b}", [128, ND * T], BF16, side="right")
        zt = [ctx.sbuf(f"zt{b}_{i}", [128, D], F32, side="right") for i in range(2)]
        xhb = [ctx.sbuf(f"xhb{b}_{i}", [128, D], BF16, side="right") for i in range(2)]
        xtm = [ctx.sbuf(f"xtm{b}_{i}", [128, D], F32, side="right") for i in range(2)]
        pz = [ctx.psum(f"pz{b}_{i}", [128, D], F32, after=frP) for i in range(2)]
        pTb = ctx.psum(f"pTb{b}", [128, 128], BF16)
        for it in range(NT):
            i = it % 2
            p = pz[i]
            for dh in range(2):
                t.op("matmul", [p], [onesb, rows_bf], p[:, dh * 512:(dh + 1) * 512],
                     onesb.h[0:1, 0:128], rows_bf.h[0:1, dh * 512:(dh + 1) * 512],
                     start=True, stop=False, inc=False)
                grp = [(ctxF.h[:, c * T + it * 128:c * T + (it + 1) * 128],
                        woT.h[:, c * D + dh * 512:c * D + (dh + 1) * 512],
                        [ctxF, woT]) for c in range(ND)]
                for j, (l, r, ins) in enumerate(grp):
                    last = (j == len(grp) - 1)
                    t.op("matmul", [p] if last else [], ins,
                         p[:, dh * 512:(dh + 1) * 512], l, r,
                         start=False, stop=last, inc=last)
            v.op("tensor_tensor", [zt[i]], [p, ybT], zt[i][:], p[:, 0:D],
                 ybT.h[:, it * D:(it + 1) * D], op=ALU.add)
            nmu, rstd = emit_ln_stats(ctx, zt[i], zt[i][:], statsl[i], eps_c, xtm[i])
            v.op("tensor_scalar", [xhb[i]], [zt[i], nmu, rstd], xhb[i][:],
                 zt[i][:], nmu.h, rstd.h, op0=ALU.add, op1=ALU.mult)
            v.op("scalar_tensor_tensor", [xtm[i]], [xhb[i], gbc], xtm[i][:],
                 gbc[:], 1.0, xhb[i][:], op0=ALU.mult, op1=ALU.mult)
            v.op("tensor_tensor", [xtm[i]], [xtm[i], bbc], xtm[i][:], xtm[i][:],
                 bbc[:], op=ALU.add)
            ctx.dma_op("gpsimd", sto1,
                       [(xl1_o.h[b * T + it * 128:b * T + (it + 1) * 128, :],
                         xtm[i][:])], [xl1_o], [xtm[i]])
            for c in range(ND):
                t.op("transpose", [pTb], [xhb[i], ident], pTb[:, 0:128],
                     xhb[i].h[:, c * 128:(c + 1) * 128], ident[:])
                v.op("tensor_scalar", [fmacc], [pTb, ln1g_c, ln1b_c],
                     fmacc.h[:, c * T + it * 128:c * T + (it + 1) * 128],
                     pTb[:, 0:128], ln1g_c.h[:, c:c + 1], ln1b_c.h[:, c:c + 1],
                     op0=ALU.mult, op1=ALU.add)
        ctx.dma_op("gpsimd", sto2,
                   [(xl1T_o.h[b, c * 128:(c + 1) * 128, :],
                     fmacc.h[:, c * T:(c + 1) * T]) for c in range(ND)],
                   [xl1T_o], [fmacc])
        frQ = ctx.free(pTb, pz[1], pz[0])
        ctx.free(xtm[1], xtm[0], xhb[1], xhb[0], zt[1], zt[0], fmacc)
        ctx.free(ctxF, ybT)
    g.raw(lambda eng, s_=sto1.sem, v_=sto1.count: eng.wait_ge(s_, v_))
    g.raw(lambda eng, s_=sto2.sem, v_=sto2.count: eng.wait_ge(s_, v_))


def build_f0attn_launch():
    ctx = Ctx()
    ext = {}
    ext["qh"] = ctx.dram("qh", [B_CORE, D, T], BF16, kind="ExternalInput")
    ext["kh"] = ctx.dram("kh", [B_CORE, D, T], BF16, kind="ExternalInput")
    ext["vp"] = ctx.dram("vp", [B_CORE, 128, NT * 1040], BF16, kind="ExternalInput")
    ext["ybT"] = ctx.dram("ybT", [B_CORE * T, D], BF16, kind="ExternalInput")
    ext["wo"] = ctx.dram("wo", [D, D], F32, kind="ExternalInput")
    ext["bo"] = ctx.dram("bo", [D], F32, kind="ExternalInput")
    ext["ln1_g"] = ctx.dram("ln1_g", [D], F32, kind="ExternalInput")
    ext["ln1_b"] = ctx.dram("ln1_b", [D], F32, kind="ExternalInput")
    ext["xl1T"] = ctx.dram("xl1T", [B_CORE, D, T], BF16, kind="ExternalOutput")
    ext["xl1"] = ctx.dram("xl1", [B_CORE * T, D], F32, kind="ExternalOutput")
    emit_f0attn(ctx, ext)
    return ctx.emit()


DFF = 4096
NEXP = 8
MFB = 512
NMFB = DFF // MFB


def emit_moe(ctx, ext):
    xl1T_i = ext["xl1T"]
    xl1_i = ext["xl1"]
    gw_h = ext["gate_w"]
    gb_h = ext["gate_b"]
    w1_h = ext["w1"]
    b1_h = ext["b1"]
    w2_h = ext["w2"]
    b2_h = ext["b2"]
    ln2g_h = ext["ln2_g"]
    ln2b_h = ext["ln2_b"]
    out_o = ext["f0out"]

    t, v, s, g = ctx.t, ctx.v, ctx.s, ctx.g
    eps_c = ctx.const(1e-5)

    onesb = ctx.sbuf("onesb", [1, 512], BF16)
    g.op("memset", [onesb], [], onesb[:], 1.0)
    gwb = ctx.sbuf("gwb", [128, ND * NEXP], BF16)
    gbrow = ctx.sbuf("gbrow", [1, NEXP], BF16)
    rowf = ctx.sbuf("rowf", [1, D], F32)
    b1c = ctx.sbuf("b1c", [128, NEXP * 32], F32)
    b2row = ctx.sbuf("b2row", [1, D], BF16)
    gbc = ctx.sbuf("gbc", [128, D], F32)
    bbc = ctx.sbuf("bbc", [128, D], F32)
    stats = ctx.sbuf("stats", [128, 8], F32)
    statsl = make_stat_slices(ctx, stats, 2)
    lb = ctx.dma_stream("lb")
    for c in range(ND):
        ctx.dma_op("sync", lb, [(gbc.h[:, 0:NEXP], gw_h.h[c * 128:(c + 1) * 128, :])],
                   [gbc], [gw_h])
        v.op("tensor_copy", [gwb], [gbc], gwb.h[:, c * NEXP:(c + 1) * NEXP],
             gbc.h[:, 0:NEXP])
    ctx.dma_op("sync", lb, [(rowf.h[0:1, 0:NEXP], gb_h.h[0:NEXP])], [rowf], [gb_h])
    v.op("tensor_copy", [gbrow], [rowf], gbrow.h[0:1, 0:NEXP], rowf.h[0:1, 0:NEXP])
    pgb = ctx.psum("pgb", [128, 512], F32)
    rbf2 = ctx.sbuf("rbf2", [1, 2 * D], BF16)
    ctx.dma_op("sync", lb, [(rowf.h[0:1, 0:1024], ln2g_h.h[0:D])], [rowf], [ln2g_h])
    v.op("tensor_copy", [rbf2], [rowf], rbf2.h[0:1, 0:D], rowf.h[0:1, 0:1024])
    ctx.dma_op("sync", lb, [(rowf.h[0:1, 0:1024], ln2b_h.h[0:D])], [rowf], [ln2b_h])
    v.op("tensor_copy", [rbf2], [rowf], rbf2.h[0:1, D:2 * D], rowf.h[0:1, 0:1024])
    for (dst, off) in ((gbc, 0), (bbc, D)):
        for half in range(2):
            t.op("matmul", [pgb], [onesb, rbf2], pgb[:, 0:512],
                 onesb.h[0:1, 0:128], rbf2.h[0:1, off + half * 512:off + (half + 1) * 512],
                 start=True, stop=True)
            v.op("tensor_copy", [dst], [pgb], dst.h[:, half * 512:(half + 1) * 512],
                 pgb[:, 0:512])
    frQ = ctx.free(pgb)
    ctx.free(rbf2)

    xf = ctx.sbuf("xf", [128, ND * T], BF16)
    probs = ctx.sbuf("probs", [128, NT * NEXP], F32)
    pexp = ctx.sbuf("pexp", [128, NEXP], F32)
    w1b = [ctx.sbuf(f"w1b{i}", [128, ND * MFB], BF16) for i in range(2)]
    w2b = [ctx.sbuf(f"w2b{i}", [128, (MFB // 128) * D], BF16) for i in range(2)]
    w1r = [ctx.sbuf(f"w1r{i}", [128, MFB], F32) for i in range(2)]
    w2r = [ctx.sbuf(f"w2r{i}", [128, D], F32) for i in range(2)]
    hb2 = [ctx.sbuf(f"hb{i}", [128, (MFB // 128) * T], BF16) for i in range(2)]
    moe = ctx.sbuf("moe", [128, NT * D], F32)
    xztile = [ctx.sbuf(f"xz{i}", [128, D], F32) for i in range(2)]
    ores = [ctx.sbuf(f"ores{i}", [128, D], F32) for i in range(2)]

    ldx = ctx.dma_stream("ldx")
    ldw1 = [ctx.dma_stream(f"ldw1_{i}") for i in range(2)]
    ldw2 = [ctx.dma_stream(f"ldw2_{i}") for i in range(2)]
    ldb1 = ctx.dma_stream("ldb1")
    ldb1b = ctx.dma_stream("ldb1b")
    ldz = [ctx.dma_stream(f"ldz{i}") for i in range(2)]
    stw = ctx.dma_stream("stw")

    for e_ in range(NEXP):
        for fc_ in range(32):
            ctx.dma_op("sync", ldb1 if fc_ % 2 == 0 else ldb1b,
                       [(b1c.h[:, e_ * 32 + fc_:e_ * 32 + fc_ + 1],
                         b1_h.h[e_, fc_ * 128:(fc_ + 1) * 128])],
                       [b1c], [b1_h])
    ph = [ctx.psum(f"ph{i}", [128, 512], F32, after=frQ) for i in range(2)]
    py = [ctx.psum(f"py{i}", [128, D], F32) for i in range(2)]
    pp = ctx.psum("pp", [128, NEXP], F32)

    for b in range(B_CORE):
        ctx.dma_op("sync", ldx,
                   [(xf.h[:, c * T:(c + 1) * T], xl1T_i.h[b, c * 128:(c + 1) * 128, :])
                    for c in range(ND)], [xf], [xl1T_i])
        for it in range(NT):
            t.op("matmul", [pp], [onesb, gbrow], pp[:, 0:NEXP],
                 onesb.h[0:1, 0:128], gbrow.h[0:1, 0:NEXP],
                 start=True, stop=False, inc=False)
            for c in range(ND):
                last = (c == ND - 1)
                t.op("matmul", [pp] if last else [], [xf, gwb], pp[:, 0:NEXP],
                     xf.h[:, c * T + it * 128:c * T + (it + 1) * 128],
                     gwb.h[:, c * NEXP:(c + 1) * NEXP],
                     start=False, stop=last, inc=last)
            sl = statsl[it % 2]
            nmx, sm = sl[0], sl[1]
            v.op("tensor_reduce", [nmx], [pp], nmx.h, pp[:, 0:NEXP], axis=AX.X,
                 op=ALU.max, negate=True)
            s.op("activation", [pexp, sm], [pp, nmx], pexp[:], pp[:, 0:NEXP],
                 AF.Exp, bias=nmx.h, accum_out=sm.h)
            v.op("reciprocal", [sm], [sm], sm.h, sm.h)
            v.op("tensor_scalar", [probs], [pexp, sm],
                 probs.h[:, it * NEXP:(it + 1) * NEXP], pexp[:], sm.h, None,
                 op0=ALU.mult, op1=ALU.bypass)
        for e in range(NEXP):
            ctx.dma_op("sync", ldb1, [(rowf.h[0:1, 0:D], b2_h.h[e, 0:D])],
                       [rowf], [b2_h])
            v.op("tensor_copy", [b2row], [rowf], b2row.h[0:1, 0:D],
                 rowf.h[0:1, 0:D])
            for fb in range(NMFB):
                i = fb % 2
                f0 = fb * MFB
                for c in range(ND):
                    j = c % 2
                    ctx.dma_op("sync", ldw1[j],
                               [(w1r[j][:], w1_h.h[e * D + c * 128:e * D + (c + 1) * 128,
                                                   f0:f0 + MFB])],
                               [w1r[j]], [w1_h])
                    g.op("tensor_copy", [w1b[i]], [w1r[j]],
                         w1b[i].h[:, c * MFB:(c + 1) * MFB], w1r[j][:])
                for fc in range(MFB // 128):
                    j = fc % 2
                    ctx.dma_op("scalar", ldw2[j],
                               [(w2r[j][:], w2_h.h[e * DFF + f0 + fc * 128:
                                                   e * DFF + f0 + (fc + 1) * 128, :])],
                               [w2r[j]], [w2_h])
                    g.op("tensor_copy", [w2b[i]], [w2r[j]],
                         w2b[i].h[:, fc * D:(fc + 1) * D], w2r[j][:])
                hb = hb2[i]
                for fc in range(MFB // 128):
                    for ts in range(4):
                        p = ph[(fc * 4 + ts) % 2]
                        for c in range(ND):
                            last = (c == ND - 1)
                            t.op("matmul", [p] if (last or c == 0) else [],
                                 [w1b[i], xf], p[:, 0:512],
                                 w1b[i].h[:, c * MFB + fc * 128:c * MFB + (fc + 1) * 128],
                                 xf.h[:, c * T + ts * 512:c * T + (ts + 1) * 512],
                                 start=(c == 0), stop=last, inc=last)
                        s.op("activation", [hb], [p, b1c],
                             hb.h[:, fc * T + ts * 512:fc * T + (ts + 1) * 512],
                             p[:, 0:512], AF.Relu,
                             bias=b1c.h[:, e * 32 + f0 // 128 + fc:e * 32 + f0 // 128 + fc + 1])
                for it in range(NT):
                    p = py[it % 2]
                    for dh in range(2):
                        terms = []
                        if fb == 0:
                            terms.append(("bias", None))
                        for fc in range(MFB // 128):
                            terms.append(("mm", fc))
                        for j, (kind, fc) in enumerate(terms):
                            fst = (j == 0)
                            last = (j == len(terms) - 1)
                            outs = [p] if (fst or last) else []
                            if kind == "bias":
                                t.op("matmul", outs, [onesb, b2row],
                                     p[:, dh * 512:(dh + 1) * 512],
                                     onesb.h[0:1, 0:128],
                                     b2row.h[0:1, dh * 512:(dh + 1) * 512],
                                     start=True, stop=False, inc=False)
                            else:
                                t.op("matmul", outs, [hb, w2b[i]],
                                     p[:, dh * 512:(dh + 1) * 512],
                                     hb.h[:, fc * T + it * 128:fc * T + (it + 1) * 128],
                                     w2b[i].h[:, fc * D + dh * 512:fc * D + (dh + 1) * 512],
                                     start=fst, stop=last, inc=last)
                    if e == 0 and fb == 0:
                        v.op("tensor_scalar", [moe], [py[it % 2], probs],
                             moe.h[:, it * D:(it + 1) * D], p[:, 0:D],
                             probs.h[:, it * NEXP + e:it * NEXP + e + 1], None,
                             op0=ALU.mult, op1=ALU.bypass)
                    else:
                        v.op("scalar_tensor_tensor", [moe], [py[it % 2], probs],
                             moe.h[:, it * D:(it + 1) * D], p[:, 0:D],
                             probs.h[:, it * NEXP + e:it * NEXP + e + 1],
                             moe.h[:, it * D:(it + 1) * D],
                             op0=ALU.mult, op1=ALU.add)
        for it in range(NT):
            i = it % 2
            ctx.dma_op("sync", ldz[i],
                       [(xztile[i][:], xl1_i.h[b * T + it * 128:b * T + (it + 1) * 128, :])],
                       [xztile[i]], [xl1_i])
            v.op("tensor_tensor", [xztile[i]], [moe], xztile[i][:], xztile[i][:],
                 moe.h[:, it * D:(it + 1) * D], op=ALU.add)
            sl = statsl[i]
            nmu, rstd = emit_ln_stats(ctx, xztile[i], xztile[i][:], sl, eps_c,
                                      ores[i])
            v.op("tensor_scalar", [ores[i]], [xztile[i], nmu, rstd], ores[i][:],
                 xztile[i][:], nmu.h, rstd.h, op0=ALU.add, op1=ALU.mult)
            v.op("scalar_tensor_tensor", [ores[i]], [gbc], ores[i][:],
                 gbc[:], 1.0, ores[i][:], op0=ALU.mult, op1=ALU.mult)
            v.op("tensor_tensor", [ores[i]], [ores[i], bbc], ores[i][:],
                 ores[i][:], bbc[:], op=ALU.add)
            ctx.dma_op("gpsimd", stw,
                       [(out_o.h[b * T + it * 128:b * T + (it + 1) * 128, :],
                         ores[i][:])], [out_o], [ores[i]])
    g.raw(lambda eng, s_=stw.sem, v_=stw.count: eng.wait_ge(s_, v_))


def build_moe_launch():
    ctx = Ctx()
    ext = {}
    ext["xl1T"] = ctx.dram("xl1T", [B_CORE, D, T], BF16, kind="ExternalInput")
    ext["xl1"] = ctx.dram("xl1", [B_CORE * T, D], F32, kind="ExternalInput")
    ext["gate_w"] = ctx.dram("gate_w", [D, NEXP], F32, kind="ExternalInput")
    ext["gate_b"] = ctx.dram("gate_b", [NEXP], F32, kind="ExternalInput")
    ext["w1"] = ctx.dram("w1", [NEXP * D, DFF], F32, kind="ExternalInput")
    ext["b1"] = ctx.dram("b1", [NEXP, DFF], F32, kind="ExternalInput")
    ext["w2"] = ctx.dram("w2", [NEXP * DFF, D], F32, kind="ExternalInput")
    ext["b2"] = ctx.dram("b2", [NEXP, D], F32, kind="ExternalInput")
    ext["ln2_g"] = ctx.dram("ln2_g", [D], F32, kind="ExternalInput")
    ext["ln2_b"] = ctx.dram("ln2_b", [D], F32, kind="ExternalInput")
    ext["f0out"] = ctx.dram("f0out", [B_CORE * T, D], F32, kind="ExternalOutput")
    emit_moe(ctx, ext)
    return ctx.emit()


def emit_sum_ln(ctx, ext):
    x0 = ext["x0"]
    a0 = ext["a0"]
    f0 = ext["f0"]
    x1_o = ext["x1"]
    xn1_o = ext["xn1"]
    xn1T_o = ext["xn1T"]
    t, v, s, g = ctx.t, ctx.v, ctx.s, ctx.g
    eps_c = ctx.const(1e-5)
    ident = ctx.identity(BF16)
    xa = [ctx.sbuf(f"xa{i}", [128, D], F32) for i in range(2)]
    xb_ = [ctx.sbuf(f"xb{i}", [128, D], F32) for i in range(2)]
    xc_ = [ctx.sbuf(f"xc{i}", [128, D], F32) for i in range(2)]
    xh = [ctx.sbuf(f"xh{i}", [128, D], F32) for i in range(2)]
    xhb = [ctx.sbuf(f"xhb{i}", [128, D], BF16) for i in range(2)]
    scratch = [ctx.sbuf(f"scr{i}", [128, D], F32) for i in range(2)]
    stats = ctx.sbuf("stats", [128, 8], F32)
    statsl = make_stat_slices(ctx, stats, 2)
    fmacc = ctx.sbuf("fmacc", [128, ND * T], BF16)
    pst = [ctx.psum(f"pst{i}", [128, 128], BF16) for i in range(2)]
    lda = [ctx.dma_stream(f"lda{i}") for i in range(2)]
    ldb = [ctx.dma_stream(f"ldb{i}") for i in range(2)]
    ldc = [ctx.dma_stream(f"ldc{i}") for i in range(2)]
    so1 = ctx.dma_stream("so1")
    so2 = ctx.dma_stream("so2")
    so3 = ctx.dma_stream("so3")
    for b in range(B_CORE):
        for it in range(NT):
            i = it % 2
            r0 = b * T + it * 128
            ctx.dma_op("sync", lda[i], [(xa[i][:], x0.h[r0:r0 + 128, :])],
                       [xa[i]], [x0])
            ctx.dma_op("scalar", ldb[i], [(xb_[i][:], a0.h[r0:r0 + 128, :])],
                       [xb_[i]], [a0])
            ctx.dma_op("gpsimd", ldc[i], [(xc_[i][:], f0.h[r0:r0 + 128, :])],
                       [xc_[i]], [f0])
            v.op("tensor_tensor", [xa[i]], [xb_[i]], xa[i][:], xa[i][:],
                 xb_[i][:], op=ALU.add)
            v.op("tensor_tensor", [xa[i]], [xc_[i]], xa[i][:], xa[i][:],
                 xc_[i][:], op=ALU.add)
            ctx.dma_op("gpsimd", so1, [(x1_o.h[r0:r0 + 128, :], xa[i][:])],
                       [x1_o], [xa[i]])
            nmu, rstd = emit_ln_stats(ctx, xa[i], xa[i][:], statsl[i], eps_c,
                                      scratch[i])
            v.op("tensor_scalar", [xh[i]], [xa[i], nmu, rstd], xh[i][:],
                 xa[i][:], nmu.h, rstd.h, op0=ALU.add, op1=ALU.mult)
            ctx.dma_op("gpsimd", so2, [(xn1_o.h[r0:r0 + 128, :], xh[i][:])],
                       [xn1_o], [xh[i]])
            v.op("tensor_copy", [xhb[i]], [xh[i]], xhb[i][:], xh[i][:])
            for c in range(ND):
                p = pst[(it * ND + c) % 2]
                t.op("transpose", [p], [xhb[i], ident], p[:],
                     xhb[i].h[:, c * 128:(c + 1) * 128], ident[:])
                v.op("tensor_copy", [fmacc], [p],
                     fmacc.h[:, c * T + it * 128:c * T + (it + 1) * 128], p[:])
        ctx.dma_op("gpsimd", so3,
                   [(xn1T_o.h[b, c * 128:(c + 1) * 128, :],
                     fmacc.h[:, c * T:(c + 1) * T]) for c in range(ND)],
                   [xn1T_o], [fmacc])
    for st_ in (so1, so2, so3):
        g.raw(lambda eng, s_=st_.sem, v_=st_.count: eng.wait_ge(s_, v_))


def build_sum_ln_launch():
    ctx = Ctx()
    ext = {}
    ext["x0"] = ctx.dram("x0", [B_CORE * T, D], F32, kind="ExternalInput")
    ext["a0"] = ctx.dram("a0", [B_CORE * T, D], F32, kind="ExternalInput")
    ext["f0"] = ctx.dram("f0", [B_CORE * T, D], F32, kind="ExternalInput")
    ext["x1"] = ctx.dram("x1", [B_CORE * T, D], F32, kind="ExternalOutput")
    ext["xn1"] = ctx.dram("xn1", [B_CORE * T, D], F32, kind="ExternalOutput")
    ext["xn1T"] = ctx.dram("xn1T", [B_CORE, D, T], BF16, kind="ExternalOutput")
    emit_sum_ln(ctx, ext)
    return ctx.emit()

def emit_final_sum(ctx, ext):
    x1 = ext["x1"]
    a1 = ext["a1"]
    f1 = ext["f1"]
    out = ext["out"]
    v, g = ctx.v, ctx.g
    xa = [ctx.sbuf(f"xa{i}", [128, D], F32) for i in range(2)]
    xb_ = [ctx.sbuf(f"xb{i}", [128, D], F32) for i in range(2)]
    lda = [ctx.dma_stream(f"lda{i}") for i in range(2)]
    ldb = [ctx.dma_stream(f"ldb{i}") for i in range(2)]
    so = ctx.dma_stream("so")
    for r in range(B_CORE * NT):
        i = r % 2
        r0 = r * 128
        ctx.dma_op("sync", lda[i], [(xa[i][:], x1.h[r0:r0 + 128, :])], [xa[i]], [x1])
        ctx.dma_op("sync", ldb[i], [(xb_[i][:], a1.h[r0:r0 + 128, :])], [xb_[i]], [a1])
        v.op("tensor_tensor", [xa[i]], [xb_[i]], xa[i][:], xa[i][:], xb_[i][:],
             op=ALU.add)
        ctx.dma_op("sync", ldb[i], [(xb_[i][:], f1.h[r0:r0 + 128, :])], [xb_[i]], [f1])
        v.op("tensor_tensor", [xa[i]], [xb_[i]], xa[i][:], xa[i][:], xb_[i][:],
             op=ALU.add)
        ctx.dma_op("gpsimd", so, [(out.h[r0:r0 + 128, :], xa[i][:])], [out], [xa[i]])
    g.raw(lambda eng, s_=so.sem, v_=so.count: eng.wait_ge(s_, v_))


def build_final_sum_launch():
    ctx = Ctx()
    ext = {}
    ext["x1"] = ctx.dram("x1", [B_CORE * T, D], F32, kind="ExternalInput")
    ext["a1"] = ctx.dram("a1", [B_CORE * T, D], F32, kind="ExternalInput")
    ext["f1"] = ctx.dram("f1", [B_CORE * T, D], F32, kind="ExternalInput")
    ext["out"] = ctx.dram("out", [B_CORE * T, D], F32, kind="ExternalOutput")
    emit_final_sum(ctx, ext)
    return ctx.emit()


ECH = 64
NECH = T // ECH
MLPD = 4096
EFB = 512
NEFB = MLPD // EFB

TWO_PI = 2.0 * np.pi


def emit_a1(ctx, ext):
    xn1_i = ext["xn1"]
    win_h = ext["w_in"]
    wout_h = ext["w_out"]
    wg_h = ext["w_gate"]
    bg_h = ext["b_gate"]
    ld_h = ext["log_decay"]
    fq_h = ext["frequency"]
    cp_h = ext["coupling"]
    wm1_h = ext["w_mlp1"]
    bm1_h = ext["b_mlp1"]
    wm2_h = ext["w_mlp2"]
    bm2_h = ext["b_mlp2"]
    l1g_h = ext["ln1_g"]
    l1b_h = ext["ln1_b"]
    l2g_h = ext["ln2_g"]
    l2b_h = ext["ln2_b"]
    out_o = ext["a1out"]
    padscr = ctx.dram("padscr", [3 * 64, 127], BF16)
    cpscr = ctx.dram("cpscr", [16, 64], BF16)
    scr1 = ctx.dram("scr1", [D, MLPD], BF16)
    scr2 = ctx.dram("scr2", [MLPD, D], BF16)

    t, v, s, g = ctx.t, ctx.v, ctx.s, ctx.g
    eps_c = ctx.const(1e-5)
    npi_c = ctx.const(-np.pi)
    ident = ctx.identity(BF16)

    winT = ctx.sbuf("winT", [128, ND * 128], BF16)
    woutT = ctx.sbuf("woutT", [128, D], BF16)
    wgT = ctx.sbuf("wgT", [128, ND * D], BF16)
    Lvr = ctx.sbuf("Lvr", [128, 64 * 64], BF16)
    Lvi = ctx.sbuf("Lvi", [128, 64 * 64], BF16)
    Lvn = ctx.sbuf("Lvn", [128, 64 * 64], BF16)
    fcR = ctx.sbuf("fcR", [64, 64], BF16)
    fcI = ctx.sbuf("fcI", [64, 64], BF16)
    fcN = ctx.sbuf("fcN", [64, 64], BF16)
    BD = ctx.sbuf("BD", [64, 64], BF16)
    colv = ctx.sbuf("colv", [128, 5 * ND + 32], F32)
    bm2row = ctx.sbuf("bm2row", [1, D], BF16)
    onesb = ctx.sbuf("onesb", [1, 512], BF16)
    rowf = ctx.sbuf("rowf", [1, 1024], F32)
    stats = ctx.sbuf("stats", [128, 8], F32)
    statsl = make_stat_slices(ctx, stats, 2)
    g.op("memset", [onesb], [], onesb[:], 1.0)

    lw = [ctx.dma_stream(f"lw{i}") for i in range(2)]
    lb = ctx.dma_stream("lb")
    tmp32 = [ctx.sbuf(f"tmp32_{i}", [128, MLPD], F32, side="right") for i in range(2)]
    tmpbf = [ctx.sbuf(f"tmpbf_{i}", [128, MLPD], BF16, side="right") for i in range(2)]
    pst0 = ctx.psum("pst0", [128, 128], BF16)
    ctx.dma_op("sync", lw[0], [(tmp32[0].h[:, 0:D], win_h.h[0:128, :])],
               [tmp32[0]], [win_h])
    v.op("tensor_copy", [tmpbf[0]], [tmp32[0]], tmpbf[0].h[:, 0:D],
         tmp32[0].h[:, 0:D])
    for c in range(ND):
        t.op("transpose", [pst0], [tmpbf[0], ident], pst0[:, 0:128],
             tmpbf[0].h[:, c * 128:(c + 1) * 128], ident[:])
        v.op("tensor_copy", [winT], [pst0], winT.h[:, c * 128:(c + 1) * 128],
             pst0[:, 0:128])
    for ro in range(ND):
        i = ro % 2
        ctx.dma_op("sync", lw[i], [(tmp32[i].h[:, 0:128],
                                    wout_h.h[ro * 128:(ro + 1) * 128, :])],
                   [tmp32[i]], [wout_h])
        v.op("tensor_copy", [tmpbf[i]], [tmp32[i]], tmpbf[i].h[:, 0:128],
             tmp32[i].h[:, 0:128])
        t.op("transpose", [pst0], [tmpbf[i], ident], pst0[:, 0:128],
             tmpbf[i].h[:, 0:128], ident[:])
        v.op("tensor_copy", [woutT], [pst0], woutT.h[:, ro * 128:(ro + 1) * 128],
             pst0[:, 0:128])
    emit_transpose_weight(ctx, wg_h, wgT, D, D, tmp32, tmpbf, pst0, lw)
    stg2 = [ctx.sbuf(f"stg{i}", [128, 1024], BF16, side="right") for i in range(2)]
    pstg = [pst0, ctx.psum("pst1", [128, 128], BF16)]
    emit_cast_store_T(ctx, wm1_h, scr1, MLPD, D, tmp32, tmpbf, pstg, stg2,
                      lw, [ctx.dma_stream("sc1a"), ctx.dma_stream("sc1b")])
    emit_cast_store_T(ctx, wm2_h, scr2, D, MLPD, tmp32, tmpbf, pstg, stg2,
                      lw, [ctx.dma_stream("sc2a"), ctx.dma_stream("sc2b")])
    for (cidx, hsrc) in ((0, bg_h), (8, l1g_h), (16, l1b_h), (24, l2g_h),
                         (32, l2b_h)):
        for c in range(ND):
            ctx.dma_op("sync", lb, [(colv.h[:, cidx + c:cidx + c + 1],
                                     hsrc.h[c * 128:(c + 1) * 128])],
                       [colv], [hsrc])
    for c in range(32):
        ctx.dma_op("sync", lb, [(colv.h[:, 40 + c:41 + c],
                                 bm1_h.h[c * 128:(c + 1) * 128])], [colv], [bm1_h])
    ctx.dma_op("sync", lb, [(rowf.h[0:1, 0:D], bm2_h.h[0:D])], [rowf], [bm2_h])
    v.op("tensor_copy", [bm2row], [rowf], bm2row.h[0:1, 0:D], rowf.h[0:1, 0:D])

    ldec = ctx.sbuf("ldec", [64, 2], F32, side="right")
    ctx.dma_op("sync", lb, [(ldec.h[0:64, 0:1], ld_h.h[0:64])], [ldec], [ld_h])
    ctx.dma_op("sync", lb, [(ldec.h[0:64, 1:2], fq_h.h[0:64])], [ldec], [fq_h])
    ew = [ctx.sbuf(f"ew{i}", [64, 64], F32, side="right") for i in range(6)]
    ewi = ctx.sbuf("ewi", [64, 64], I32, side="right")
    vbf = [ctx.sbuf(f"vbf{i}", [64, 64], BF16, side="right") for i in range(2)]
    padt = ctx.sbuf("padt", [64, 127], BF16, side="right")
    s.op("activation", [ew[5]], [ldec], ew[5].h[0:64, 0:1], ldec.h[0:64, 0:1],
         AF.Sigmoid)
    s.op("activation", [ew[5]], [ew[5]], ew[5].h[0:64, 1:2], ew[5].h[0:64, 0:1],
         AF.Ln)
    g.op("iota", [ew[0]], [], ew[0].h[0:64, 0:64], pattern=[[1, 64]], base=0,
         channel_multiplier=0, allow_small_or_imprecise_dtypes=True)
    v.op("tensor_scalar", [ew[1]], [ew[0], ew[5]], ew[1].h[0:64, :],
         ew[0].h[0:64, :], ew[5].h[0:64, 1:2], None, op0=ALU.mult, op1=ALU.bypass)
    s.op("activation", [ew[1]], [ew[1]], ew[1].h[0:64, :], ew[1].h[0:64, :], AF.Exp)
    v.op("tensor_scalar", [ew[2]], [ew[0], ldec], ew[2].h[0:64, :],
         ew[0].h[0:64, :], ldec.h[0:64, 1:2], None, op0=ALU.mult, op1=ALU.bypass)
    v.op("tensor_scalar", [ew[3]], [ew[2]], ew[3].h[0:64, :], ew[2].h[0:64, :],
         float(np.pi / 2), None, op0=ALU.add, op1=ALU.bypass)
    def sinred(dst, srcb):
        v.op("tensor_scalar", [dst], [srcb], dst.h[0:64, :], srcb.h[0:64, :],
             float(1.0 / TWO_PI), None, op0=ALU.mult, op1=ALU.bypass)
        v.op("tensor_copy", [ewi], [dst], ewi.h[0:64, :], dst.h[0:64, :])
        v.op("tensor_copy", [ew[4]], [ewi], ew[4].h[0:64, :], ewi.h[0:64, :])
        v.op("tensor_tensor", [dst], [dst, ew[4]], dst.h[0:64, :], dst.h[0:64, :],
             ew[4].h[0:64, :], op=ALU.subtract)
        v.op("tensor_single_scalar", [ew[4]], [dst], ew[4].h[0:64, :],
             dst.h[0:64, :], 0.5, op=ALU.is_ge)
        v.op("tensor_tensor", [dst], [dst, ew[4]], dst.h[0:64, :], dst.h[0:64, :],
             ew[4].h[0:64, :], op=ALU.subtract)
        s.op("activation", [dst], [dst], dst.h[0:64, :], dst.h[0:64, :],
             AF.Sin, scale=float(TWO_PI))
    sinred(ew[2], ew[2])
    sinred(ew[3], ew[3])
    v.op("tensor_tensor", [fcR], [ew[1], ew[3]], fcR.h[0:64, :],
         ew[1].h[0:64, :], ew[3].h[0:64, :], op=ALU.mult)
    v.op("tensor_tensor", [fcI], [ew[1], ew[2]], fcI.h[0:64, :],
         ew[1].h[0:64, :], ew[2].h[0:64, :], op=ALU.mult)
    v.op("tensor_scalar", [fcN], [fcI], fcN.h[0:64, :], fcI.h[0:64, :],
         -1.0, None, op0=ALU.mult, op1=ALU.bypass)
    stp = ctx.dma_stream("stp")
    for (vi, src) in ((0, fcR), (1, fcI), (2, fcN)):
        g.op("memset", [padt], [], padt.h[0:64, 0:63], 0.0)
        v.op("tensor_copy", [padt], [src], padt.h[0:64, 63:127],
             src.h[0:64, 0:64])
        ctx.dma_op("gpsimd", stp, [(padscr.h[vi * 64:(vi + 1) * 64, :],
                                    padt.h[0:64, :])], [padscr], [padt])
    ldt3 = [ctx.dma_stream(f"ldt{i}") for i in range(3)]
    for (vi, dst) in ((0, Lvr), (1, Lvi), (2, Lvn)):
        for k in range(64):
            ap_in = mkap(padscr, (vi * 64 + k) * 127 + 63,
                            [[0, 2], [-1, 64], [1, 64]])
            ctx.dma_op(("sync", "scalar", "gpsimd")[vi], ldt3[vi],
                       [(dst.h[:, k * 64:(k + 1) * 64], ap_in)],
                       [dst], [padscr])
    for dst in (Lvr, Lvi, Lvn):
        for k in range(64):
            t.op("transpose", [pst0], [dst, ident], pst0[0:64, 0:64],
                 dst.h[0:64, k * 64:(k + 1) * 64], ident.h[0:64, 0:64])
            v.op("tensor_copy", [dst], [pst0], dst.h[0:64, k * 64:(k + 1) * 64],
                 pst0[0:64, 0:64])
            v.op("tensor_copy", [dst], [pst0], dst.h[64:128, k * 64:(k + 1) * 64],
                 pst0[0:64, 0:64])
    cpt = ctx.sbuf("cpt", [64, 16], F32, side="right")
    cptb = ctx.sbuf("cptb", [64, 16], BF16, side="right")
    cpT = ctx.sbuf("cpT", [16, 64], BF16, side="right")
    ctx.dma_op("sync", lb, [(cpt.h[0:64, 0:16], cp_h.h[:, :])], [cpt], [cp_h])
    v.op("tensor_copy", [cptb], [cpt], cptb.h[0:64, :], cpt.h[0:64, :])
    t.op("transpose", [pst0], [cptb, ident], pst0[0:16, 0:64],
         cptb.h[0:64, 0:16], ident.h[0:64, 0:64])
    v.op("tensor_copy", [cpT], [pst0], cpT.h[0:16, 0:64], pst0[0:16, 0:64])
    ctx.dma_op("gpsimd", stp, [(cpscr.h[:, :], cpT.h[0:16, :])], [cpscr], [cpT])
    g.op("memset", [BD], [], BD.h[0:64, :], 0.0)
    for h in range(4):
        ctx.dma_op("sync", ldt3[0], [(mkap(BD, (h * 16) * 64 + h * 16,
                                           [[64, 16], [1, 16]]),
                                   cpscr.h[0:16, h * 16:(h + 1) * 16])],
                   [BD], [cpscr])
    ctx.free(cpT, cptb, cpt, padt, vbf[1], vbf[0], ewi, *reversed(ew), ldec)
    ctx.free(stg2[1], stg2[0], tmpbf[1], tmpbf[0], tmp32[1], tmp32[0])
    frP = ctx.free(pstg[1], pst0)

    ldx = [ctx.dma_stream(f"ldx{i}") for i in range(2)]
    ldw1 = [ctx.dma_stream(f"ldw1_{i}") for i in range(2)]
    stw = ctx.dma_stream("stw")

    for b in range(B_CORE):
        x1 = ctx.sbuf(f"x1_{b}", [128, NT * D], F32)
        xnorm = ctx.sbuf(f"xnorm{b}", [128, ND * T], BF16, side="right")
        bT = ctx.sbuf(f"bT{b}", [128, NT * 128], BF16, side="right")
        c2 = ctx.sbuf(f"c2_{b}", [128, T], BF16, side="right")
        intraS = ctx.sbuf(f"intraS{b}", [64, 2 * T], BF16, side="right")
        cTr = ctx.sbuf(f"cTr{b}", [64, T], F32, side="right")
        cTi = ctx.sbuf(f"cTi{b}", [64, T], F32, side="right")
        cbf = ctx.sbuf(f"cbf{b}", [64, 2 * T], BF16, side="right")
        xz = [ctx.sbuf(f"xz{b}_{i}", [128, D], F32, side="right") for i in range(2)]
        xhb = [ctx.sbuf(f"xhb{b}_{i}", [128, D], BF16, side="right") for i in range(2)]
        gatet = [ctx.sbuf(f"gt{b}_{i}", [128, 512], BF16, side="right") for i in range(2)]
        ghb = [ctx.sbuf(f"ghb{b}_{i}", [128, 512], BF16, side="right") for i in range(2)]
        pTl = ctx.psum(f"pTl{b}", [128, 128], BF16, after=frP)
        for it in range(NT):
            i = it % 2
            ctx.dma_op("sync", ldx[i], [(xz[i][:], xn1_i.h[b * T + it * 128:
                                                           b * T + (it + 1) * 128, :])],
                       [xz[i]], [xn1_i])
            nmu, rstd = emit_ln_stats(ctx, xz[i], xz[i][:], statsl[i], eps_c, x1)
            v.op("tensor_scalar", [xhb[i]], [xz[i], nmu, rstd], xhb[i][:],
                 xz[i][:], nmu.h, rstd.h, op0=ALU.add, op1=ALU.mult)
            for c in range(ND):
                t.op("transpose", [pTl], [xhb[i], ident], pTl[:, 0:128],
                     xhb[i].h[:, c * 128:(c + 1) * 128], ident[:])
                v.op("tensor_scalar", [xnorm], [pTl, colv],
                     xnorm.h[:, c * T + it * 128:c * T + (it + 1) * 128],
                     pTl[:, 0:128], colv.h[:, 8 + c:9 + c], colv.h[:, 16 + c:17 + c],
                     op0=ALU.mult, op1=ALU.add)
        pb = ctx.psum(f"pb{b}", [128, 128], F32)
        for it in range(NT):
            grp = [(xnorm.h[:, c * T + it * 128:c * T + (it + 1) * 128],
                    winT.h[:, c * 128:(c + 1) * 128], [xnorm, winT])
                   for c in range(ND)]
            mm_group(ctx, pb, pb[:, 0:128], grp)
            v.op("tensor_copy", [bT], [pb], bT.h[:, it * 128:(it + 1) * 128],
                 pb[:, 0:128])
        pir = ctx.psum(f"pir{b}", [64, 512], F32)
        pii = ctx.psum(f"pii{b}", [64, 512], F32)
        for gi in range(4):
            for ki in range(16):
                k = gi * 16 + ki
                for p in range(2):
                    co = ki * 32 + p * 16
                    rb_r = mkap(bT, p * 64 * (NT * 128) + k,
                                   [[NT * 128, 64], [128, 16]])
                    rb_i = mkap(bT, p * 64 * (NT * 128) + 64 + k,
                                   [[NT * 128, 64], [128, 16]])
                    pr = p * 64
                    t.op("matmul", [pir], [Lvr, bT], pir[0:64, co:co + 16],
                         Lvr.h[pr:pr + 64, k * 64:(k + 1) * 64], rb_r,
                         start=True, stop=False, inc=False,
                         tile_position=(pr, 0))
                    t.op("matmul", [pir], [Lvn, bT], pir[0:64, co:co + 16],
                         Lvn.h[pr:pr + 64, k * 64:(k + 1) * 64], rb_i,
                         start=False, stop=True, tile_position=(pr, 0))
                    t.op("matmul", [pii], [Lvr, bT], pii[0:64, co:co + 16],
                         Lvr.h[pr:pr + 64, k * 64:(k + 1) * 64], rb_i,
                         start=True, stop=False, inc=False,
                         tile_position=(pr, 0))
                    t.op("matmul", [pii], [Lvi, bT], pii[0:64, co:co + 16],
                         Lvi.h[pr:pr + 64, k * 64:(k + 1) * 64], rb_r,
                         start=False, stop=True, tile_position=(pr, 0))
            v.op("tensor_copy", [intraS], [pir],
                 intraS.h[0:64, gi * 512:(gi + 1) * 512], pir[0:64, 0:512])
            v.op("tensor_copy", [intraS], [pii],
                 intraS.h[0:64, T + gi * 512:T + (gi + 1) * 512], pii[0:64, 0:512])
        pTr = ctx.psum(f"pTr{b}", [64, 64], BF16)
        pTi = ctx.psum(f"pTi{b}", [64, 64], BF16)
        FS = 2 * T
        for n in range(NECH):
            p2, n2 = n % 2, n // 2
            for (var, pT_) in ((0, pTr), (1, pTi)):
                src = mkap(intraS, var * T + p2 * 16 + n2,
                              [[FS, 64], [512, 4], [32, 16]])
                t.op("transpose", [pT_], [intraS, ident], pT_[0:64, 0:64],
                     src, ident.h[0:64, 0:64])
            co = n * 64
            v.op("tensor_copy", [cTr], [pTr], cTr.h[0:64, co:co + 64],
                 pTr[0:64, 0:64])
            v.op("tensor_copy", [cTi], [pTi], cTi.h[0:64, co:co + 64],
                 pTi[0:64, 0:64])
            if n > 0:
                v.op("tensor_tensor", [cTr], [cTr], cTr.h[0:64, co:co + 1],
                     cTr.h[0:64, co:co + 1], cTr.h[0:64, co - 1:co], op=ALU.add)
                v.op("tensor_tensor", [cTi], [cTi], cTi.h[0:64, co:co + 1],
                     cTi.h[0:64, co:co + 1], cTi.h[0:64, co - 1:co], op=ALU.add)
        v.op("tensor_copy", [cbf], [cTr], cbf.h[0:64, 0:T], cTr.h[0:64, 0:T])
        v.op("tensor_copy", [cbf], [cTi], cbf.h[0:64, T:2 * T], cTi.h[0:64, 0:T])
        for ts in range(4):
            for (vo, vi) in ((0, 0), (64, T)):
                t.op("matmul", [pir], [BD, cbf], pir[0:64, 0:512],
                     BD.h[0:64, 0:64], cbf.h[0:64, vi + ts * 512:vi + (ts + 1) * 512],
                     start=True, stop=True)
                v.op("tensor_copy", [c2], [pir], c2.h[vo:vo + 64, ts * 512:(ts + 1) * 512],
                     pir[0:64, 0:512])
        frE0 = ctx.free(pTi, pTr, pii, pir, pb, pTl)
        ctx.dma_op("sync", ldx[0],
                   [(x1.h[:, it * D:(it + 1) * D],
                     xn1_i.h[b * T + it * 128:b * T + (it + 1) * 128, :])
                    for it in range(NT)], [x1], [xn1_i])
        pg = [ctx.psum(f"pg{b}_{i}", [128, 512], F32, after=frE0) for i in range(2)]
        pTg = ctx.psum(f"pTg{b}", [128, 128], BF16)
        for dc in range(ND):
            for ts in range(4):
                i = ts % 2
                grp = [(wgT.h[:, c * D + dc * 128:c * D + dc * 128 + 128],
                        xnorm.h[:, c * T + ts * 512:c * T + (ts + 1) * 512],
                        [wgT, xnorm]) for c in range(ND)]
                mm_group(ctx, pg[i], pg[i][:, 0:512], grp)
                s.op("activation", [gatet[i]], [pg[i], colv], gatet[i][:],
                     pg[i][:, 0:512], AF.Sigmoid, bias=colv.h[:, dc:dc + 1])
                t.op("matmul", [pg[i]], [woutT, c2], pg[i][:, 0:512],
                     woutT.h[:, dc * 128:(dc + 1) * 128],
                     c2.h[:, ts * 512:(ts + 1) * 512],
                     start=True, stop=True)
                v.op("tensor_tensor", [ghb[i]], [pg[i], gatet[i]], ghb[i][:],
                     pg[i][:, 0:512], gatet[i][:], op=ALU.mult)
                for q in range(4):
                    it = ts * 4 + q
                    t.op("transpose", [pTg], [ghb[i], ident], pTg[:, 0:128],
                         ghb[i].h[:, q * 128:(q + 1) * 128], ident[:])
                    v.op("scalar_tensor_tensor", [x1], [pTg], 
                         x1.h[:, it * D + dc * 128:it * D + (dc + 1) * 128],
                         pTg[:, 0:128], 1.0,
                         x1.h[:, it * D + dc * 128:it * D + (dc + 1) * 128],
                         op0=ALU.mult, op1=ALU.add)
        frE = ctx.free(pTg, pg[1], pg[0])
        frS = ctx.free(ghb[1], ghb[0], gatet[1], gatet[0], xhb[1], xhb[0],
                       xz[1], xz[0], cbf, cTi, cTr, intraS, c2, bT, xnorm)
        x2n = ctx.sbuf(f"x2n{b}", [128, ND * T], BF16, side="right")
        w1T = [ctx.sbuf(f"w1T{b}_{i}", [128, ND * 512], BF16, side="right") for i in range(2)]
        w2T = [ctx.sbuf(f"w2T{b}_{i}", [128, 4 * D], BF16, side="right") for i in range(2)]
        h1 = ctx.sbuf(f"h1_{b}", [128, 4 * T], BF16, side="right")
        xz2 = [ctx.sbuf(f"xz2{b}_0", [128, D], F32, side="right")] * 2
        xh2 = [ctx.sbuf(f"xh2{b}_{i}", [128, D], BF16, side="right") for i in range(2)]
        pT2 = ctx.psum(f"pT2{b}", [128, 128], BF16, after=frE)
        for it in range(NT):
            i = it % 2
            nmu, rstd = emit_ln_stats(ctx, Buf(x1.h[:, it * D:(it + 1) * D], "x1s"),
                                      x1.h[:, it * D:(it + 1) * D], statsl[i],
                                      eps_c, xz2[i])
            v.op("tensor_scalar", [xh2[i]], [x1, nmu, rstd], xh2[i][:],
                 x1.h[:, it * D:(it + 1) * D], nmu.h, rstd.h,
                 op0=ALU.add, op1=ALU.mult)
            for c in range(ND):
                t.op("transpose", [pT2], [xh2[i], ident], pT2[:, 0:128],
                     xh2[i].h[:, c * 128:(c + 1) * 128], ident[:])
                v.op("tensor_scalar", [x2n], [pT2, colv],
                     x2n.h[:, c * T + it * 128:c * T + (it + 1) * 128],
                     pT2[:, 0:128], colv.h[:, 24 + c:25 + c], colv.h[:, 32 + c:33 + c],
                     op0=ALU.mult, op1=ALU.add)
        ph2 = [ctx.psum(f"ph2{b}_{i}", [128, 512], F32) for i in range(2)]
        pd2 = [ctx.psum(f"pd2{b}_{i}", [128, D], F32) for i in range(2)]
        for fb in range(NEFB):
            i = fb % 2
            f0 = fb * EFB
            for c in range(ND):
                ctx.dma_op("sync" if c % 2 == 0 else "scalar", ldw1[i],
                           [(w1T[i].h[:, c * EFB:(c + 1) * EFB],
                             scr1.h[c * 128:(c + 1) * 128, f0:f0 + EFB])],
                           [w1T[i]], [scr1])
            for fc in range(4):
                ctx.dma_op("sync", ldw1[i],
                           [(w2T[i].h[:, fc * D:(fc + 1) * D],
                             scr2.h[f0 + fc * 128:f0 + (fc + 1) * 128, 0:D])],
                           [w2T[i]], [scr2])
            for fc in range(4):
                for ts in range(4):
                    p = ph2[(fc * 4 + ts) % 2]
                    grp = [(w1T[i].h[:, c * EFB + fc * 128:c * EFB + (fc + 1) * 128],
                            x2n.h[:, c * T + ts * 512:c * T + (ts + 1) * 512],
                            [w1T[i], x2n]) for c in range(ND)]
                    mm_group(ctx, p, p[:, 0:512], grp)
                    s.op("activation", [h1], [p, colv],
                         h1.h[:, fc * T + ts * 512:fc * T + (ts + 1) * 512],
                         p[:, 0:512], AF.Silu,
                         bias=colv.h[:, 40 + fb * 4 + fc:41 + fb * 4 + fc])
            for it in range(NT):
                p = pd2[it % 2]
                for dh in range(2):
                    terms = []
                    if fb == 0:
                        terms.append(("bias", None))
                    for fc in range(4):
                        terms.append(("mm", fc))
                    for j, (kind, fc) in enumerate(terms):
                        fst = (j == 0)
                        last = (j == len(terms) - 1)
                        outs = [p] if (fst or last) else []
                        if kind == "bias":
                            t.op("matmul", outs, [onesb, bm2row],
                                 p[:, dh * 512:(dh + 1) * 512],
                                 onesb.h[0:1, 0:128],
                                 bm2row.h[0:1, dh * 512:(dh + 1) * 512],
                                 start=True, stop=False, inc=False)
                        else:
                            t.op("matmul", outs, [h1, w2T[i]],
                                 p[:, dh * 512:(dh + 1) * 512],
                                 h1.h[:, fc * T + it * 128:fc * T + (it + 1) * 128],
                                 w2T[i].h[:, fc * D + dh * 512:fc * D + (dh + 1) * 512],
                                 start=fst, stop=last, inc=last)
                v.op("scalar_tensor_tensor", [x1], [pd2[it % 2]],
                     x1.h[:, it * D:(it + 1) * D], p[:, 0:D], 1.0,
                     x1.h[:, it * D:(it + 1) * D], op0=ALU.mult, op1=ALU.add)
        for it in range(NT):
            ctx.dma_op("gpsimd", stw,
                       [(out_o.h[b * T + it * 128:b * T + (it + 1) * 128, :],
                         x1.h[:, it * D:(it + 1) * D])], [out_o], [x1])
        ctx.free(pd2[1], pd2[0], ph2[1], ph2[0], pT2)
        ctx.free(xh2[1], xh2[0], xz2[0], h1, w2T[1], w2T[0], w1T[1], w1T[0], x2n)
        frP = ctx.free(x1)
    g.raw(lambda eng, s_=stw.sem, v_=stw.count: eng.wait_ge(s_, v_))


def build_a1_launch():
    ctx = Ctx()
    ext = {}
    ext["xn1"] = ctx.dram("xn1", [B_CORE * T, D], F32, kind="ExternalInput")
    ext["w_in"] = ctx.dram("w_in", [128, D], F32, kind="ExternalInput")
    ext["w_out"] = ctx.dram("w_out", [D, 128], F32, kind="ExternalInput")
    ext["w_gate"] = ctx.dram("w_gate", [D, D], F32, kind="ExternalInput")
    ext["b_gate"] = ctx.dram("b_gate", [D], F32, kind="ExternalInput")
    ext["log_decay"] = ctx.dram("log_decay", [64], F32, kind="ExternalInput")
    ext["frequency"] = ctx.dram("frequency", [64], F32, kind="ExternalInput")
    ext["coupling"] = ctx.dram("coupling", [64, 16], F32, kind="ExternalInput")
    ext["w_mlp1"] = ctx.dram("w_mlp1", [MLPD, D], F32, kind="ExternalInput")
    ext["b_mlp1"] = ctx.dram("b_mlp1", [MLPD], F32, kind="ExternalInput")
    ext["w_mlp2"] = ctx.dram("w_mlp2", [D, MLPD], F32, kind="ExternalInput")
    ext["b_mlp2"] = ctx.dram("b_mlp2", [D], F32, kind="ExternalInput")
    ext["ln1_g"] = ctx.dram("ln1_g", [D], F32, kind="ExternalInput")
    ext["ln1_b"] = ctx.dram("ln1_b", [D], F32, kind="ExternalInput")
    ext["ln2_g"] = ctx.dram("ln2_g", [D], F32, kind="ExternalInput")
    ext["ln2_b"] = ctx.dram("ln2_b", [D], F32, kind="ExternalInput")
    ext["a1out"] = ctx.dram("a1out", [B_CORE * T, D], F32, kind="ExternalOutput")
    emit_a1(ctx, ext)
    return ctx.emit()


# version 28
# speedup vs baseline: 1.0727x; 1.0727x over previous
import numpy as np
import ml_dtypes
from contextlib import ExitStack

import concourse.bass as bass
import concourse.mybir as mybir
from concourse.bass_utils import run_bass_kernel_spmd

F32 = mybir.dt.float32
BF16 = mybir.dt.bfloat16
I32 = mybir.dt.int32
AF = mybir.ActivationFunctionType
ALU = mybir.AluOpType
AX = mybir.AxisListType

ENGINES = ("tensor", "vector", "scalar", "gpsimd", "sync")

N_CORES = 8
B_CORE = 4
T = 2048
D = 1024
NT = T // 128
ND = D // 128



class DmaStream:
    __slots__ = ("sem", "count")

    def __init__(self, sem):
        self.sem = sem
        self.count = 0

    def tok(self):
        return (self.sem, self.count)


class Buf:
    __slots__ = ("h", "name", "w", "readers", "cm", "freed")

    def __init__(self, h, name, cm=None):
        self.h = h
        self.name = name
        self.w = None
        self.readers = []
        self.cm = cm
        self.freed = False

    def __getitem__(self, idx):
        return self.h[idx]


class Eng:
    def __init__(self, ctx, key, sem):
        self.ctx = ctx
        self.key = key
        self.sem = sem
        self.count = 0
        self.watermark = {}

    def _rec(self, fn):
        self.ctx.cmds[self.key].append(fn)

    def _wait_tok(self, tok):
        if tok is None:
            return
        sem, val = tok
        key = sem.name
        if self.watermark.get(key, -1) >= val:
            return
        self.watermark[key] = val
        self._rec(lambda eng, s=sem, v=val: eng.wait_ge(s, v))

    def dep(self, outs, ins):
        for b in ins:
            if b is not None:
                self._wait_tok(b.w)
        for b in outs:
            if b is not None:
                self._wait_tok(b.w)
                for t_ in b.readers:
                    self._wait_tok(t_)

    def _mark(self, outs, ins):
        self.count += 1
        tok = (self.sem, self.count)
        for b in outs:
            if b is not None:
                b.w = tok
                b.readers = []
        for b in ins:
            if b is not None:
                b.readers.append(tok)
        return tok

    def op(self, method, outs, ins, *args, inc=True, **kwargs):
        self.dep(outs, ins)
        if inc:
            tok = self._mark(outs, ins)
            sem = self.sem
            self._rec(lambda eng, m=method, a=args, k=kwargs:
                      getattr(eng, m)(*a, **k).then_inc(sem, 1))
            return tok
        else:
            self._rec(lambda eng, m=method, a=args, k=kwargs:
                      getattr(eng, m)(*a, **k))
            return None

    def raw(self, fn):
        self._rec(fn)

    def tok(self):
        return (self.sem, self.count)


class Ctx:
    def __init__(self):
        self.nc = bass.Bass("TRN2", target_bir_lowering=False)
        self.es = ExitStack()
        self.pfx = ""
        self.alloc_log = []
        self.cmds = {k: [] for k in ENGINES}
        self.t = Eng(self, "tensor", self._sem("semP"))
        self.v = Eng(self, "vector", self._sem("semD"))
        self.s = Eng(self, "scalar", self._sem("semA"))
        self.g = Eng(self, "gpsimd", self._sem("semG"))
        self.sy = Eng(self, "sync", self._sem("semS"))
        self._consts = {}
        self._const_tile = None
        self._nconst = 0
        self._ident = {}
        self._iota = None
        self._retired = {}

    def _sem(self, name):
        return self.es.enter_context(self.nc.semaphore(self.pfx + name))

    def eng(self, key):
        return {"tensor": self.t, "vector": self.v, "scalar": self.s,
                "gpsimd": self.g, "sync": self.sy}[key]

    def sbuf(self, name, shape, dt=F32, side=None):
        name = self.pfx + name
        cm = self.nc.sbuf_tensor(name, shape, dt, side=side)
        b = Buf(cm.__enter__(), name, cm)
        b.readers = list(self._retired.values())
        self.alloc_log.append(b)
        return b

    def free(self, *bufs):
        toks = []
        for b in bufs:
            if b.freed or b.cm is None:
                continue
            if b.w is not None:
                toks.append(b.w)
            toks.extend(b.readers)
            b.cm.__exit__(None, None, None)
            b.freed = True
        for tk in toks:
            k = tk[0].name
            if k not in self._retired or self._retired[k][1] < tk[1]:
                self._retired[k] = tk
        return toks

    def psum(self, name, shape, dt=F32, after=None):
        name = self.pfx + name
        cm = self.nc.psum_tensor(name, shape, dt)
        b = Buf(cm.__enter__(), name, cm)
        b.readers = list(self._retired.values())
        if after:
            b.readers += list(after)
        self.alloc_log.append(b)
        return b

    def dram(self, name, shape, dt=F32, kind="Internal"):
        if kind == "Internal":
            name = self.pfx + name
        return Buf(self.nc.dram_tensor(name, shape, dt, kind=kind).ap(), name)

    def phase(self, pfx):
        from contextlib import contextmanager

        @contextmanager
        def _cm():
            old = self.pfx
            self.pfx = pfx
            mark = len(self.alloc_log)
            yield
            for b in reversed(self.alloc_log[mark:]):
                self.free(b)
            del self.alloc_log[mark:]
            self.pfx = old
        return _cm()

    def dma_stream(self, name):
        return DmaStream(self._sem(name))

    def dma_op(self, queue, stream, pairs, outs, ins, **kwargs):
        e = self.eng(queue)
        e.dep(outs, ins)
        if stream.count > 0:
            e._wait_tok((stream.sem, stream.count))
        sem = stream.sem
        for (o, i) in pairs:
            e._rec(lambda eng, o=o, i=i, k=kwargs, s=sem:
                   eng.dma_start(out=o, in_=i, **k).then_inc(s, 16))
        stream.count += 16 * len(pairs)
        tok = (sem, stream.count)
        for b in outs:
            if b is not None:
                b.w = tok
                b.readers = []
        for b in ins:
            if b is not None:
                b.readers.append(tok)
        return tok

    def dma_transpose_op(self, queue, stream, out, in_, outs, ins):
        assert queue == "sync", "keep every dma_start_transpose on the sync queue"
        e = self.eng(queue)
        e.dep(outs, ins)
        if stream.count > 0:
            e._wait_tok((stream.sem, stream.count))
        sem = stream.sem
        e._rec(lambda eng, o=out, i=in_, s=sem:
               eng.dma_start_transpose(o, i).then_inc(s, 16))
        stream.count += 16
        tok = (sem, stream.count)
        for b in outs:
            if b is not None:
                b.w = tok
                b.readers = []
        for b in ins:
            if b is not None:
                b.readers.append(tok)
        return tok

    def const(self, value):
        if value in self._consts:
            return self._consts[value]
        if self._const_tile is None:
            self._const_tile = self.sbuf("consts", [128, 32], F32)
        i = self._nconst
        self._nconst += 1
        assert i < 32
        sl = Buf(self._const_tile.h[:, i:i + 1], f"const_{value}")
        self.g.op("memset", [sl], [], sl.h, float(value))
        self._consts[value] = sl
        return sl

    def identity(self, dt=BF16):
        if dt in self._ident:
            return self._ident[dt]
        if self._iota is None:
            self._iota = self.sbuf("iden_iota", [128, 128], F32)
            self.g.op("iota", [self._iota], [], self._iota[:],
                      pattern=[[1, 128]], base=0, channel_multiplier=-1,
                      allow_small_or_imprecise_dtypes=True)
        idb = self.sbuf(f"ident_{dt}", [128, 128], dt)
        self.v.op("tensor_single_scalar", [idb], [self._iota], idb[:],
                  self._iota[:], 0.0, op=ALU.is_equal)
        self._ident[dt] = idb
        return idb

    def emit(self):
        nc = self.nc
        with nc.Block() as block:
            for key, dec in (("sync", block.sync), ("tensor", block.tensor),
                             ("vector", block.vector), ("scalar", block.scalar),
                             ("gpsimd", block.gpsimd)):
                cmds = self.cmds[key]

                def section(eng, cmds=cmds):
                    for fn in cmds:
                        fn(eng)
                dec(section)
        self.es.close()
        return nc

    def n_inst(self):
        return sum(len(c) for c in self.cmds.values())


def alias_pull(dst, *srcs):
    for s_ in srcs:
        if s_.w is not None:
            dst.readers.append(s_.w)
        dst.readers.extend(s_.readers)
        s_.readers = []



def make_stat_slices(ctx, stats, nsets):
    out = []
    for j in range(nsets):
        out.append(tuple(Buf(stats.h[:, 4 * j + k:4 * j + k + 1],
                             f"{stats.name}_{j}_{k}") for k in range(4)))
    return out


def emit_ln_stats(ctx, x, x_ap, sl, eps_c, scratch, dim=1024):
    v, s = ctx.v, ctx.s
    nsum, sumsq, nmu, rstd = sl
    v.op("tensor_reduce", [nsum], [x], nsum.h, x_ap, axis=AX.X, op=ALU.add,
         negate=True)
    s.op("activation", [scratch, sumsq], [x], scratch.h[:, 0:x_ap.shape[-1]],
         x_ap, AF.Square, accum_out=sumsq.h)
    v.op("tensor_scalar", [nmu], [nsum], nmu.h, nsum.h, 1.0 / dim, None,
         op0=ALU.mult, op1=ALU.bypass)
    v.op("tensor_tensor", [scratch], [nmu], scratch.h[:, 0:1], nmu.h, nmu.h,
         op=ALU.mult)
    v.op("scalar_tensor_tensor", [rstd], [sumsq, scratch], rstd.h, sumsq.h,
         1.0 / dim, scratch.h[:, 0:1], op0=ALU.mult, op1=ALU.subtract)
    s.op("activation", [rstd], [rstd, eps_c], rstd.h, rstd.h, AF.Sqrt,
         bias=eps_c.h)
    v.op("reciprocal", [rstd], [rstd], rstd.h, rstd.h)
    return nmu, rstd



def emit_ln(ctx, ext, tag="ln0"):
    x = ext["x"]
    xnT = ext["xnT"]
    eps_c = ctx.const(1e-5)
    ident = ctx.identity(BF16)

    t, v, s, g, sy = ctx.t, ctx.v, ctx.s, ctx.g, ctx.sy
    xin = [ctx.sbuf(f"xin{i}", [128, D], F32) for i in range(2)]
    xh = [ctx.sbuf(f"xh{i}", [128, D], BF16) for i in range(2)]
    scratch = [ctx.sbuf(f"scr{i}", [128, D], F32) for i in range(2)]
    stats = ctx.sbuf("stats", [128, 8], F32)
    statsl = make_stat_slices(ctx, stats, 2)
    fmacc = ctx.sbuf("fmacc", [128, ND * T], BF16)
    pst = [ctx.psum(f"pst{i}", [128, 128], BF16) for i in range(4)]
    ld = [ctx.dma_stream(f"ld{i}") for i in range(2)]
    sto = ctx.dma_stream("sto")

    for b in range(B_CORE):
        for it in range(NT):
            i = (b * NT + it) % 2
            r0 = (b * NT + it) * 128
            ctx.dma_op("sync", ld[i], [(xin[i][:], x.h[r0:r0 + 128, :])],
                       [xin[i]], [x])
            nmu, rstd = emit_ln_stats(ctx, xin[i], xin[i][:], statsl[i],
                                      eps_c, scratch[i])
            v.op("tensor_scalar", [xh[i]], [xin[i], nmu, rstd], xh[i][:],
                 xin[i][:], nmu.h, rstd.h, op0=ALU.add, op1=ALU.mult)
            for c in range(ND):
                p = pst[(it * ND + c) % 4]
                t.op("transpose", [p], [xh[i], ident], p[:],
                     xh[i][:, c * 128:(c + 1) * 128], ident[:])
                v.op("tensor_copy", [fmacc], [p],
                     fmacc.h[:, c * T + it * 128: c * T + (it + 1) * 128], p[:])
        ctx.dma_op("gpsimd", sto,
                   [(xnT.h[b, c * 128:(c + 1) * 128, :],
                     fmacc.h[0:128, c * T:(c + 1) * T]) for c in range(ND)],
                   [xnT], [fmacc])
    g.raw(lambda eng, s_=sto.sem, v_=sto.count: eng.wait_ge(s_, v_))


def build_ln_launch(tag="ln0"):
    ctx = Ctx()
    ext = {}
    ext["x"] = ctx.dram("x", [B_CORE * T, D], F32, kind="ExternalInput")
    ext["xnT"] = ctx.dram("xnT", [B_CORE, D, T], BF16, kind="ExternalOutput")
    emit_ln(ctx, ext, tag)
    return ctx.emit()


_PROGS = {}


def _get_prog(name, builder):
    if name not in _PROGS:
        _PROGS[name] = builder()
    return _PROGS[name]


def _run(name, builder, per_core, shared):
    nc = _get_prog(name, builder)
    in_maps = []
    for c in range(N_CORES):
        m = {k: v[c] for k, v in per_core.items()}
        m.update(shared)
        in_maps.append(m)
    res = run_bass_kernel_spmd(nc, in_maps, core_ids=list(range(N_CORES)))
    return res.results


def build_fused():
    ctx = Ctx()
    ctx.const(1e-5)
    ctx.const(-np.pi)
    ctx.identity(BF16)
    d = ctx.dram
    ex = {}
    ex["x"] = d("x", [B_CORE * T, D], F32, kind="ExternalInput")
    for n in ("W_q", "W_k", "W_v", "W_gate", "W_o"):
        ex[n] = d(n, [D, D], F32, kind="ExternalInput")
    ex["b_gate0"] = d("b_gate0", [D], F32, kind="ExternalInput")
    ex["conv_w"] = d("conv_w", [D, D * 3], F32, kind="ExternalInput")
    ex["conv_b"] = d("conv_b", [D], F32, kind="ExternalInput")
    ex["wqkv"] = d("wqkv", [3 * D, D], F32, kind="ExternalInput")
    ex["bqkv"] = d("bqkv", [3 * D], F32, kind="ExternalInput")
    ex["wo"] = d("wo", [D, D], F32, kind="ExternalInput")
    ex["bo"] = d("bo", [D], F32, kind="ExternalInput")
    ex["f0_ln1_g"] = d("f0_ln1_g", [D], F32, kind="ExternalInput")
    ex["f0_ln1_b"] = d("f0_ln1_b", [D], F32, kind="ExternalInput")
    ex["gate_w"] = d("gate_w", [D, NEXP], F32, kind="ExternalInput")
    ex["gate_b"] = d("gate_b", [NEXP], F32, kind="ExternalInput")
    ex["w1"] = d("w1", [NEXP * D, DFF], F32, kind="ExternalInput")
    ex["b1"] = d("b1", [NEXP, DFF], F32, kind="ExternalInput")
    ex["w2"] = d("w2", [NEXP * DFF, D], F32, kind="ExternalInput")
    ex["b2"] = d("b2", [NEXP, D], F32, kind="ExternalInput")
    ex["f0_ln2_g"] = d("f0_ln2_g", [D], F32, kind="ExternalInput")
    ex["f0_ln2_b"] = d("f0_ln2_b", [D], F32, kind="ExternalInput")
    ex["w_in"] = d("w_in", [128, D], F32, kind="ExternalInput")
    ex["w_out"] = d("w_out", [D, 128], F32, kind="ExternalInput")
    ex["a1_w_gate"] = d("a1_w_gate", [D, D], F32, kind="ExternalInput")
    ex["a1_b_gate"] = d("a1_b_gate", [D], F32, kind="ExternalInput")
    ex["log_decay"] = d("log_decay", [64], F32, kind="ExternalInput")
    ex["frequency"] = d("frequency", [64], F32, kind="ExternalInput")
    ex["coupling"] = d("coupling", [64, 16], F32, kind="ExternalInput")
    ex["w_mlp1"] = d("w_mlp1", [MLPD, D], F32, kind="ExternalInput")
    ex["b_mlp1"] = d("b_mlp1", [MLPD], F32, kind="ExternalInput")
    ex["w_mlp2"] = d("w_mlp2", [D, MLPD], F32, kind="ExternalInput")
    ex["b_mlp2"] = d("b_mlp2", [D], F32, kind="ExternalInput")
    ex["a1_ln1_g"] = d("a1_ln1_g", [D], F32, kind="ExternalInput")
    ex["a1_ln1_b"] = d("a1_ln1_b", [D], F32, kind="ExternalInput")
    ex["a1_ln2_g"] = d("a1_ln2_g", [D], F32, kind="ExternalInput")
    ex["a1_ln2_b"] = d("a1_ln2_b", [D], F32, kind="ExternalInput")
    ex["f1_w_gate"] = d("f1_w_gate", [HID1, D], I32, kind="ExternalInput")
    ex["f1_w_up"] = d("f1_w_up", [HID1, D], I32, kind="ExternalInput")
    ex["f1_w_down"] = d("f1_w_down", [D, HID1], I32, kind="ExternalInput")
    ex["out"] = d("out", [B_CORE * T, D], F32, kind="ExternalOutput")
    io = {}
    io["xnT"] = d("io_xnT", [B_CORE, D, T], BF16)
    io["a0out"] = d("io_a0out", [B_CORE * T, D], F32)
    io["qh"] = d("io_qh", [B_CORE, D, T], BF16)
    io["kh"] = d("io_kh", [B_CORE, D, T], BF16)
    io["vp"] = d("io_vp", [B_CORE, 128, NT * 1040], BF16)
    io["ybT"] = d("io_ybT", [B_CORE * T, D], BF16)
    io["xl1T"] = d("io_xl1T", [B_CORE, D, T], BF16)
    io["xl1"] = d("io_xl1", [B_CORE * T, D], F32)
    io["f0out"] = d("io_f0out", [B_CORE * T, D], F32)
    io["x1"] = d("io_x1", [B_CORE * T, D], F32)
    io["xn1"] = d("io_xn1", [B_CORE * T, D], F32)
    io["xn1T"] = d("io_xn1T", [B_CORE, D, T], BF16)
    io["a1out"] = d("io_a1out", [B_CORE * T, D], F32)
    io["f1out"] = d("io_f1out", [B_CORE * T, D], F32)

    with ctx.phase("p0_"):
        emit_ln(ctx, {"x": ex["x"], "xnT": io["xnT"]})
    with ctx.phase("p1_"):
        emit_gla(ctx, {"xnT": io["xnT"], "W_q": ex["W_q"], "W_k": ex["W_k"],
                       "W_v": ex["W_v"], "W_gate": ex["W_gate"],
                       "b_gate": ex["b_gate0"], "W_o": ex["W_o"],
                       "a0out": io["a0out"]})
    with ctx.phase("p2_"):
        emit_f0conv(ctx, {"xnT": io["xnT"], "conv_w": ex["conv_w"],
                          "conv_b": ex["conv_b"], "wqkv": ex["wqkv"],
                          "bqkv": ex["bqkv"], "qh": io["qh"], "kh": io["kh"],
                          "vp": io["vp"], "ybT": io["ybT"]})
    with ctx.phase("p3_"):
        emit_f0attn(ctx, {"qh": io["qh"], "kh": io["kh"], "vp": io["vp"],
                          "ybT": io["ybT"], "wo": ex["wo"], "bo": ex["bo"],
                          "ln1_g": ex["f0_ln1_g"], "ln1_b": ex["f0_ln1_b"],
                          "xl1T": io["xl1T"], "xl1": io["xl1"]})
    with ctx.phase("p4_"):
        emit_moe(ctx, {"xl1T": io["xl1T"], "xl1": io["xl1"],
                       "gate_w": ex["gate_w"], "gate_b": ex["gate_b"],
                       "w1": ex["w1"], "b1": ex["b1"], "w2": ex["w2"],
                       "b2": ex["b2"], "ln2_g": ex["f0_ln2_g"],
                       "ln2_b": ex["f0_ln2_b"], "f0out": io["f0out"]})
    with ctx.phase("p5_"):
        emit_sum_ln(ctx, {"x0": ex["x"], "a0": io["a0out"], "f0": io["f0out"],
                          "x1": io["x1"], "xn1": io["xn1"], "xn1T": io["xn1T"]})
    with ctx.phase("p6_"):
        emit_a1(ctx, {"xn1": io["xn1"], "w_in": ex["w_in"], "w_out": ex["w_out"],
                      "w_gate": ex["a1_w_gate"], "b_gate": ex["a1_b_gate"],
                      "log_decay": ex["log_decay"], "frequency": ex["frequency"],
                      "coupling": ex["coupling"], "w_mlp1": ex["w_mlp1"],
                      "b_mlp1": ex["b_mlp1"], "w_mlp2": ex["w_mlp2"],
                      "b_mlp2": ex["b_mlp2"], "ln1_g": ex["a1_ln1_g"],
                      "ln1_b": ex["a1_ln1_b"], "ln2_g": ex["a1_ln2_g"],
                      "ln2_b": ex["a1_ln2_b"], "a1out": io["a1out"]})
    with ctx.phase("p7_"):
        emit_f1(ctx, {"xnT": io["xn1T"], "w_gate": ex["f1_w_gate"],
                      "w_up": ex["f1_w_up"], "w_down": ex["f1_w_down"],
                      "f1out": ex["out"], "x1": io["x1"], "a1": io["a1out"]})
    return ctx.emit()


def kernel(**inp):
    x = np.ascontiguousarray(inp["L0_attn_x"].reshape(32, T, D))
    xs = [np.ascontiguousarray(x[c * B_CORE:(c + 1) * B_CORE].reshape(B_CORE * T, D))
          for c in range(N_CORES)]
    shared = {
        "W_q": inp["L0_attn_W_q"], "W_k": inp["L0_attn_W_k"],
        "W_v": inp["L0_attn_W_v"], "W_gate": inp["L0_attn_W_gate"],
        "b_gate0": inp["L0_attn_b_gate"], "W_o": inp["L0_attn_W_o"],
        "conv_w": np.ascontiguousarray(inp["L0_ffn_conv_w"].reshape(D, 3 * D)),
        "conv_b": inp["L0_ffn_conv_b"], "wqkv": inp["L0_ffn_wqkv"],
        "bqkv": inp["L0_ffn_bqkv"], "wo": inp["L0_ffn_wo"],
        "bo": inp["L0_ffn_bo"], "f0_ln1_g": inp["L0_ffn_ln1_g"],
        "f0_ln1_b": inp["L0_ffn_ln1_b"], "gate_w": inp["L0_ffn_gate_w"],
        "gate_b": inp["L0_ffn_gate_b"],
        "w1": np.ascontiguousarray(inp["L0_ffn_w1"].reshape(NEXP * D, DFF)),
        "b1": inp["L0_ffn_b1"],
        "w2": np.ascontiguousarray(inp["L0_ffn_w2"].reshape(NEXP * DFF, D)),
        "b2": inp["L0_ffn_b2"], "f0_ln2_g": inp["L0_ffn_ln2_g"],
        "f0_ln2_b": inp["L0_ffn_ln2_b"],
        "w_in": inp["L1_attn_w_in"], "w_out": inp["L1_attn_w_out"],
        "a1_w_gate": inp["L1_attn_w_gate"], "a1_b_gate": inp["L1_attn_b_gate"],
        "log_decay": inp["L1_attn_log_decay"],
        "frequency": inp["L1_attn_frequency"],
        "coupling": np.ascontiguousarray(inp["L1_attn_coupling"].reshape(64, 16)),
        "w_mlp1": inp["L1_attn_w_mlp1"], "b_mlp1": inp["L1_attn_b_mlp1"],
        "w_mlp2": inp["L1_attn_w_mlp2"], "b_mlp2": inp["L1_attn_b_mlp2"],
        "a1_ln1_g": inp["L1_attn_ln1_g"], "a1_ln1_b": inp["L1_attn_ln1_b"],
        "a1_ln2_g": inp["L1_attn_ln2_g"], "a1_ln2_b": inp["L1_attn_ln2_b"],
        "f1_w_gate": inp["L1_ffn_w_gate"], "f1_w_up": inp["L1_ffn_w_up"],
        "f1_w_down": inp["L1_ffn_w_down"],
    }
    nc = _get_prog("fused", build_fused)
    in_maps = []
    for c in range(N_CORES):
        m = {"x": xs[c]}
        m.update(shared)
        in_maps.append(m)
    res = run_bass_kernel_spmd(nc, in_maps, core_ids=list(range(N_CORES)))
    out = np.concatenate([np.asarray(r["out"]).reshape(B_CORE, T, D)
                          for r in res.results], axis=0)
    return out


def mkap(buf, offset, dims):
    import concourse.bass as _b
    h = buf.h
    tn = h.tensor if hasattr(h, "tensor") else h
    return bass.AP(tn, offset, dims)


def mm_group(ctx, psum_buf, psum_ap, terms):
    t = ctx.t
    n = len(terms)
    for i, (l, r, ins) in enumerate(terms):
        last = (i == n - 1)
        t.op("matmul", [psum_buf] if (last or i == 0) else [], ins,
             psum_ap, l, r, start=(i == 0), stop=last, inc=last)



def emit_cast_store_T(ctx, src_hbm, scrT_hbm, rows, cols, tmp_raw, tmp_bf,
                      pst, stg, ld_stream, st_stream):
    ident = ctx.identity(BF16)
    t, v, g = ctx.t, ctx.v, ctx.g
    for r in range(rows // 128):
        i = r % 2
        ctx.dma_op("sync", ld_stream[i],
                   [(tmp_raw[i].h[:, 0:cols], src_hbm.h[r * 128:(r + 1) * 128, :])],
                   [tmp_raw[i]], [src_hbm])
        g.op("tensor_copy", [tmp_bf[i]], [tmp_raw[i]], tmp_bf[i].h[:, 0:cols],
             tmp_raw[i].h[:, 0:cols])
        nch = cols // 128
        for c0 in range(0, nch, 8):
            grp = min(8, nch - c0)
            j = (r * nch + c0) % 2
            for c in range(c0, c0 + grp):
                jj = c % 2
                t.op("transpose", [pst[jj]], [tmp_bf[i], ident], pst[jj][:, 0:128],
                     tmp_bf[i].h[:, c * 128:(c + 1) * 128], ident[:])
                (ctx.v if c % 2 == 0 else ctx.s).op(
                     "tensor_copy" if c % 2 == 0 else "copy",
                     [stg[j]], [pst[jj]],
                     stg[j].h[:, (c - c0) * 128:(c - c0 + 1) * 128],
                     pst[jj][:, 0:128])
            ctx.dma_op("gpsimd" if j == 0 else "scalar", st_stream[j],
                       [(mkap(scrT_hbm, (c0 * 128) * rows + r * 128,
                              [[rows, 128], [rows * 128, grp], [1, 128]]),
                         mkap(stg[j], 0, [[1024, 128], [128, grp], [1, 128]]))],
                       [scrT_hbm], [stg[j]])


def emit_cast_store(ctx, src_hbm, scr_hbm, rows, cols, tmp_raw, tmp_bf,
                    ld_stream, st_stream, queue_in="sync", queue_out="gpsimd"):
    nch = rows // 128
    for r in range(nch):
        i = r % 2
        ctx.dma_op(queue_in, ld_stream[i],
                   [(tmp_raw[i].h[:, 0:cols], src_hbm.h[r * 128:(r + 1) * 128, :])],
                   [tmp_raw[i]], [src_hbm])
        ctx.v.op("tensor_copy", [tmp_bf[i]], [tmp_raw[i]], tmp_bf[i].h[:, 0:cols],
                 tmp_raw[i].h[:, 0:cols])
        ctx.dma_op(queue_out, st_stream[i],
                   [(scr_hbm.h[r * 128:(r + 1) * 128, :], tmp_bf[i].h[:, 0:cols])],
                   [scr_hbm], [tmp_bf[i]])



HID1 = 3072
FB = 512
NFB = HID1 // FB


def emit_f1(ctx, ext):
    fuse_sum = "x1" in ext
    xnT = ext["xnT"]
    wg = ext["w_gate"]
    wu = ext["w_up"]
    wd = ext["w_down"]
    out = ext["f1out"]
    wg_s = ctx.dram("wg_s", [D, HID1], BF16)
    wu_s = ctx.dram("wu_s", [D, HID1], BF16)
    wd_s = ctx.dram("wd_s", [HID1, D], BF16)

    t, v, s, g = ctx.t, ctx.v, ctx.s, ctx.g

    tmp_raw = [ctx.sbuf(f"tr{i}", [128, HID1], I32, side="right") for i in range(2)]
    tmp_bf = [ctx.sbuf(f"tb{i}", [128, HID1], BF16, side="right") for i in range(2)]
    stg2 = [ctx.sbuf(f"stg{i}", [128, 1024], BF16, side="right") for i in range(2)]
    pstg = [ctx.psum(f"pstg{i}", [128, 128], BF16) for i in range(2)]
    lds = [ctx.dma_stream(f"plds{i}") for i in range(2)]
    sts = [ctx.dma_stream(f"psts{i}") for i in range(2)]
    for (src, scr, rows, cols) in ((wg, wg_s, HID1, D), (wu, wu_s, HID1, D),
                                   (wd, wd_s, D, HID1)):
        emit_cast_store_T(ctx, src, scr, rows, cols, tmp_raw, tmp_bf, pstg,
                          stg2, lds, sts)
    ctx.free(stg2[1], stg2[0], tmp_bf[1], tmp_bf[0], tmp_raw[1], tmp_raw[0])
    frF = ctx.free(pstg[1], pstg[0])

    xf = ctx.sbuf("xf", [128, ND * T], BF16)
    if fuse_sum:
        resx = [ctx.sbuf(f"resx{i}", [128, D], F32) for i in range(2)]
        resa = [ctx.sbuf(f"resa{i}", [128, D], F32) for i in range(2)]
        lrx = [ctx.dma_stream(f"lrx{i}") for i in range(2)]
        lra = [ctx.dma_stream(f"lra{i}") for i in range(2)]
    wgt = [ctx.sbuf(f"wgt{i}", [128, ND * FB], BF16) for i in range(2)]
    wut = [ctx.sbuf(f"wut{i}", [128, ND * FB], BF16) for i in range(2)]
    wdt = [ctx.sbuf(f"wdt{i}", [128, (FB // 128) * D], BF16) for i in range(2)]
    gsil2 = [ctx.sbuf(f"gsil{i}", [128, 512], BF16) for i in range(2)]
    ht2 = [ctx.sbuf(f"ht{i}", [128, (FB // 128) * T], BF16) for i in range(2)]
    yacc = ctx.sbuf("yacc", [128, NT * D], F32)
    ps_g = [ctx.psum(f"ps_g{i}", [128, 512], F32) for i in range(2)]
    ps_u = [ctx.psum(f"ps_u{i}", [128, 512], F32) for i in range(2)]
    ps_y = [ctx.psum(f"ps_y{i}", [128, D], F32) for i in range(2)]

    ldx = ctx.dma_stream("ldx")
    ldw = [ctx.dma_stream(f"ldw{i}") for i in range(2)]
    stw = ctx.dma_stream("stw")
    SCALE = 1.0 / 127.0

    for b in range(B_CORE):
        ctx.dma_op("sync", ldx,
                   [(xf.h[:, c * T:(c + 1) * T], xnT.h[b, c * 128:(c + 1) * 128, :])
                    for c in range(ND)], [xf], [xnT])
        for fb in range(NFB):
            i = fb % 2
            ht = ht2[i]
            f0 = fb * FB
            for c in range(ND):
                ctx.dma_op("sync" if c % 2 == 0 else "scalar", ldw[i],
                           [(wgt[i].h[:, c * FB:(c + 1) * FB],
                             wg_s.h[c * 128:(c + 1) * 128, f0:f0 + FB]),
                            (wut[i].h[:, c * FB:(c + 1) * FB],
                             wu_s.h[c * 128:(c + 1) * 128, f0:f0 + FB])],
                           [wgt[i], wut[i]], [wg_s, wu_s])
            for fc in range(FB // 128):
                ctx.dma_op("sync", ldw[i],
                           [(wdt[i].h[:, fc * D:(fc + 1) * D],
                             wd_s.h[f0 + fc * 128:f0 + (fc + 1) * 128, 0:D])],
                           [wdt[i]], [wd_s])
            for fc in range(FB // 128):
                for ts in range(T // 512):
                    pg = ps_g[(fc * (T // 512) + ts) % 2]
                    pu = ps_u[(fc * (T // 512) + ts) % 2]
                    mm_group(ctx, pg, pg[:, 0:512],
                             [(wgt[i].h[:, c * FB + fc * 128:c * FB + fc * 128 + 128],
                               xf.h[:, c * T + ts * 512:c * T + (ts + 1) * 512],
                               [wgt[i], xf]) for c in range(ND)])
                    mm_group(ctx, pu, pu[:, 0:512],
                             [(wut[i].h[:, c * FB + fc * 128:c * FB + fc * 128 + 128],
                               xf.h[:, c * T + ts * 512:c * T + (ts + 1) * 512],
                               [wut[i], xf]) for c in range(ND)])
                    gs = gsil2[ts % 2]
                    s.op("activation", [gs], [pg], gs[:],
                         pg[:, 0:512], AF.Silu, scale=SCALE)
                    v.op("tensor_tensor", [ht], [gs, pu],
                         ht.h[:, fc * T + ts * 512:fc * T + (ts + 1) * 512],
                         gs[:], pu[:, 0:512], op=ALU.mult)
            for it in range(NT):
                py = ps_y[it % 2]
                terms = []
                for fc in range(FB // 128):
                    for dh in range(2):
                        terms.append((ht.h[:, fc * T + it * 128:fc * T + (it + 1) * 128],
                                      wdt[i].h[:, fc * D + dh * 512:fc * D + (dh + 1) * 512],
                                      [ht, wdt[i]]))
                terms = [terms[fc * 2 + dh] for dh in range(2) for fc in range(FB // 128)]
                for dh in range(2):
                    grp = [(ht.h[:, fc * T + it * 128:fc * T + (it + 1) * 128],
                            wdt[i].h[:, fc * D + dh * 512:fc * D + (dh + 1) * 512],
                            [ht, wdt[i]]) for fc in range(FB // 128)]
                    mm_group(ctx, py, py[:, dh * 512:(dh + 1) * 512], grp)
                if fb == 0:
                    v.op("tensor_scalar", [yacc], [py],
                         yacc.h[:, it * D:(it + 1) * D], py[:, 0:D],
                         SCALE * SCALE, None, op0=ALU.mult, op1=ALU.bypass)
                else:
                    v.op("scalar_tensor_tensor", [yacc], [py],
                         yacc.h[:, it * D:(it + 1) * D], py[:, 0:D],
                         SCALE * SCALE, yacc.h[:, it * D:(it + 1) * D],
                         op0=ALU.mult, op1=ALU.add)
        if fuse_sum:
            for it in range(NT):
                i2 = it % 2
                r0 = b * T + it * 128
                ctx.dma_op("sync", lrx[i2], [(resx[i2][:], ext["x1"].h[r0:r0 + 128, :])],
                           [resx[i2]], [ext["x1"]])
                ctx.dma_op("scalar", lra[i2], [(resa[i2][:], ext["a1"].h[r0:r0 + 128, :])],
                           [resa[i2]], [ext["a1"]])
                v.op("tensor_tensor", [yacc], [resx[i2]],
                     yacc.h[:, it * D:(it + 1) * D],
                     yacc.h[:, it * D:(it + 1) * D], resx[i2][:], op=ALU.add)
                v.op("tensor_tensor", [yacc], [resa[i2]],
                     yacc.h[:, it * D:(it + 1) * D],
                     yacc.h[:, it * D:(it + 1) * D], resa[i2][:], op=ALU.add)
                ctx.dma_op("gpsimd", stw,
                           [(out.h[r0:r0 + 128, :], yacc.h[:, it * D:(it + 1) * D])],
                           [out], [yacc])
        else:
            ctx.dma_op("gpsimd", stw,
                       [(out.h[b * T + it * 128:b * T + (it + 1) * 128, :],
                         yacc.h[:, it * D:(it + 1) * D]) for it in range(NT)],
                       [out], [yacc])
    g.raw(lambda eng, s_=stw.sem, v_=stw.count: eng.wait_ge(s_, v_))


def build_f1_launch():
    ctx = Ctx()
    ext = {}
    ext["xnT"] = ctx.dram("xnT", [B_CORE, D, T], BF16, kind="ExternalInput")
    ext["w_gate"] = ctx.dram("w_gate", [HID1, D], I32, kind="ExternalInput")
    ext["w_up"] = ctx.dram("w_up", [HID1, D], I32, kind="ExternalInput")
    ext["w_down"] = ctx.dram("w_down", [D, HID1], I32, kind="ExternalInput")
    ext["f1out"] = ctx.dram("f1out", [B_CORE * T, D], F32, kind="ExternalOutput")
    emit_f1(ctx, ext)
    return ctx.emit()


CHUNK = 128
NCH = T // CHUNK


def emit_gla(ctx, ext):
    xnT = ext["xnT"]
    w_hbm = {n: ext[n] for n in ("W_q", "W_k", "W_v", "W_gate", "W_o")}
    bg_hbm = ext["b_gate"]
    out = ext["a0out"]

    t, v, s, g = ctx.t, ctx.v, ctx.s, ctx.g
    ident = ctx.identity(BF16)
    iot = ctx._iota
    maskb = ctx.sbuf("maskb", [128, 128], BF16)
    v.op("tensor_single_scalar", [maskb], [iot], maskb[:], iot[:], 0.0,
         op=ALU.is_ge)

    wts = {}
    wraw = [ctx.sbuf(f"wraw{i}", [128, D], F32, side="right") for i in range(2)]
    lw = [ctx.dma_stream(f"lw{i}") for i in range(2)]
    for n in ("W_q", "W_k", "W_v", "W_gate", "W_o"):
        wt = ctx.sbuf("b" + n, [128, ND * D], BF16)
        for c in range(ND):
            i = c % 2
            ctx.dma_op("sync", lw[i],
                       [(wraw[i][:], w_hbm[n].h[c * 128:(c + 1) * 128, :])],
                       [wraw[i]], [w_hbm[n]])
            v.op("tensor_copy", [wt], [wraw[i]],
                 wt.h[:, c * D:(c + 1) * D], wraw[i][:])
        wts[n] = wt
    ctx.free(wraw[1], wraw[0])
    bgt = ctx.sbuf("bgt", [128, ND], F32)
    lbg = ctx.dma_stream("lbg")
    ctx.dma_op("sync", lbg,
               [(bgt.h[:, c:c + 1], bg_hbm.h[c * 128:(c + 1) * 128]) for c in range(ND)],
               [bgt], [bg_hbm])

    xf = ctx.sbuf("xf", [128, ND * T], BF16)
    vT = ctx.sbuf("vT", [128, NT * D], BF16)
    oF = ctx.sbuf("oF", [128, ND * T], BF16)
    at = ctx.sbuf("at", [128, T], F32)
    bt = ctx.sbuf("bt", [128, T], F32)
    qt = ctx.sbuf("qt", [128, T], BF16)
    kt = ctx.sbuf("kt", [128, T], BF16)
    khc = [ctx.sbuf(f"khc{i}", [128, 128], BF16) for i in range(2)]
    khT = [ctx.sbuf(f"khT{i}", [128, 128], BF16) for i in range(2)]
    sTm = [ctx.sbuf(f"sTm{i}", [128, 128], BF16) for i in range(2)]
    St = ctx.sbuf("St", [128, 64], F32)
    Sb = ctx.sbuf("Sb", [128, 64], BF16)

    yo = [Buf(at.h[:, 0:D], "yo0"), Buf(at.h[:, D:2 * D], "yo1")]
    ldx = ctx.dma_stream("ldx")
    stw = ctx.dma_stream("stw")

    for b in range(B_CORE):
        alias_pull(at, yo[0], yo[1])
        ctx.dma_op("sync", ldx,
                   [(xf.h[:, c * T:(c + 1) * T], xnT.h[b, c * 128:(c + 1) * 128, :])
                    for c in range(ND)], [xf], [xnT])
        pv = [ctx.psum(f"pv{b}_{i}", [128, D], F32) for i in range(2)]
        for it in range(NT):
            p = pv[it % 2]
            for dh in range(2):
                grp = [(xf.h[:, c * T + it * 128:c * T + (it + 1) * 128],
                        wts["W_v"].h[:, c * D + dh * 512:c * D + (dh + 1) * 512],
                        [xf, wts["W_v"]]) for c in range(ND)]
                mm_group(ctx, p, p[:, dh * 512:(dh + 1) * 512], grp)
            v.op("tensor_copy", [vT], [p], vT.h[:, it * D:(it + 1) * D], p[:])
        fr = ctx.free(pv[1], pv[0])
        pz = [ctx.psum(f"pz{b}_{i}", [128, 512], F32, after=fr) for i in range(2)]
        psA = [ctx.psum(f"psA{b}_{i}", [128, 128], F32) for i in range(2)]
        po = [ctx.psum(f"po{b}_{i}", [64, 128], F32) for i in range(2)]
        pD = [ctx.psum(f"pD{b}_0", [64, 64], F32)] * 2
        pT = ctx.psum(f"pT{b}", [128, 128], BF16)
        for hp in range(ND):
            for ts in range(4):
                p = pz[ts % 2]
                grp = [(wts["W_gate"].h[:, c * D + hp * 128:c * D + hp * 128 + 128],
                        xf.h[:, c * T + ts * 512:c * T + (ts + 1) * 512],
                        [wts["W_gate"], xf]) for c in range(ND)]
                mm_group(ctx, p, p[:, 0:512], grp)
                s.op("activation", [at], [p, bgt], at.h[:, ts * 512:(ts + 1) * 512],
                     p[:, 0:512], AF.Sigmoid, bias=bgt.h[:, hp:hp + 1])
            for c in range(NCH):
                v.op("tensor_tensor_scan", [bt], [at],
                     bt.h[:, c * CHUNK:(c + 1) * CHUNK],
                     at.h[:, c * CHUNK:(c + 1) * CHUNK],
                     at.h[:, c * CHUNK:(c + 1) * CHUNK], 1.0,
                     op0=ALU.mult, op1=ALU.bypass)
            s.op("activation", [at], [bt], at.h[:, :], bt.h[:, :], AF.Ln)
            s.op("activation", [at], [at], at.h[:, :], at.h[:, :], AF.Exp,
                 scale=-1.0)
            for ts in range(4):
                for (wn, dst, other) in (("W_q", qt, bt), ("W_k", kt, at)):
                    p = pz[ts % 2]
                    grp = [(wts[wn].h[:, c * D + hp * 128:c * D + hp * 128 + 128],
                            xf.h[:, c * T + ts * 512:c * T + (ts + 1) * 512],
                            [wts[wn], xf]) for c in range(ND)]
                    mm_group(ctx, p, p[:, 0:512], grp)
                    v.op("tensor_tensor", [dst], [p, other],
                         dst.h[:, ts * 512:(ts + 1) * 512], p[:, 0:512],
                         other.h[:, ts * 512:(ts + 1) * 512], op=ALU.mult)
            g.op("memset", [St], [], St[:], 0.0)
            v.op("tensor_copy", [Sb], [St], Sb[:], St[:])
            for c in range(NCH):
                i = c % 2
                cs = slice(c * CHUNK, (c + 1) * CHUNK)
                t.op("matmul", [psA[0]], [kt, qt], psA[0][:, 0:128],
                     kt.h[0:64, cs], qt.h[0:64, cs], start=True, stop=True,
                     tile_position=(0, 0))
                t.op("matmul", [psA[1]], [kt, qt], psA[1][:, 0:128],
                     kt.h[64:128, cs], qt.h[64:128, cs], start=True, stop=True,
                     tile_position=(64, 0))
                for h in range(2):
                    v.op("tensor_tensor", [sTm[h]], [psA[h], maskb], sTm[h][:],
                         psA[h][:, 0:128], maskb[:], op=ALU.mult)
                v.op("tensor_scalar", [khc[i]], [kt, bt], khc[i][:],
                     kt.h[:, cs],
                     bt.h[:, c * CHUNK + CHUNK - 1:c * CHUNK + CHUNK], None,
                     op0=ALU.mult, op1=ALU.bypass)
                t.op("transpose", [pT], [khc[i], ident], pT[:, 0:128],
                     khc[i][:], ident[:])
                v.op("tensor_copy", [khT[i]], [pT], khT[i][:], pT[:, 0:128])
                for h in range(2):
                    hb = 64 * h
                    t.op("matmul", [], [vT, sTm[h]], po[h][0:64, 0:128],
                         vT.h[:, c * D + (hp * 2 + h) * 64:c * D + (hp * 2 + h) * 64 + 64],
                         sTm[h][:], start=True, stop=False, inc=False)
                    t.op("matmul", [po[h]], [Sb, qt], po[h][0:64, 0:128],
                         Sb.h[hb:hb + 64, 0:64], qt.h[hb:hb + 64, cs],
                         start=False, stop=True,
                         tile_position=(hb, 0))
                    t.op("matmul", [pD[h]], [khT[i], vT], pD[h][0:64, 0:64],
                         khT[i][:, hb:hb + 64],
                         vT.h[:, c * D + (hp * 2 + h) * 64:c * D + (hp * 2 + h) * 64 + 64],
                         start=True, stop=True)
                    v.op("scalar_tensor_tensor", [St], [pD[h], bt],
                         St.h[hb:hb + 64, 0:64], St.h[hb:hb + 64, 0:64],
                         bt.h[hb:hb + 64, c * CHUNK + CHUNK - 1:c * CHUNK + CHUNK],
                         pD[h][0:64, 0:64], op0=ALU.mult, op1=ALU.add)
                    v.op("tensor_copy", [oF], [po[h]],
                         oF.h[hb:hb + 64, hp * T + c * CHUNK:hp * T + (c + 1) * CHUNK],
                         po[h][0:64, 0:128])
                v.op("tensor_copy", [Sb], [St], Sb[:], St[:])
        frr = ctx.free(pT, pD[0], po[1], po[0], psA[1], psA[0], pz[1], pz[0])
        pq = [ctx.psum(f"pq{b}_{i}", [128, D], F32, after=frr) for i in range(2)]
        for it in range(NT):
            p = pq[it % 2]
            for dh in range(2):
                grp = [(oF.h[:, c * T + it * 128:c * T + (it + 1) * 128],
                        wts["W_o"].h[:, c * D + dh * 512:c * D + (dh + 1) * 512],
                        [oF, wts["W_o"]]) for c in range(ND)]
                mm_group(ctx, p, p[:, dh * 512:(dh + 1) * 512], grp)
            v.op("tensor_copy", [yo[it % 2]], [p, at], yo[it % 2].h, p[:])
            ctx.dma_op("gpsimd", stw,
                       [(out.h[b * T + it * 128:b * T + (it + 1) * 128, :],
                         yo[it % 2].h)], [out], [yo[it % 2]])
        ctx.free(pq[1], pq[0])
    g.raw(lambda eng, s_=stw.sem, v_=stw.count: eng.wait_ge(s_, v_))


def build_gla_launch():
    ctx = Ctx()
    ext = {}
    for n in ("W_q", "W_k", "W_v", "W_gate", "W_o"):
        ext[n] = ctx.dram(n, [D, D], F32, kind="ExternalInput")
    ext["b_gate"] = ctx.dram("b_gate", [D], F32, kind="ExternalInput")
    ext["a0out"] = ctx.dram("a0out", [B_CORE * T, D], F32, kind="ExternalOutput")
    ext["xnT"] = ctx.dram("xnT", [B_CORE, D, T], BF16, kind="ExternalInput")
    emit_gla(ctx, ext)
    return ctx.emit()

def emit_transpose_weight(ctx, src_hbm, wt, rows, cols, tmp32, tmpbf, pst, ld,
                          row0=0):
    ident = ctx.identity(BF16)
    nro, nci = rows // 128, cols // 128
    for ro in range(nro):
        i = ro % 2
        ctx.dma_op("sync", ld[i],
                   [(tmp32[i].h[:, 0:cols], src_hbm.h[row0 + ro * 128:row0 + (ro + 1) * 128, :])],
                   [tmp32[i]], [src_hbm])
        ctx.v.op("tensor_copy", [tmpbf[i]], [tmp32[i]], tmpbf[i].h[:, 0:cols],
                 tmp32[i].h[:, 0:cols])
        for ci in range(nci):
            ctx.t.op("transpose", [pst], [tmpbf[i], ident], pst[:, 0:128],
                     tmpbf[i].h[:, ci * 128:(ci + 1) * 128], ident[:])
            ctx.v.op("tensor_copy", [wt], [pst],
                     wt.h[:, ci * rows + ro * 128:ci * rows + (ro + 1) * 128],
                     pst[:, 0:128])


def emit_row_load(ctx, dst_row_ap, src_hbm, n, dstbuf, ld, cast_via=None):
    if cast_via is None:
        ctx.dma_op("sync", ld, [(dst_row_ap, src_hbm.h[0:n])], [dstbuf], [src_hbm])
    else:
        via, viabuf = cast_via
        ctx.dma_op("sync", ld, [(via, src_hbm.h[0:n])], [viabuf], [src_hbm])
        ctx.v.op("tensor_copy", [dstbuf], [viabuf], dst_row_ap, via)



def emit_f0conv(ctx, ext):
    xnT = ext["xnT"]
    conv_w = ext["conv_w"]
    conv_b = ext["conv_b"]
    wqkv = ext["wqkv"]
    bqkv = ext["bqkv"]
    qh_o = ext["qh"]
    kh_o = ext["kh"]
    vp_o = ext["vp"]
    ybT_o = ext["ybT"]

    t, v, s, g = ctx.t, ctx.v, ctx.s, ctx.g
    ident = ctx.identity(BF16)

    wqkvT = ctx.sbuf("wqkvT", [128, ND * 3 * D], BF16)
    convT = ctx.sbuf("convT", [128, 3 * ND * D], BF16)
    colv = ctx.sbuf("colv", [128, 16], F32)
    convb_c = ctx.sbuf("convb_c", [128, ND], F32)
    rows_bf = ctx.sbuf("rows_bf", [1, 1024], BF16)
    rowf = ctx.sbuf("rowf", [1, 1024], F32)
    onesb = ctx.sbuf("onesb", [1, 128], BF16)
    g.op("memset", [onesb], [], onesb[:], 1.0)

    lw = [ctx.dma_stream(f"lw{i}") for i in range(2)]
    tmp32 = [ctx.sbuf(f"tmp32_{i}", [128, 3 * D], F32, side="right") for i in range(2)]
    tmpbf = [ctx.sbuf(f"tmpbf_{i}", [128, 3 * D], BF16, side="right") for i in range(2)]
    pst0 = ctx.psum("pst0", [128, 128], BF16)
    emit_transpose_weight(ctx, wqkv, wqkvT, 3 * D, D, tmp32, tmpbf, pst0, lw)
    for oc in range(ND):
        i = oc % 2
        ctx.dma_op("sync", lw[i],
                   [(tmp32[i].h[:, 0:3 * D], conv_w.h[oc * 128:(oc + 1) * 128, :])],
                   [tmp32[i]], [conv_w])
        v.op("tensor_copy", [tmpbf[i]], [tmp32[i]], tmpbf[i].h[:, 0:3 * D],
             tmp32[i].h[:, 0:3 * D])
        for tap in range(3):
            for ic in range(ND):
                src = tmpbf[i].h[:, ic * 384 + tap: ic * 384 + 384: 3]
                t.op("transpose", [pst0], [tmpbf[i], ident], pst0[:, 0:128],
                     src, ident[:])
                v.op("tensor_copy", [convT], [pst0],
                     convT.h[:, tap * ND * D + ic * D + oc * 128:
                             tap * ND * D + ic * D + (oc + 1) * 128],
                     pst0[:, 0:128])
    lb = ctx.dma_stream("lb")
    for c in range(ND):
        ctx.dma_op("sync", lb, [(colv.h[:, c:c + 1], bqkv.h[c * 128:(c + 1) * 128])],
                   [colv], [bqkv])
        ctx.dma_op("sync", lb, [(colv.h[:, 8 + c:9 + c], bqkv.h[D + c * 128:D + (c + 1) * 128])],
                   [colv], [bqkv])
        ctx.dma_op("sync", lb, [(convb_c.h[:, c:c + 1], conv_b.h[c * 128:(c + 1) * 128])],
                   [convb_c], [conv_b])
    ctx.dma_op("sync", lb, [(rowf.h[0:1, 0:1024], bqkv.h[2 * D:3 * D])], [rowf], [bqkv])
    v.op("tensor_copy", [rows_bf], [rowf], rows_bf.h[0:1, 0:1024], rowf.h[0:1, 0:1024])
    ctx.free(tmpbf[1], tmpbf[0], tmp32[1], tmp32[0])

    yb = ctx.sbuf("yb", [128, ND * T], BF16, side="right")
    out_st = [ctx.sbuf(f"ost{i}", [128, 1040], BF16, side="right") for i in range(2)]
    ldx = ctx.dma_stream("ldx")
    so = [ctx.dma_stream(f"so{i}") for i in range(2)]
    pcv = [ctx.psum(f"pcv{i}", [128, 512], F32) for i in range(2)]
    pTt = ctx.psum("pTt", [128, 128], BF16)
    pv = [ctx.psum(f"pv{i}", [128, 128], F32) for i in range(2)]

    for b in range(B_CORE):
        xf = ctx.sbuf(f"xf{b}", [128, ND * T], BF16, side="right")
        ctx.dma_op("sync", ldx,
                   [(xf.h[:, c * T:(c + 1) * T], xnT.h[b, c * 128:(c + 1) * 128, :])
                    for c in range(ND)], [xf], [xnT])
        for oc in range(ND):
            for ts in range(4):
                p = pcv[(oc * 4 + ts) % 2]
                first = True
                nterm = 0
                ops = []
                for tap in range(3):
                    off = tap - 1
                    lo = max(0, ts * 512 + off)
                    hi = min(T, (ts + 1) * 512 + off)
                    plo = lo - off - ts * 512
                    phi = hi - off - ts * 512
                    for ic in range(ND):
                        ops.append((tap, ic, lo, hi, plo, phi))
                for j, (tap, ic, lo, hi, plo, phi) in enumerate(ops):
                    lT = convT.h[:, tap * ND * D + ic * D + oc * 128:
                                 tap * ND * D + ic * D + (oc + 1) * 128]
                    rT = xf.h[:, ic * T + lo:ic * T + hi]
                    last = (j == len(ops) - 1)
                    first_ = (j == 0)
                    t.op("matmul", [p] if (first_ or last) else [],
                         [convT, xf], p[:, plo:phi], lT, rT,
                         start=first_, stop=last, inc=last)
                v.op("tensor_scalar", [yb], [p, convb_c],
                     yb.h[:, oc * T + ts * 512:oc * T + (ts + 1) * 512],
                     p[:, 0:512], convb_c.h[:, oc:oc + 1], None,
                     op0=ALU.add, op1=ALU.bypass)
        ctx.free(xf)
        for it in range(NT):
            i = it % 2
            for oc in range(ND):
                t.op("transpose", [pTt], [yb, ident], pTt[:, 0:128],
                     yb.h[:, oc * T + it * 128:oc * T + (it + 1) * 128], ident[:])
                v.op("tensor_copy", [out_st[i]], [pTt],
                     out_st[i].h[:, oc * 128:(oc + 1) * 128], pTt[:, 0:128])
            ctx.dma_op("gpsimd", so[i],
                       [(ybT_o.h[b * T + it * 128:b * T + (it + 1) * 128, :],
                         out_st[i][:, 0:1024])], [ybT_o], [out_st[i]])
        for (name, oh, bcol, scale) in (("q", qh_o, 0, 0.125), ("k", kh_o, 8, 1.0)):
            for hp in range(ND):
                outcol = hp * 128 if name == "q" else D + hp * 128
                for ts in range(4):
                    i = ts % 2
                    p = pcv[i]
                    grp = [(wqkvT.h[:, c * 3 * D + outcol:c * 3 * D + outcol + 128],
                            yb.h[:, c * T + ts * 512:c * T + (ts + 1) * 512],
                            [wqkvT, yb]) for c in range(ND)]
                    mm_group(ctx, p, p[:, 0:512], grp)
                    v.op("tensor_scalar", [out_st[i]], [p, colv],
                         out_st[i].h[:, 0:512], p[:, 0:512],
                         colv.h[:, bcol + hp:bcol + hp + 1], scale,
                         op0=ALU.add, op1=ALU.mult)
                    ctx.dma_op("gpsimd", so[i],
                               [(oh.h[b, hp * 128:(hp + 1) * 128, ts * 512:(ts + 1) * 512],
                                 out_st[i][:, 0:512])], [oh], [out_st[i]])
        for it in range(NT):
            i = it % 2
            for hp in range(ND):
                p = pv[i]
                t.op("matmul", [p], [onesb, rows_bf], p[:, 0:128],
                     onesb.h[0:1, 0:128], rows_bf.h[0:1, hp * 128:(hp + 1) * 128],
                     start=True, stop=False, inc=False)
                grp_ins = [yb, wqkvT]
                for c in range(ND):
                    last = (c == ND - 1)
                    t.op("matmul", [p] if last else [], grp_ins, p[:, 0:128],
                         yb.h[:, c * T + it * 128:c * T + (it + 1) * 128],
                         wqkvT.h[:, c * 3 * D + 2 * D + hp * 128:c * 3 * D + 2 * D + (hp + 1) * 128],
                         start=False, stop=last, inc=last)
                v.op("tensor_copy", [out_st[i]], [p],
                     out_st[i].h[:, hp * 130:hp * 130 + 64], p[:, 0:64])
                v.op("tensor_copy", [out_st[i]], [p],
                     out_st[i].h[:, hp * 130 + 65:hp * 130 + 129], p[:, 64:128])
            v.op("memset", [out_st[i]], [], out_st[i].h[:, 64:1040:130], 1.0)
            v.op("memset", [out_st[i]], [], out_st[i].h[:, 129:1040:130], 1.0)
            ctx.dma_op("gpsimd", so[i],
                       [(vp_o.h[b, 0:128, it * 1040:(it + 1) * 1040],
                         out_st[i][:, 0:1040])], [vp_o], [out_st[i]])
    g.raw(lambda eng, s_=so[0].sem, v_=so[0].count: eng.wait_ge(s_, v_))
    g.raw(lambda eng, s_=so[1].sem, v_=so[1].count: eng.wait_ge(s_, v_))


def build_f0conv_launch():
    ctx = Ctx()
    ext = {}
    ext["xnT"] = ctx.dram("xnT", [B_CORE, D, T], BF16, kind="ExternalInput")
    ext["conv_w"] = ctx.dram("conv_w", [D, D * 3], F32, kind="ExternalInput")
    ext["conv_b"] = ctx.dram("conv_b", [D], F32, kind="ExternalInput")
    ext["wqkv"] = ctx.dram("wqkv", [3 * D, D], F32, kind="ExternalInput")
    ext["bqkv"] = ctx.dram("bqkv", [3 * D], F32, kind="ExternalInput")
    ext["qh"] = ctx.dram("qh", [B_CORE, D, T], BF16, kind="ExternalOutput")
    ext["kh"] = ctx.dram("kh", [B_CORE, D, T], BF16, kind="ExternalOutput")
    ext["vp"] = ctx.dram("vp", [B_CORE, 128, NT * 1040], BF16, kind="ExternalOutput")
    ext["ybT"] = ctx.dram("ybT", [B_CORE * T, D], BF16, kind="ExternalOutput")
    emit_f0conv(ctx, ext)
    return ctx.emit()


def emit_f0attn(ctx, ext):
    qh_i = ext["qh"]
    kh_i = ext["kh"]
    vp_i = ext["vp"]
    ybT_i = ext["ybT"]
    wo_h = ext["wo"]
    bo_h = ext["bo"]
    ln1g_h = ext["ln1_g"]
    ln1b_h = ext["ln1_b"]
    xl1T_o = ext["xl1T"]
    xl1_o = ext["xl1"]

    t, v, s, g = ctx.t, ctx.v, ctx.s, ctx.g
    eps_c = ctx.const(1e-5)
    ident = ctx.identity(BF16)

    woT = ctx.sbuf("woT", [128, ND * D], BF16)
    ln1g_c = ctx.sbuf("ln1g_c", [128, ND], F32)
    ln1b_c = ctx.sbuf("ln1b_c", [128, ND], F32)
    rows_bf = ctx.sbuf("rows_bf", [1, 1024], BF16)
    rowf = ctx.sbuf("rowf", [1, 1024], F32)
    onesb = ctx.sbuf("onesb", [1, 128], BF16)
    onescols = ctx.sbuf("onescols", [128, 2], BF16)
    gbc = ctx.sbuf("gbc", [128, D], F32)
    bbc = ctx.sbuf("bbc", [128, D], F32)
    stats = ctx.sbuf("stats", [128, 8], F32)
    statsl = make_stat_slices(ctx, stats, 2)
    g.op("memset", [onesb], [], onesb[:], 1.0)
    g.op("memset", [onescols], [], onescols[:], 0.0)
    g.op("memset", [onescols], [], onescols.h[0:64, 0:1], 1.0)
    g.op("memset", [onescols], [], onescols.h[0:64, 1:2], 1.0)

    lw = [ctx.dma_stream(f"lw{i}") for i in range(2)]
    tmp32 = [ctx.sbuf(f"tmp32_{i}", [128, D], F32, side="right") for i in range(2)]
    tmpbf = [ctx.sbuf(f"tmpbf_{i}", [128, D], BF16, side="right") for i in range(2)]
    pst0 = ctx.psum("pst0", [128, 128], BF16)
    emit_transpose_weight(ctx, wo_h, woT, D, D, tmp32, tmpbf, pst0, lw)
    lb = ctx.dma_stream("lb")
    for c in range(ND):
        ctx.dma_op("sync", lb, [(ln1g_c.h[:, c:c + 1], ln1g_h.h[c * 128:(c + 1) * 128])],
                   [ln1g_c], [ln1g_h])
        ctx.dma_op("sync", lb, [(ln1b_c.h[:, c:c + 1], ln1b_h.h[c * 128:(c + 1) * 128])],
                   [ln1b_c], [ln1b_h])
    ctx.dma_op("sync", lb, [(rowf.h[0:1, 0:1024], bo_h.h[0:D])], [rowf], [bo_h])
    v.op("tensor_copy", [rows_bf], [rowf], rows_bf.h[0:1, 0:1024], rowf.h[0:1, 0:1024])
    rbf2 = ctx.sbuf("rbf2", [1, 2 * D], BF16)
    ctx.dma_op("sync", lb, [(rowf.h[0:1, 0:1024], ln1g_h.h[0:D])], [rowf], [ln1g_h])
    v.op("tensor_copy", [rbf2], [rowf], rbf2.h[0:1, 0:D], rowf.h[0:1, 0:1024])
    ctx.dma_op("sync", lb, [(rowf.h[0:1, 0:1024], ln1b_h.h[0:D])], [rowf], [ln1b_h])
    v.op("tensor_copy", [rbf2], [rowf], rbf2.h[0:1, D:2 * D], rowf.h[0:1, 0:1024])
    pgb = ctx.psum("pgb", [128, 512], F32)
    for (dst, off) in ((gbc, 0), (bbc, D)):
        for half in range(2):
            t.op("matmul", [pgb], [onesb, rbf2], pgb[:, 0:512],
                 onesb.h[0:1, 0:128], rbf2.h[0:1, off + half * 512:off + (half + 1) * 512],
                 start=True, stop=True)
            v.op("tensor_copy", [dst], [pgb], dst.h[:, half * 512:(half + 1) * 512],
                 pgb[:, 0:512])
    ctx.free(tmpbf[1], tmpbf[0], tmp32[1], tmp32[0])
    frQ = ctx.free(pgb, pst0)

    ldb = ctx.dma_stream("ldb")
    ldp = ctx.dma_stream("ldp")
    sto1 = ctx.dma_stream("sto1")
    sto2 = ctx.dma_stream("sto2")

    for b in range(B_CORE):
        ybT = ctx.sbuf(f"ybT{b}", [128, NT * D], BF16, side="right")
        ctxF = ctx.sbuf(f"ctxF{b}", [128, ND * T], BF16, side="right")
        vpa = ctx.sbuf(f"vpa{b}", [128, NT * 1040], BF16, side="right")
        qp = ctx.sbuf(f"qp{b}", [65, 2 * T], BF16, side="right")
        kp = ctx.sbuf(f"kp{b}", [65, 2 * T], BF16, side="right")
        qnr = ctx.sbuf(f"qnr{b}", [1, 2 * T], BF16, side="right")
        knr = ctx.sbuf(f"knr{b}", [1, T], BF16, side="right")
        sqk2 = [ctx.sbuf(f"sqk{b}_{i}", [128, 512], BF16, side="right") for i in range(4)]
        sqk = sqk2[0]
        bcb = ctx.sbuf(f"bcb{b}", [64, 2048], BF16, side="right")
        rr = ctx.sbuf(f"rr{b}", [1, 2048], F32, side="right")
        rrb = ctx.sbuf(f"rrb{b}", [1, 2048], BF16, side="right")
        pending = []
        km2 = ctx.sbuf(f"km2{b}", [1, 4], F32, side="right")
        ctx.dma_op("sync", ldb,
                   [(ybT.h[:, it * D:(it + 1) * D],
                     ybT_i.h[b * T + it * 128:b * T + (it + 1) * 128, :])
                    for it in range(NT)], [ybT], [ybT_i])
        ctx.dma_op("sync", ldb,
                   [(vpa.h[:, it * 1040:(it + 1) * 1040],
                     vp_i.h[b, 0:128, it * 1040:(it + 1) * 1040])
                    for it in range(NT)], [vpa], [vp_i])
        g.op("memset", [kp], [], kp.h[64:65, :], 1.0)
        pe_ = [ctx.psum(f"pe{b}_{i}", [128, 512], F32, after=frQ) for i in range(2)]
        pcx = [ctx.psum(f"pcx{b}_{h}", [65, 1024], F32) for h in range(2)]
        paux2 = [ctx.psum(f"paux{b}_{i}", [64, 512], F32) for i in range(2)]
        for hp in range(ND):
            ctx.dma_op("sync", ldp,
                       [(qp.h[0:64, 0:T], qh_i.h[b, hp * 128:hp * 128 + 64, :]),
                        (qp.h[0:64, T:2 * T], qh_i.h[b, hp * 128 + 64:hp * 128 + 128, :])],
                       [qp], [qh_i])
            ctx.dma_op("sync", ldp,
                       [(kp.h[0:64, 0:T], kh_i.h[b, hp * 128:hp * 128 + 64, :]),
                        (kp.h[0:64, T:2 * T], kh_i.h[b, hp * 128 + 64:hp * 128 + 128, :])],
                       [kp], [kh_i])
            for (src, dstrow, dcols) in ((qp, qnr, 2), (kp, knr, 1)):
                for h in range(2):
                    for ts in range(4):
                        sq = sqk2[ts % 2]
                        px = paux2[ts % 2]
                        s.op("activation", [sq], [src], sq.h[0:64, 0:512],
                             src.h[0:64, h * T + ts * 512:h * T + (ts + 1) * 512],
                             AF.Square)
                        t.op("matmul", [px], [sq, onescols], px[0:1, 0:512],
                             onescols.h[0:64, h:h + 1], sq.h[0:64, 0:512],
                             start=True, stop=True)
                        if dcols == 2:
                            s.op("activation", [dstrow], [px],
                                 dstrow.h[0:1, h * T + ts * 512:h * T + (ts + 1) * 512],
                                 px[0:1, 0:512], AF.Sqrt)
                        else:
                            v.op("tensor_copy", [dstrow], [px],
                                 dstrow.h[0:1, ts * 512:(ts + 1) * 512],
                                 px[0:1, 0:512])
                    if dcols == 1:
                        v.op("tensor_reduce", [km2], [dstrow], km2.h[0:1, h:h + 1],
                             dstrow.h[0:1, 0:T], axis=AX.X, op=ALU.max)
                        s.op("activation", [km2], [km2], km2.h[0:1, h:h + 1],
                             km2.h[0:1, h:h + 1], AF.Sqrt)
                        v.op("tensor_scalar", [qp], [qnr, km2],
                             qp.h[64:65, h * T:(h + 1) * T],
                             qnr.h[0:1, h * T:(h + 1) * T],
                             km2.h[0:1, h:h + 1], -1.0,
                             op0=ALU.mult, op1=ALU.mult)
            for ts in range(4):
                def ctx_mm(h, kc):
                    sq = sqk2[2 * h + kc % 2]
                    t.op("matmul", [pcx[h]] if kc == NT - 1 else [],
                         [vpa, sq], pcx[h][0:65, (ts % 2) * 512:(ts % 2) * 512 + 512],
                         vpa.h[:, kc * 1040 + (hp * 2 + h) * 65:kc * 1040 + (hp * 2 + h) * 65 + 65],
                         sq.h[:, 0:512],
                         start=(kc == 0), stop=(kc == NT - 1),
                         inc=(kc == NT - 1))
                for kc in range(NT):
                    for h in range(2):
                        p = pe_[h]
                        sq = sqk2[2 * h + kc % 2]
                        t.op("matmul", [p], [kp, qp], p[:, 0:512],
                             kp.h[0:65, h * T + kc * 128:h * T + (kc + 1) * 128],
                             qp.h[0:65, h * T + ts * 512:h * T + (ts + 1) * 512],
                             start=True, stop=True)
                        s.op("activation", [sq], [p], sq.h[:, 0:512], p[:, 0:512], AF.Exp)
                        if kc >= 1:
                            ctx_mm(h, kc - 1)
                for h in range(2):
                    ctx_mm(h, NT - 1)
                for h in range(2):
                    hb = 64 * h
                    j4 = (ts % 2) * 2 + h
                    if len(pending) >= 2:
                        pending.pop(0)()
                    v.op("reciprocal", [rr], [pcx[h]], rr.h[0:1, j4 * 512:j4 * 512 + 512],
                         pcx[h][64:65, (ts % 2) * 512:(ts % 2) * 512 + 512])
                    v.op("tensor_copy", [rrb], [rr],
                         rrb.h[0:1, j4 * 512:j4 * 512 + 512],
                         rr.h[0:1, j4 * 512:j4 * 512 + 512])

                    def fin(h=h, ts=ts, hp=hp, hb=hb, j4=j4):
                        px = paux2[h]
                        t.op("matmul", [px], [onesb, rrb], px[0:64, 0:512],
                             onesb.h[0:1, 0:64],
                             rrb.h[0:1, j4 * 512:j4 * 512 + 512],
                             start=True, stop=True)
                        v.op("tensor_copy", [bcb], [px],
                             bcb.h[0:64, j4 * 512:j4 * 512 + 512],
                             px[0:64, 0:512])
                        v.op("tensor_tensor", [ctxF], [pcx[h], bcb],
                             ctxF.h[hb:hb + 64, hp * T + ts * 512:hp * T + (ts + 1) * 512],
                             pcx[h][0:64, (ts % 2) * 512:(ts % 2) * 512 + 512],
                             bcb.h[0:64, j4 * 512:j4 * 512 + 512],
                             op=ALU.mult)
                    pending.append(fin)
        for fn_ in pending:
            fn_()
        pending = []
        frA = ctx.free(km2, rrb, rr, bcb, sqk2[3], sqk2[2], sqk2[1], sqk2[0],
                       knr, qnr, kp, qp, vpa)
        frP = ctx.free(paux2[1], paux2[0], pcx[1], pcx[0], pe_[1], pe_[0])
        fmacc = ctx.sbuf(f"fmacc{b}", [128, ND * T], BF16, side="right")
        zt = [ctx.sbuf(f"zt{b}_{i}", [128, D], F32, side="right") for i in range(2)]
        xhb = [ctx.sbuf(f"xhb{b}_{i}", [128, D], BF16, side="right") for i in range(2)]
        xtm = [ctx.sbuf(f"xtm{b}_{i}", [128, D], F32, side="right") for i in range(2)]
        pz = [ctx.psum(f"pz{b}_{i}", [128, D], F32, after=frP) for i in range(2)]
        pTb = ctx.psum(f"pTb{b}", [128, 128], BF16)
        for it in range(NT):
            i = it % 2
            p = pz[i]
            for dh in range(2):
                t.op("matmul", [p], [onesb, rows_bf], p[:, dh * 512:(dh + 1) * 512],
                     onesb.h[0:1, 0:128], rows_bf.h[0:1, dh * 512:(dh + 1) * 512],
                     start=True, stop=False, inc=False)
                grp = [(ctxF.h[:, c * T + it * 128:c * T + (it + 1) * 128],
                        woT.h[:, c * D + dh * 512:c * D + (dh + 1) * 512],
                        [ctxF, woT]) for c in range(ND)]
                for j, (l, r, ins) in enumerate(grp):
                    last = (j == len(grp) - 1)
                    t.op("matmul", [p] if last else [], ins,
                         p[:, dh * 512:(dh + 1) * 512], l, r,
                         start=False, stop=last, inc=last)
            v.op("tensor_tensor", [zt[i]], [p, ybT], zt[i][:], p[:, 0:D],
                 ybT.h[:, it * D:(it + 1) * D], op=ALU.add)
            nmu, rstd = emit_ln_stats(ctx, zt[i], zt[i][:], statsl[i], eps_c, xtm[i])
            v.op("tensor_scalar", [xhb[i]], [zt[i], nmu, rstd], xhb[i][:],
                 zt[i][:], nmu.h, rstd.h, op0=ALU.add, op1=ALU.mult)
            v.op("scalar_tensor_tensor", [xtm[i]], [xhb[i], gbc], xtm[i][:],
                 gbc[:], 1.0, xhb[i][:], op0=ALU.mult, op1=ALU.mult)
            v.op("tensor_tensor", [xtm[i]], [xtm[i], bbc], xtm[i][:], xtm[i][:],
                 bbc[:], op=ALU.add)
            ctx.dma_op("gpsimd", sto1,
                       [(xl1_o.h[b * T + it * 128:b * T + (it + 1) * 128, :],
                         xtm[i][:])], [xl1_o], [xtm[i]])
            for c in range(ND):
                t.op("transpose", [pTb], [xhb[i], ident], pTb[:, 0:128],
                     xhb[i].h[:, c * 128:(c + 1) * 128], ident[:])
                v.op("tensor_scalar", [fmacc], [pTb, ln1g_c, ln1b_c],
                     fmacc.h[:, c * T + it * 128:c * T + (it + 1) * 128],
                     pTb[:, 0:128], ln1g_c.h[:, c:c + 1], ln1b_c.h[:, c:c + 1],
                     op0=ALU.mult, op1=ALU.add)
        ctx.dma_op("gpsimd", sto2,
                   [(xl1T_o.h[b, c * 128:(c + 1) * 128, :],
                     fmacc.h[:, c * T:(c + 1) * T]) for c in range(ND)],
                   [xl1T_o], [fmacc])
        frQ = ctx.free(pTb, pz[1], pz[0])
        ctx.free(xtm[1], xtm[0], xhb[1], xhb[0], zt[1], zt[0], fmacc)
        ctx.free(ctxF, ybT)
    g.raw(lambda eng, s_=sto1.sem, v_=sto1.count: eng.wait_ge(s_, v_))
    g.raw(lambda eng, s_=sto2.sem, v_=sto2.count: eng.wait_ge(s_, v_))


def build_f0attn_launch():
    ctx = Ctx()
    ext = {}
    ext["qh"] = ctx.dram("qh", [B_CORE, D, T], BF16, kind="ExternalInput")
    ext["kh"] = ctx.dram("kh", [B_CORE, D, T], BF16, kind="ExternalInput")
    ext["vp"] = ctx.dram("vp", [B_CORE, 128, NT * 1040], BF16, kind="ExternalInput")
    ext["ybT"] = ctx.dram("ybT", [B_CORE * T, D], BF16, kind="ExternalInput")
    ext["wo"] = ctx.dram("wo", [D, D], F32, kind="ExternalInput")
    ext["bo"] = ctx.dram("bo", [D], F32, kind="ExternalInput")
    ext["ln1_g"] = ctx.dram("ln1_g", [D], F32, kind="ExternalInput")
    ext["ln1_b"] = ctx.dram("ln1_b", [D], F32, kind="ExternalInput")
    ext["xl1T"] = ctx.dram("xl1T", [B_CORE, D, T], BF16, kind="ExternalOutput")
    ext["xl1"] = ctx.dram("xl1", [B_CORE * T, D], F32, kind="ExternalOutput")
    emit_f0attn(ctx, ext)
    return ctx.emit()


DFF = 4096
NEXP = 8
MFB = 512
NMFB = DFF // MFB


def emit_moe(ctx, ext):
    xl1T_i = ext["xl1T"]
    xl1_i = ext["xl1"]
    gw_h = ext["gate_w"]
    gb_h = ext["gate_b"]
    w1_h = ext["w1"]
    b1_h = ext["b1"]
    w2_h = ext["w2"]
    b2_h = ext["b2"]
    ln2g_h = ext["ln2_g"]
    ln2b_h = ext["ln2_b"]
    out_o = ext["f0out"]

    t, v, s, g = ctx.t, ctx.v, ctx.s, ctx.g
    eps_c = ctx.const(1e-5)

    onesb = ctx.sbuf("onesb", [1, 512], BF16)
    g.op("memset", [onesb], [], onesb[:], 1.0)
    gwb = ctx.sbuf("gwb", [128, ND * NEXP], BF16)
    gbrow = ctx.sbuf("gbrow", [1, NEXP], BF16)
    rowf = ctx.sbuf("rowf", [1, D], F32)
    b1c = ctx.sbuf("b1c", [128, NEXP * 32], F32)
    b2row = ctx.sbuf("b2row", [1, D], BF16)
    gbc = ctx.sbuf("gbc", [128, D], F32)
    bbc = ctx.sbuf("bbc", [128, D], F32)
    stats = ctx.sbuf("stats", [128, 8], F32)
    statsl = make_stat_slices(ctx, stats, 2)
    lb = ctx.dma_stream("lb")
    for c in range(ND):
        ctx.dma_op("sync", lb, [(gbc.h[:, 0:NEXP], gw_h.h[c * 128:(c + 1) * 128, :])],
                   [gbc], [gw_h])
        v.op("tensor_copy", [gwb], [gbc], gwb.h[:, c * NEXP:(c + 1) * NEXP],
             gbc.h[:, 0:NEXP])
    ctx.dma_op("sync", lb, [(rowf.h[0:1, 0:NEXP], gb_h.h[0:NEXP])], [rowf], [gb_h])
    v.op("tensor_copy", [gbrow], [rowf], gbrow.h[0:1, 0:NEXP], rowf.h[0:1, 0:NEXP])
    pgb = ctx.psum("pgb", [128, 512], F32)
    rbf2 = ctx.sbuf("rbf2", [1, 2 * D], BF16)
    ctx.dma_op("sync", lb, [(rowf.h[0:1, 0:1024], ln2g_h.h[0:D])], [rowf], [ln2g_h])
    v.op("tensor_copy", [rbf2], [rowf], rbf2.h[0:1, 0:D], rowf.h[0:1, 0:1024])
    ctx.dma_op("sync", lb, [(rowf.h[0:1, 0:1024], ln2b_h.h[0:D])], [rowf], [ln2b_h])
    v.op("tensor_copy", [rbf2], [rowf], rbf2.h[0:1, D:2 * D], rowf.h[0:1, 0:1024])
    for (dst, off) in ((gbc, 0), (bbc, D)):
        for half in range(2):
            t.op("matmul", [pgb], [onesb, rbf2], pgb[:, 0:512],
                 onesb.h[0:1, 0:128], rbf2.h[0:1, off + half * 512:off + (half + 1) * 512],
                 start=True, stop=True)
            v.op("tensor_copy", [dst], [pgb], dst.h[:, half * 512:(half + 1) * 512],
                 pgb[:, 0:512])
    frQ = ctx.free(pgb)
    ctx.free(rbf2)

    xf = ctx.sbuf("xf", [128, ND * T], BF16)
    probs = ctx.sbuf("probs", [128, NT * NEXP], F32)
    pexp = ctx.sbuf("pexp", [128, NEXP], F32)
    w1b = [ctx.sbuf(f"w1b{i}", [128, ND * MFB], BF16) for i in range(2)]
    w2b = [ctx.sbuf(f"w2b{i}", [128, (MFB // 128) * D], BF16) for i in range(2)]
    w1r = [ctx.sbuf(f"w1r{i}", [128, MFB], F32) for i in range(2)]
    w2r = [ctx.sbuf(f"w2r{i}", [128, D], F32) for i in range(2)]
    hb2 = [ctx.sbuf(f"hb{i}", [128, (MFB // 128) * T], BF16) for i in range(2)]
    moe = ctx.sbuf("moe", [128, NT * D], F32)
    xztile = [ctx.sbuf(f"xz{i}", [128, D], F32) for i in range(2)]
    ores = [ctx.sbuf(f"ores{i}", [128, D], F32) for i in range(2)]

    ldx = ctx.dma_stream("ldx")
    ldw1 = [ctx.dma_stream(f"ldw1_{i}") for i in range(2)]
    ldw2 = [ctx.dma_stream(f"ldw2_{i}") for i in range(2)]
    ldb1 = ctx.dma_stream("ldb1")
    ldb1b = ctx.dma_stream("ldb1b")
    ldz = [ctx.dma_stream(f"ldz{i}") for i in range(2)]
    stw = ctx.dma_stream("stw")

    for e_ in range(NEXP):
        for fc_ in range(32):
            ctx.dma_op("sync", ldb1 if fc_ % 2 == 0 else ldb1b,
                       [(b1c.h[:, e_ * 32 + fc_:e_ * 32 + fc_ + 1],
                         b1_h.h[e_, fc_ * 128:(fc_ + 1) * 128])],
                       [b1c], [b1_h])
    ph = [ctx.psum(f"ph{i}", [128, 512], F32, after=frQ) for i in range(2)]
    py = [ctx.psum(f"py{i}", [128, D], F32) for i in range(2)]
    pp = ctx.psum("pp", [128, NEXP], F32)

    for b in range(B_CORE):
        ctx.dma_op("sync", ldx,
                   [(xf.h[:, c * T:(c + 1) * T], xl1T_i.h[b, c * 128:(c + 1) * 128, :])
                    for c in range(ND)], [xf], [xl1T_i])
        for it in range(NT):
            t.op("matmul", [pp], [onesb, gbrow], pp[:, 0:NEXP],
                 onesb.h[0:1, 0:128], gbrow.h[0:1, 0:NEXP],
                 start=True, stop=False, inc=False)
            for c in range(ND):
                last = (c == ND - 1)
                t.op("matmul", [pp] if last else [], [xf, gwb], pp[:, 0:NEXP],
                     xf.h[:, c * T + it * 128:c * T + (it + 1) * 128],
                     gwb.h[:, c * NEXP:(c + 1) * NEXP],
                     start=False, stop=last, inc=last)
            sl = statsl[it % 2]
            nmx, sm = sl[0], sl[1]
            v.op("tensor_reduce", [nmx], [pp], nmx.h, pp[:, 0:NEXP], axis=AX.X,
                 op=ALU.max, negate=True)
            s.op("activation", [pexp, sm], [pp, nmx], pexp[:], pp[:, 0:NEXP],
                 AF.Exp, bias=nmx.h, accum_out=sm.h)
            v.op("reciprocal", [sm], [sm], sm.h, sm.h)
            v.op("tensor_scalar", [probs], [pexp, sm],
                 probs.h[:, it * NEXP:(it + 1) * NEXP], pexp[:], sm.h, None,
                 op0=ALU.mult, op1=ALU.bypass)
        for e in range(NEXP):
            ctx.dma_op("sync", ldb1, [(rowf.h[0:1, 0:D], b2_h.h[e, 0:D])],
                       [rowf], [b2_h])
            v.op("tensor_copy", [b2row], [rowf], b2row.h[0:1, 0:D],
                 rowf.h[0:1, 0:D])
            for fb in range(NMFB):
                i = fb % 2
                f0 = fb * MFB
                for c in range(ND):
                    j = c % 2
                    ctx.dma_op("sync", ldw1[j],
                               [(w1r[j][:], w1_h.h[e * D + c * 128:e * D + (c + 1) * 128,
                                                   f0:f0 + MFB])],
                               [w1r[j]], [w1_h])
                    g.op("tensor_copy", [w1b[i]], [w1r[j]],
                         w1b[i].h[:, c * MFB:(c + 1) * MFB], w1r[j][:])
                for fc in range(MFB // 128):
                    j = fc % 2
                    ctx.dma_op("scalar", ldw2[j],
                               [(w2r[j][:], w2_h.h[e * DFF + f0 + fc * 128:
                                                   e * DFF + f0 + (fc + 1) * 128, :])],
                               [w2r[j]], [w2_h])
                    g.op("tensor_copy", [w2b[i]], [w2r[j]],
                         w2b[i].h[:, fc * D:(fc + 1) * D], w2r[j][:])
                hb = hb2[i]
                for fc in range(MFB // 128):
                    for ts in range(4):
                        p = ph[(fc * 4 + ts) % 2]
                        for c in range(ND):
                            last = (c == ND - 1)
                            t.op("matmul", [p] if (last or c == 0) else [],
                                 [w1b[i], xf], p[:, 0:512],
                                 w1b[i].h[:, c * MFB + fc * 128:c * MFB + (fc + 1) * 128],
                                 xf.h[:, c * T + ts * 512:c * T + (ts + 1) * 512],
                                 start=(c == 0), stop=last, inc=last)
                        s.op("activation", [hb], [p, b1c],
                             hb.h[:, fc * T + ts * 512:fc * T + (ts + 1) * 512],
                             p[:, 0:512], AF.Relu,
                             bias=b1c.h[:, e * 32 + f0 // 128 + fc:e * 32 + f0 // 128 + fc + 1])
                for it in range(NT):
                    p = py[it % 2]
                    for dh in range(2):
                        terms = []
                        if fb == 0:
                            terms.append(("bias", None))
                        for fc in range(MFB // 128):
                            terms.append(("mm", fc))
                        for j, (kind, fc) in enumerate(terms):
                            fst = (j == 0)
                            last = (j == len(terms) - 1)
                            outs = [p] if (fst or last) else []
                            if kind == "bias":
                                t.op("matmul", outs, [onesb, b2row],
                                     p[:, dh * 512:(dh + 1) * 512],
                                     onesb.h[0:1, 0:128],
                                     b2row.h[0:1, dh * 512:(dh + 1) * 512],
                                     start=True, stop=False, inc=False)
                            else:
                                t.op("matmul", outs, [hb, w2b[i]],
                                     p[:, dh * 512:(dh + 1) * 512],
                                     hb.h[:, fc * T + it * 128:fc * T + (it + 1) * 128],
                                     w2b[i].h[:, fc * D + dh * 512:fc * D + (dh + 1) * 512],
                                     start=fst, stop=last, inc=last)
                    if e == 0 and fb == 0:
                        v.op("tensor_scalar", [moe], [py[it % 2], probs],
                             moe.h[:, it * D:(it + 1) * D], p[:, 0:D],
                             probs.h[:, it * NEXP + e:it * NEXP + e + 1], None,
                             op0=ALU.mult, op1=ALU.bypass)
                    else:
                        v.op("scalar_tensor_tensor", [moe], [py[it % 2], probs],
                             moe.h[:, it * D:(it + 1) * D], p[:, 0:D],
                             probs.h[:, it * NEXP + e:it * NEXP + e + 1],
                             moe.h[:, it * D:(it + 1) * D],
                             op0=ALU.mult, op1=ALU.add)
        for it in range(NT):
            i = it % 2
            ctx.dma_op("sync", ldz[i],
                       [(xztile[i][:], xl1_i.h[b * T + it * 128:b * T + (it + 1) * 128, :])],
                       [xztile[i]], [xl1_i])
            v.op("tensor_tensor", [xztile[i]], [moe], xztile[i][:], xztile[i][:],
                 moe.h[:, it * D:(it + 1) * D], op=ALU.add)
            sl = statsl[i]
            nmu, rstd = emit_ln_stats(ctx, xztile[i], xztile[i][:], sl, eps_c,
                                      ores[i])
            v.op("tensor_scalar", [ores[i]], [xztile[i], nmu, rstd], ores[i][:],
                 xztile[i][:], nmu.h, rstd.h, op0=ALU.add, op1=ALU.mult)
            v.op("scalar_tensor_tensor", [ores[i]], [gbc], ores[i][:],
                 gbc[:], 1.0, ores[i][:], op0=ALU.mult, op1=ALU.mult)
            v.op("tensor_tensor", [ores[i]], [ores[i], bbc], ores[i][:],
                 ores[i][:], bbc[:], op=ALU.add)
            ctx.dma_op("gpsimd", stw,
                       [(out_o.h[b * T + it * 128:b * T + (it + 1) * 128, :],
                         ores[i][:])], [out_o], [ores[i]])
    g.raw(lambda eng, s_=stw.sem, v_=stw.count: eng.wait_ge(s_, v_))


def build_moe_launch():
    ctx = Ctx()
    ext = {}
    ext["xl1T"] = ctx.dram("xl1T", [B_CORE, D, T], BF16, kind="ExternalInput")
    ext["xl1"] = ctx.dram("xl1", [B_CORE * T, D], F32, kind="ExternalInput")
    ext["gate_w"] = ctx.dram("gate_w", [D, NEXP], F32, kind="ExternalInput")
    ext["gate_b"] = ctx.dram("gate_b", [NEXP], F32, kind="ExternalInput")
    ext["w1"] = ctx.dram("w1", [NEXP * D, DFF], F32, kind="ExternalInput")
    ext["b1"] = ctx.dram("b1", [NEXP, DFF], F32, kind="ExternalInput")
    ext["w2"] = ctx.dram("w2", [NEXP * DFF, D], F32, kind="ExternalInput")
    ext["b2"] = ctx.dram("b2", [NEXP, D], F32, kind="ExternalInput")
    ext["ln2_g"] = ctx.dram("ln2_g", [D], F32, kind="ExternalInput")
    ext["ln2_b"] = ctx.dram("ln2_b", [D], F32, kind="ExternalInput")
    ext["f0out"] = ctx.dram("f0out", [B_CORE * T, D], F32, kind="ExternalOutput")
    emit_moe(ctx, ext)
    return ctx.emit()


def emit_sum_ln(ctx, ext):
    x0 = ext["x0"]
    a0 = ext["a0"]
    f0 = ext["f0"]
    x1_o = ext["x1"]
    xn1_o = ext["xn1"]
    xn1T_o = ext["xn1T"]
    t, v, s, g = ctx.t, ctx.v, ctx.s, ctx.g
    eps_c = ctx.const(1e-5)
    ident = ctx.identity(BF16)
    xa = [ctx.sbuf(f"xa{i}", [128, D], F32) for i in range(2)]
    xb_ = [ctx.sbuf(f"xb{i}", [128, D], F32) for i in range(2)]
    xc_ = [ctx.sbuf(f"xc{i}", [128, D], F32) for i in range(2)]
    xh = [ctx.sbuf(f"xh{i}", [128, D], F32) for i in range(2)]
    xhb = [ctx.sbuf(f"xhb{i}", [128, D], BF16) for i in range(2)]
    scratch = [ctx.sbuf(f"scr{i}", [128, D], F32) for i in range(2)]
    stats = ctx.sbuf("stats", [128, 8], F32)
    statsl = make_stat_slices(ctx, stats, 2)
    fmacc = ctx.sbuf("fmacc", [128, ND * T], BF16)
    pst = [ctx.psum(f"pst{i}", [128, 128], BF16) for i in range(2)]
    lda = [ctx.dma_stream(f"lda{i}") for i in range(2)]
    ldb = [ctx.dma_stream(f"ldb{i}") for i in range(2)]
    ldc = [ctx.dma_stream(f"ldc{i}") for i in range(2)]
    so1 = ctx.dma_stream("so1")
    so2 = ctx.dma_stream("so2")
    so3 = ctx.dma_stream("so3")
    for b in range(B_CORE):
        for it in range(NT):
            i = it % 2
            r0 = b * T + it * 128
            ctx.dma_op("sync", lda[i], [(xa[i][:], x0.h[r0:r0 + 128, :])],
                       [xa[i]], [x0])
            ctx.dma_op("scalar", ldb[i], [(xb_[i][:], a0.h[r0:r0 + 128, :])],
                       [xb_[i]], [a0])
            ctx.dma_op("gpsimd", ldc[i], [(xc_[i][:], f0.h[r0:r0 + 128, :])],
                       [xc_[i]], [f0])
            v.op("tensor_tensor", [xa[i]], [xb_[i]], xa[i][:], xa[i][:],
                 xb_[i][:], op=ALU.add)
            v.op("tensor_tensor", [xa[i]], [xc_[i]], xa[i][:], xa[i][:],
                 xc_[i][:], op=ALU.add)
            ctx.dma_op("gpsimd", so1, [(x1_o.h[r0:r0 + 128, :], xa[i][:])],
                       [x1_o], [xa[i]])
            nmu, rstd = emit_ln_stats(ctx, xa[i], xa[i][:], statsl[i], eps_c,
                                      scratch[i])
            v.op("tensor_scalar", [xh[i]], [xa[i], nmu, rstd], xh[i][:],
                 xa[i][:], nmu.h, rstd.h, op0=ALU.add, op1=ALU.mult)
            ctx.dma_op("gpsimd", so2, [(xn1_o.h[r0:r0 + 128, :], xh[i][:])],
                       [xn1_o], [xh[i]])
            v.op("tensor_copy", [xhb[i]], [xh[i]], xhb[i][:], xh[i][:])
            for c in range(ND):
                p = pst[(it * ND + c) % 2]
                t.op("transpose", [p], [xhb[i], ident], p[:],
                     xhb[i].h[:, c * 128:(c + 1) * 128], ident[:])
                v.op("tensor_copy", [fmacc], [p],
                     fmacc.h[:, c * T + it * 128:c * T + (it + 1) * 128], p[:])
        ctx.dma_op("gpsimd", so3,
                   [(xn1T_o.h[b, c * 128:(c + 1) * 128, :],
                     fmacc.h[:, c * T:(c + 1) * T]) for c in range(ND)],
                   [xn1T_o], [fmacc])
    for st_ in (so1, so2, so3):
        g.raw(lambda eng, s_=st_.sem, v_=st_.count: eng.wait_ge(s_, v_))


def build_sum_ln_launch():
    ctx = Ctx()
    ext = {}
    ext["x0"] = ctx.dram("x0", [B_CORE * T, D], F32, kind="ExternalInput")
    ext["a0"] = ctx.dram("a0", [B_CORE * T, D], F32, kind="ExternalInput")
    ext["f0"] = ctx.dram("f0", [B_CORE * T, D], F32, kind="ExternalInput")
    ext["x1"] = ctx.dram("x1", [B_CORE * T, D], F32, kind="ExternalOutput")
    ext["xn1"] = ctx.dram("xn1", [B_CORE * T, D], F32, kind="ExternalOutput")
    ext["xn1T"] = ctx.dram("xn1T", [B_CORE, D, T], BF16, kind="ExternalOutput")
    emit_sum_ln(ctx, ext)
    return ctx.emit()

def emit_final_sum(ctx, ext):
    x1 = ext["x1"]
    a1 = ext["a1"]
    f1 = ext["f1"]
    out = ext["out"]
    v, g = ctx.v, ctx.g
    xa = [ctx.sbuf(f"xa{i}", [128, D], F32) for i in range(2)]
    xb_ = [ctx.sbuf(f"xb{i}", [128, D], F32) for i in range(2)]
    lda = [ctx.dma_stream(f"lda{i}") for i in range(2)]
    ldb = [ctx.dma_stream(f"ldb{i}") for i in range(2)]
    so = ctx.dma_stream("so")
    for r in range(B_CORE * NT):
        i = r % 2
        r0 = r * 128
        ctx.dma_op("sync", lda[i], [(xa[i][:], x1.h[r0:r0 + 128, :])], [xa[i]], [x1])
        ctx.dma_op("sync", ldb[i], [(xb_[i][:], a1.h[r0:r0 + 128, :])], [xb_[i]], [a1])
        v.op("tensor_tensor", [xa[i]], [xb_[i]], xa[i][:], xa[i][:], xb_[i][:],
             op=ALU.add)
        ctx.dma_op("sync", ldb[i], [(xb_[i][:], f1.h[r0:r0 + 128, :])], [xb_[i]], [f1])
        v.op("tensor_tensor", [xa[i]], [xb_[i]], xa[i][:], xa[i][:], xb_[i][:],
             op=ALU.add)
        ctx.dma_op("gpsimd", so, [(out.h[r0:r0 + 128, :], xa[i][:])], [out], [xa[i]])
    g.raw(lambda eng, s_=so.sem, v_=so.count: eng.wait_ge(s_, v_))


def build_final_sum_launch():
    ctx = Ctx()
    ext = {}
    ext["x1"] = ctx.dram("x1", [B_CORE * T, D], F32, kind="ExternalInput")
    ext["a1"] = ctx.dram("a1", [B_CORE * T, D], F32, kind="ExternalInput")
    ext["f1"] = ctx.dram("f1", [B_CORE * T, D], F32, kind="ExternalInput")
    ext["out"] = ctx.dram("out", [B_CORE * T, D], F32, kind="ExternalOutput")
    emit_final_sum(ctx, ext)
    return ctx.emit()


ECH = 64
NECH = T // ECH
MLPD = 4096
EFB = 512
NEFB = MLPD // EFB

TWO_PI = 2.0 * np.pi


def emit_a1(ctx, ext):
    xn1_i = ext["xn1"]
    win_h = ext["w_in"]
    wout_h = ext["w_out"]
    wg_h = ext["w_gate"]
    bg_h = ext["b_gate"]
    ld_h = ext["log_decay"]
    fq_h = ext["frequency"]
    cp_h = ext["coupling"]
    wm1_h = ext["w_mlp1"]
    bm1_h = ext["b_mlp1"]
    wm2_h = ext["w_mlp2"]
    bm2_h = ext["b_mlp2"]
    l1g_h = ext["ln1_g"]
    l1b_h = ext["ln1_b"]
    l2g_h = ext["ln2_g"]
    l2b_h = ext["ln2_b"]
    out_o = ext["a1out"]
    padscr = ctx.dram("padscr", [3 * 64, 127], BF16)
    cpscr = ctx.dram("cpscr", [16, 64], BF16)
    scr1 = ctx.dram("scr1", [D, MLPD], BF16)
    scr2 = ctx.dram("scr2", [MLPD, D], BF16)

    t, v, s, g = ctx.t, ctx.v, ctx.s, ctx.g
    eps_c = ctx.const(1e-5)
    npi_c = ctx.const(-np.pi)
    ident = ctx.identity(BF16)

    winT = ctx.sbuf("winT", [128, ND * 128], BF16)
    woutT = ctx.sbuf("woutT", [128, D], BF16)
    wgT = ctx.sbuf("wgT", [128, ND * D], BF16)
    Lvr = ctx.sbuf("Lvr", [128, 64 * 64], BF16)
    Lvi = ctx.sbuf("Lvi", [128, 64 * 64], BF16)
    Lvn = ctx.sbuf("Lvn", [128, 64 * 64], BF16)
    fcR = ctx.sbuf("fcR", [64, 64], BF16)
    fcI = ctx.sbuf("fcI", [64, 64], BF16)
    fcN = ctx.sbuf("fcN", [64, 64], BF16)
    BD = ctx.sbuf("BD", [64, 64], BF16)
    colv = ctx.sbuf("colv", [128, 5 * ND + 32], F32)
    bm2row = ctx.sbuf("bm2row", [1, D], BF16)
    onesb = ctx.sbuf("onesb", [1, 512], BF16)
    rowf = ctx.sbuf("rowf", [1, 1024], F32)
    stats = ctx.sbuf("stats", [128, 8], F32)
    statsl = make_stat_slices(ctx, stats, 2)
    g.op("memset", [onesb], [], onesb[:], 1.0)

    lw = [ctx.dma_stream(f"lw{i}") for i in range(2)]
    lb = ctx.dma_stream("lb")
    tmp32 = [ctx.sbuf(f"tmp32_{i}", [128, MLPD], F32, side="right") for i in range(2)]
    tmpbf = [ctx.sbuf(f"tmpbf_{i}", [128, MLPD], BF16, side="right") for i in range(2)]
    pst0 = ctx.psum("pst0", [128, 128], BF16)
    ctx.dma_op("sync", lw[0], [(tmp32[0].h[:, 0:D], win_h.h[0:128, :])],
               [tmp32[0]], [win_h])
    v.op("tensor_copy", [tmpbf[0]], [tmp32[0]], tmpbf[0].h[:, 0:D],
         tmp32[0].h[:, 0:D])
    for c in range(ND):
        t.op("transpose", [pst0], [tmpbf[0], ident], pst0[:, 0:128],
             tmpbf[0].h[:, c * 128:(c + 1) * 128], ident[:])
        v.op("tensor_copy", [winT], [pst0], winT.h[:, c * 128:(c + 1) * 128],
             pst0[:, 0:128])
    for ro in range(ND):
        i = ro % 2
        ctx.dma_op("sync", lw[i], [(tmp32[i].h[:, 0:128],
                                    wout_h.h[ro * 128:(ro + 1) * 128, :])],
                   [tmp32[i]], [wout_h])
        v.op("tensor_copy", [tmpbf[i]], [tmp32[i]], tmpbf[i].h[:, 0:128],
             tmp32[i].h[:, 0:128])
        t.op("transpose", [pst0], [tmpbf[i], ident], pst0[:, 0:128],
             tmpbf[i].h[:, 0:128], ident[:])
        v.op("tensor_copy", [woutT], [pst0], woutT.h[:, ro * 128:(ro + 1) * 128],
             pst0[:, 0:128])
    emit_transpose_weight(ctx, wg_h, wgT, D, D, tmp32, tmpbf, pst0, lw)
    stg2 = [ctx.sbuf(f"stg{i}", [128, 1024], BF16, side="right") for i in range(2)]
    pstg = [pst0, ctx.psum("pst1", [128, 128], BF16)]
    emit_cast_store_T(ctx, wm1_h, scr1, MLPD, D, tmp32, tmpbf, pstg, stg2,
                      lw, [ctx.dma_stream("sc1a"), ctx.dma_stream("sc1b")])
    emit_cast_store_T(ctx, wm2_h, scr2, D, MLPD, tmp32, tmpbf, pstg, stg2,
                      lw, [ctx.dma_stream("sc2a"), ctx.dma_stream("sc2b")])
    for (cidx, hsrc) in ((0, bg_h), (8, l1g_h), (16, l1b_h), (24, l2g_h),
                         (32, l2b_h)):
        for c in range(ND):
            ctx.dma_op("sync", lb, [(colv.h[:, cidx + c:cidx + c + 1],
                                     hsrc.h[c * 128:(c + 1) * 128])],
                       [colv], [hsrc])
    for c in range(32):
        ctx.dma_op("sync", lb, [(colv.h[:, 40 + c:41 + c],
                                 bm1_h.h[c * 128:(c + 1) * 128])], [colv], [bm1_h])
    ctx.dma_op("sync", lb, [(rowf.h[0:1, 0:D], bm2_h.h[0:D])], [rowf], [bm2_h])
    v.op("tensor_copy", [bm2row], [rowf], bm2row.h[0:1, 0:D], rowf.h[0:1, 0:D])

    ldec = ctx.sbuf("ldec", [64, 2], F32, side="right")
    ctx.dma_op("sync", lb, [(ldec.h[0:64, 0:1], ld_h.h[0:64])], [ldec], [ld_h])
    ctx.dma_op("sync", lb, [(ldec.h[0:64, 1:2], fq_h.h[0:64])], [ldec], [fq_h])
    ew = [ctx.sbuf(f"ew{i}", [64, 64], F32, side="right") for i in range(6)]
    ewi = ctx.sbuf("ewi", [64, 64], I32, side="right")
    vbf = [ctx.sbuf(f"vbf{i}", [64, 64], BF16, side="right") for i in range(2)]
    padt = ctx.sbuf("padt", [64, 127], BF16, side="right")
    s.op("activation", [ew[5]], [ldec], ew[5].h[0:64, 0:1], ldec.h[0:64, 0:1],
         AF.Sigmoid)
    s.op("activation", [ew[5]], [ew[5]], ew[5].h[0:64, 1:2], ew[5].h[0:64, 0:1],
         AF.Ln)
    g.op("iota", [ew[0]], [], ew[0].h[0:64, 0:64], pattern=[[1, 64]], base=0,
         channel_multiplier=0, allow_small_or_imprecise_dtypes=True)
    v.op("tensor_scalar", [ew[1]], [ew[0], ew[5]], ew[1].h[0:64, :],
         ew[0].h[0:64, :], ew[5].h[0:64, 1:2], None, op0=ALU.mult, op1=ALU.bypass)
    s.op("activation", [ew[1]], [ew[1]], ew[1].h[0:64, :], ew[1].h[0:64, :], AF.Exp)
    v.op("tensor_scalar", [ew[2]], [ew[0], ldec], ew[2].h[0:64, :],
         ew[0].h[0:64, :], ldec.h[0:64, 1:2], None, op0=ALU.mult, op1=ALU.bypass)
    v.op("tensor_scalar", [ew[3]], [ew[2]], ew[3].h[0:64, :], ew[2].h[0:64, :],
         float(np.pi / 2), None, op0=ALU.add, op1=ALU.bypass)
    def sinred(dst, srcb):
        v.op("tensor_scalar", [dst], [srcb], dst.h[0:64, :], srcb.h[0:64, :],
             float(1.0 / TWO_PI), None, op0=ALU.mult, op1=ALU.bypass)
        v.op("tensor_copy", [ewi], [dst], ewi.h[0:64, :], dst.h[0:64, :])
        v.op("tensor_copy", [ew[4]], [ewi], ew[4].h[0:64, :], ewi.h[0:64, :])
        v.op("tensor_tensor", [dst], [dst, ew[4]], dst.h[0:64, :], dst.h[0:64, :],
             ew[4].h[0:64, :], op=ALU.subtract)
        v.op("tensor_single_scalar", [ew[4]], [dst], ew[4].h[0:64, :],
             dst.h[0:64, :], 0.5, op=ALU.is_ge)
        v.op("tensor_tensor", [dst], [dst, ew[4]], dst.h[0:64, :], dst.h[0:64, :],
             ew[4].h[0:64, :], op=ALU.subtract)
        s.op("activation", [dst], [dst], dst.h[0:64, :], dst.h[0:64, :],
             AF.Sin, scale=float(TWO_PI))
    sinred(ew[2], ew[2])
    sinred(ew[3], ew[3])
    v.op("tensor_tensor", [fcR], [ew[1], ew[3]], fcR.h[0:64, :],
         ew[1].h[0:64, :], ew[3].h[0:64, :], op=ALU.mult)
    v.op("tensor_tensor", [fcI], [ew[1], ew[2]], fcI.h[0:64, :],
         ew[1].h[0:64, :], ew[2].h[0:64, :], op=ALU.mult)
    v.op("tensor_scalar", [fcN], [fcI], fcN.h[0:64, :], fcI.h[0:64, :],
         -1.0, None, op0=ALU.mult, op1=ALU.bypass)
    stp = ctx.dma_stream("stp")
    for (vi, src) in ((0, fcR), (1, fcI), (2, fcN)):
        g.op("memset", [padt], [], padt.h[0:64, 0:63], 0.0)
        v.op("tensor_copy", [padt], [src], padt.h[0:64, 63:127],
             src.h[0:64, 0:64])
        ctx.dma_op("gpsimd", stp, [(padscr.h[vi * 64:(vi + 1) * 64, :],
                                    padt.h[0:64, :])], [padscr], [padt])
    ldt3 = [ctx.dma_stream(f"ldt{i}") for i in range(3)]
    for (vi, dst) in ((0, Lvr), (1, Lvi), (2, Lvn)):
        for k in range(64):
            ap_in = mkap(padscr, (vi * 64 + k) * 127 + 63,
                            [[0, 2], [-1, 64], [1, 64]])
            ctx.dma_op(("sync", "scalar", "gpsimd")[vi], ldt3[vi],
                       [(dst.h[:, k * 64:(k + 1) * 64], ap_in)],
                       [dst], [padscr])
    for dst in (Lvr, Lvi, Lvn):
        for k in range(64):
            t.op("transpose", [pst0], [dst, ident], pst0[0:64, 0:64],
                 dst.h[0:64, k * 64:(k + 1) * 64], ident.h[0:64, 0:64])
            v.op("tensor_copy", [dst], [pst0], dst.h[0:64, k * 64:(k + 1) * 64],
                 pst0[0:64, 0:64])
            v.op("tensor_copy", [dst], [pst0], dst.h[64:128, k * 64:(k + 1) * 64],
                 pst0[0:64, 0:64])
    cpt = ctx.sbuf("cpt", [64, 16], F32, side="right")
    cptb = ctx.sbuf("cptb", [64, 16], BF16, side="right")
    cpT = ctx.sbuf("cpT", [16, 64], BF16, side="right")
    ctx.dma_op("sync", lb, [(cpt.h[0:64, 0:16], cp_h.h[:, :])], [cpt], [cp_h])
    v.op("tensor_copy", [cptb], [cpt], cptb.h[0:64, :], cpt.h[0:64, :])
    t.op("transpose", [pst0], [cptb, ident], pst0[0:16, 0:64],
         cptb.h[0:64, 0:16], ident.h[0:64, 0:64])
    v.op("tensor_copy", [cpT], [pst0], cpT.h[0:16, 0:64], pst0[0:16, 0:64])
    ctx.dma_op("gpsimd", stp, [(cpscr.h[:, :], cpT.h[0:16, :])], [cpscr], [cpT])
    g.op("memset", [BD], [], BD.h[0:64, :], 0.0)
    for h in range(4):
        ctx.dma_op("sync", ldt3[0], [(mkap(BD, (h * 16) * 64 + h * 16,
                                           [[64, 16], [1, 16]]),
                                   cpscr.h[0:16, h * 16:(h + 1) * 16])],
                   [BD], [cpscr])
    ctx.free(cpT, cptb, cpt, padt, vbf[1], vbf[0], ewi, *reversed(ew), ldec)
    ctx.free(stg2[1], stg2[0], tmpbf[1], tmpbf[0], tmp32[1], tmp32[0])
    frP = ctx.free(pstg[1], pst0)

    ldx = [ctx.dma_stream(f"ldx{i}") for i in range(2)]
    ldw1 = [ctx.dma_stream(f"ldw1_{i}") for i in range(2)]
    stw = ctx.dma_stream("stw")

    for b in range(B_CORE):
        x1 = ctx.sbuf(f"x1_{b}", [128, NT * D], F32)
        xnorm = ctx.sbuf(f"xnorm{b}", [128, ND * T], BF16, side="right")
        bT = ctx.sbuf(f"bT{b}", [128, NT * 128], BF16, side="right")
        c2 = ctx.sbuf(f"c2_{b}", [128, T], BF16, side="right")
        intraS = ctx.sbuf(f"intraS{b}", [64, 2 * T], BF16, side="right")
        cTr = ctx.sbuf(f"cTr{b}", [64, T], F32, side="right")
        cTi = ctx.sbuf(f"cTi{b}", [64, T], F32, side="right")
        cbf = ctx.sbuf(f"cbf{b}", [64, 2 * T], BF16, side="right")
        xz = [ctx.sbuf(f"xz{b}_{i}", [128, D], F32, side="right") for i in range(2)]
        xhb = [ctx.sbuf(f"xhb{b}_{i}", [128, D], BF16, side="right") for i in range(2)]
        gatet = [ctx.sbuf(f"gt{b}_{i}", [128, 512], BF16, side="right") for i in range(2)]
        ghb = [ctx.sbuf(f"ghb{b}_{i}", [128, 512], BF16, side="right") for i in range(2)]
        pTl = ctx.psum(f"pTl{b}", [128, 128], BF16, after=frP)
        for it in range(NT):
            i = it % 2
            ctx.dma_op("sync", ldx[i], [(xz[i][:], xn1_i.h[b * T + it * 128:
                                                           b * T + (it + 1) * 128, :])],
                       [xz[i]], [xn1_i])
            nmu, rstd = emit_ln_stats(ctx, xz[i], xz[i][:], statsl[i], eps_c, x1)
            v.op("tensor_scalar", [xhb[i]], [xz[i], nmu, rstd], xhb[i][:],
                 xz[i][:], nmu.h, rstd.h, op0=ALU.add, op1=ALU.mult)
            for c in range(ND):
                t.op("transpose", [pTl], [xhb[i], ident], pTl[:, 0:128],
                     xhb[i].h[:, c * 128:(c + 1) * 128], ident[:])
                v.op("tensor_scalar", [xnorm], [pTl, colv],
                     xnorm.h[:, c * T + it * 128:c * T + (it + 1) * 128],
                     pTl[:, 0:128], colv.h[:, 8 + c:9 + c], colv.h[:, 16 + c:17 + c],
                     op0=ALU.mult, op1=ALU.add)
        pb = ctx.psum(f"pb{b}", [128, 128], F32)
        for it in range(NT):
            grp = [(xnorm.h[:, c * T + it * 128:c * T + (it + 1) * 128],
                    winT.h[:, c * 128:(c + 1) * 128], [xnorm, winT])
                   for c in range(ND)]
            mm_group(ctx, pb, pb[:, 0:128], grp)
            v.op("tensor_copy", [bT], [pb], bT.h[:, it * 128:(it + 1) * 128],
                 pb[:, 0:128])
        pir = ctx.psum(f"pir{b}", [64, 512], F32)
        pii = ctx.psum(f"pii{b}", [64, 512], F32)
        for gi in range(4):
            for ki in range(16):
                k = gi * 16 + ki
                for p in range(2):
                    co = ki * 32 + p * 16
                    rb_r = mkap(bT, p * 64 * (NT * 128) + k,
                                   [[NT * 128, 64], [128, 16]])
                    rb_i = mkap(bT, p * 64 * (NT * 128) + 64 + k,
                                   [[NT * 128, 64], [128, 16]])
                    pr = p * 64
                    t.op("matmul", [pir], [Lvr, bT], pir[0:64, co:co + 16],
                         Lvr.h[pr:pr + 64, k * 64:(k + 1) * 64], rb_r,
                         start=True, stop=False, inc=False,
                         tile_position=(pr, 0))
                    t.op("matmul", [pir], [Lvn, bT], pir[0:64, co:co + 16],
                         Lvn.h[pr:pr + 64, k * 64:(k + 1) * 64], rb_i,
                         start=False, stop=True, tile_position=(pr, 0))
                    t.op("matmul", [pii], [Lvr, bT], pii[0:64, co:co + 16],
                         Lvr.h[pr:pr + 64, k * 64:(k + 1) * 64], rb_i,
                         start=True, stop=False, inc=False,
                         tile_position=(pr, 0))
                    t.op("matmul", [pii], [Lvi, bT], pii[0:64, co:co + 16],
                         Lvi.h[pr:pr + 64, k * 64:(k + 1) * 64], rb_r,
                         start=False, stop=True, tile_position=(pr, 0))
            v.op("tensor_copy", [intraS], [pir],
                 intraS.h[0:64, gi * 512:(gi + 1) * 512], pir[0:64, 0:512])
            v.op("tensor_copy", [intraS], [pii],
                 intraS.h[0:64, T + gi * 512:T + (gi + 1) * 512], pii[0:64, 0:512])
        pTr = ctx.psum(f"pTr{b}", [64, 64], BF16)
        pTi = ctx.psum(f"pTi{b}", [64, 64], BF16)
        FS = 2 * T
        for n in range(NECH):
            p2, n2 = n % 2, n // 2
            for (var, pT_) in ((0, pTr), (1, pTi)):
                src = mkap(intraS, var * T + p2 * 16 + n2,
                              [[FS, 64], [512, 4], [32, 16]])
                t.op("transpose", [pT_], [intraS, ident], pT_[0:64, 0:64],
                     src, ident.h[0:64, 0:64])
            co = n * 64
            v.op("tensor_copy", [cTr], [pTr], cTr.h[0:64, co:co + 64],
                 pTr[0:64, 0:64])
            v.op("tensor_copy", [cTi], [pTi], cTi.h[0:64, co:co + 64],
                 pTi[0:64, 0:64])
            if n > 0:
                v.op("tensor_tensor", [cTr], [cTr], cTr.h[0:64, co:co + 1],
                     cTr.h[0:64, co:co + 1], cTr.h[0:64, co - 1:co], op=ALU.add)
                v.op("tensor_tensor", [cTi], [cTi], cTi.h[0:64, co:co + 1],
                     cTi.h[0:64, co:co + 1], cTi.h[0:64, co - 1:co], op=ALU.add)
        v.op("tensor_copy", [cbf], [cTr], cbf.h[0:64, 0:T], cTr.h[0:64, 0:T])
        v.op("tensor_copy", [cbf], [cTi], cbf.h[0:64, T:2 * T], cTi.h[0:64, 0:T])
        for ts in range(4):
            for (vo, vi) in ((0, 0), (64, T)):
                t.op("matmul", [pir], [BD, cbf], pir[0:64, 0:512],
                     BD.h[0:64, 0:64], cbf.h[0:64, vi + ts * 512:vi + (ts + 1) * 512],
                     start=True, stop=True)
                v.op("tensor_copy", [c2], [pir], c2.h[vo:vo + 64, ts * 512:(ts + 1) * 512],
                     pir[0:64, 0:512])
        frE0 = ctx.free(pTi, pTr, pii, pir, pb, pTl)
        ctx.dma_op("sync", ldx[0],
                   [(x1.h[:, it * D:(it + 1) * D],
                     xn1_i.h[b * T + it * 128:b * T + (it + 1) * 128, :])
                    for it in range(NT)], [x1], [xn1_i])
        pg = [ctx.psum(f"pg{b}_{i}", [128, 512], F32, after=frE0) for i in range(2)]
        pTg = ctx.psum(f"pTg{b}", [128, 128], BF16)
        for dc in range(ND):
            for ts in range(4):
                i = ts % 2
                grp = [(wgT.h[:, c * D + dc * 128:c * D + dc * 128 + 128],
                        xnorm.h[:, c * T + ts * 512:c * T + (ts + 1) * 512],
                        [wgT, xnorm]) for c in range(ND)]
                mm_group(ctx, pg[i], pg[i][:, 0:512], grp)
                s.op("activation", [gatet[i]], [pg[i], colv], gatet[i][:],
                     pg[i][:, 0:512], AF.Sigmoid, bias=colv.h[:, dc:dc + 1])
                t.op("matmul", [pg[i]], [woutT, c2], pg[i][:, 0:512],
                     woutT.h[:, dc * 128:(dc + 1) * 128],
                     c2.h[:, ts * 512:(ts + 1) * 512],
                     start=True, stop=True)
                v.op("tensor_tensor", [ghb[i]], [pg[i], gatet[i]], ghb[i][:],
                     pg[i][:, 0:512], gatet[i][:], op=ALU.mult)
                for q in range(4):
                    it = ts * 4 + q
                    t.op("transpose", [pTg], [ghb[i], ident], pTg[:, 0:128],
                         ghb[i].h[:, q * 128:(q + 1) * 128], ident[:])
                    v.op("scalar_tensor_tensor", [x1], [pTg], 
                         x1.h[:, it * D + dc * 128:it * D + (dc + 1) * 128],
                         pTg[:, 0:128], 1.0,
                         x1.h[:, it * D + dc * 128:it * D + (dc + 1) * 128],
                         op0=ALU.mult, op1=ALU.add)
        frE = ctx.free(pTg, pg[1], pg[0])
        frS = ctx.free(ghb[1], ghb[0], gatet[1], gatet[0], xhb[1], xhb[0],
                       xz[1], xz[0], cbf, cTi, cTr, intraS, c2, bT, xnorm)
        x2n = ctx.sbuf(f"x2n{b}", [128, ND * T], BF16, side="right")
        w1T = [ctx.sbuf(f"w1T{b}_{i}", [128, ND * 512], BF16, side="right") for i in range(2)]
        w2T = [ctx.sbuf(f"w2T{b}_{i}", [128, 4 * D], BF16, side="right") for i in range(2)]
        h1 = ctx.sbuf(f"h1_{b}", [128, 4 * T], BF16, side="right")
        xz2 = [ctx.sbuf(f"xz2{b}_0", [128, D], F32, side="right")] * 2
        xh2 = [ctx.sbuf(f"xh2{b}_{i}", [128, D], BF16, side="right") for i in range(2)]
        pT2 = ctx.psum(f"pT2{b}", [128, 128], BF16, after=frE)
        for it in range(NT):
            i = it % 2
            nmu, rstd = emit_ln_stats(ctx, Buf(x1.h[:, it * D:(it + 1) * D], "x1s"),
                                      x1.h[:, it * D:(it + 1) * D], statsl[i],
                                      eps_c, xz2[i])
            v.op("tensor_scalar", [xh2[i]], [x1, nmu, rstd], xh2[i][:],
                 x1.h[:, it * D:(it + 1) * D], nmu.h, rstd.h,
                 op0=ALU.add, op1=ALU.mult)
            for c in range(ND):
                t.op("transpose", [pT2], [xh2[i], ident], pT2[:, 0:128],
                     xh2[i].h[:, c * 128:(c + 1) * 128], ident[:])
                v.op("tensor_scalar", [x2n], [pT2, colv],
                     x2n.h[:, c * T + it * 128:c * T + (it + 1) * 128],
                     pT2[:, 0:128], colv.h[:, 24 + c:25 + c], colv.h[:, 32 + c:33 + c],
                     op0=ALU.mult, op1=ALU.add)
        ph2 = [ctx.psum(f"ph2{b}_{i}", [128, 512], F32) for i in range(2)]
        pd2 = [ctx.psum(f"pd2{b}_{i}", [128, D], F32) for i in range(2)]
        for fb in range(NEFB):
            i = fb % 2
            f0 = fb * EFB
            for c in range(ND):
                ctx.dma_op("sync" if c % 2 == 0 else "scalar", ldw1[i],
                           [(w1T[i].h[:, c * EFB:(c + 1) * EFB],
                             scr1.h[c * 128:(c + 1) * 128, f0:f0 + EFB])],
                           [w1T[i]], [scr1])
            for fc in range(4):
                ctx.dma_op("sync", ldw1[i],
                           [(w2T[i].h[:, fc * D:(fc + 1) * D],
                             scr2.h[f0 + fc * 128:f0 + (fc + 1) * 128, 0:D])],
                           [w2T[i]], [scr2])
            for fc in range(4):
                for ts in range(4):
                    p = ph2[(fc * 4 + ts) % 2]
                    grp = [(w1T[i].h[:, c * EFB + fc * 128:c * EFB + (fc + 1) * 128],
                            x2n.h[:, c * T + ts * 512:c * T + (ts + 1) * 512],
                            [w1T[i], x2n]) for c in range(ND)]
                    mm_group(ctx, p, p[:, 0:512], grp)
                    s.op("activation", [h1], [p, colv],
                         h1.h[:, fc * T + ts * 512:fc * T + (ts + 1) * 512],
                         p[:, 0:512], AF.Silu,
                         bias=colv.h[:, 40 + fb * 4 + fc:41 + fb * 4 + fc])
            for it in range(NT):
                p = pd2[it % 2]
                for dh in range(2):
                    terms = []
                    if fb == 0:
                        terms.append(("bias", None))
                    for fc in range(4):
                        terms.append(("mm", fc))
                    for j, (kind, fc) in enumerate(terms):
                        fst = (j == 0)
                        last = (j == len(terms) - 1)
                        outs = [p] if (fst or last) else []
                        if kind == "bias":
                            t.op("matmul", outs, [onesb, bm2row],
                                 p[:, dh * 512:(dh + 1) * 512],
                                 onesb.h[0:1, 0:128],
                                 bm2row.h[0:1, dh * 512:(dh + 1) * 512],
                                 start=True, stop=False, inc=False)
                        else:
                            t.op("matmul", outs, [h1, w2T[i]],
                                 p[:, dh * 512:(dh + 1) * 512],
                                 h1.h[:, fc * T + it * 128:fc * T + (it + 1) * 128],
                                 w2T[i].h[:, fc * D + dh * 512:fc * D + (dh + 1) * 512],
                                 start=fst, stop=last, inc=last)
                v.op("scalar_tensor_tensor", [x1], [pd2[it % 2]],
                     x1.h[:, it * D:(it + 1) * D], p[:, 0:D], 1.0,
                     x1.h[:, it * D:(it + 1) * D], op0=ALU.mult, op1=ALU.add)
        for it in range(NT):
            ctx.dma_op("gpsimd", stw,
                       [(out_o.h[b * T + it * 128:b * T + (it + 1) * 128, :],
                         x1.h[:, it * D:(it + 1) * D])], [out_o], [x1])
        ctx.free(pd2[1], pd2[0], ph2[1], ph2[0], pT2)
        ctx.free(xh2[1], xh2[0], xz2[0], h1, w2T[1], w2T[0], w1T[1], w1T[0], x2n)
        frP = ctx.free(x1)
    g.raw(lambda eng, s_=stw.sem, v_=stw.count: eng.wait_ge(s_, v_))


def build_a1_launch():
    ctx = Ctx()
    ext = {}
    ext["xn1"] = ctx.dram("xn1", [B_CORE * T, D], F32, kind="ExternalInput")
    ext["w_in"] = ctx.dram("w_in", [128, D], F32, kind="ExternalInput")
    ext["w_out"] = ctx.dram("w_out", [D, 128], F32, kind="ExternalInput")
    ext["w_gate"] = ctx.dram("w_gate", [D, D], F32, kind="ExternalInput")
    ext["b_gate"] = ctx.dram("b_gate", [D], F32, kind="ExternalInput")
    ext["log_decay"] = ctx.dram("log_decay", [64], F32, kind="ExternalInput")
    ext["frequency"] = ctx.dram("frequency", [64], F32, kind="ExternalInput")
    ext["coupling"] = ctx.dram("coupling", [64, 16], F32, kind="ExternalInput")
    ext["w_mlp1"] = ctx.dram("w_mlp1", [MLPD, D], F32, kind="ExternalInput")
    ext["b_mlp1"] = ctx.dram("b_mlp1", [MLPD], F32, kind="ExternalInput")
    ext["w_mlp2"] = ctx.dram("w_mlp2", [D, MLPD], F32, kind="ExternalInput")
    ext["b_mlp2"] = ctx.dram("b_mlp2", [D], F32, kind="ExternalInput")
    ext["ln1_g"] = ctx.dram("ln1_g", [D], F32, kind="ExternalInput")
    ext["ln1_b"] = ctx.dram("ln1_b", [D], F32, kind="ExternalInput")
    ext["ln2_g"] = ctx.dram("ln2_g", [D], F32, kind="ExternalInput")
    ext["ln2_b"] = ctx.dram("ln2_b", [D], F32, kind="ExternalInput")
    ext["a1out"] = ctx.dram("a1out", [B_CORE * T, D], F32, kind="ExternalOutput")
    emit_a1(ctx, ext)
    return ctx.emit()
